# version 02
# baseline (speedup 1.0000x reference)
import numpy as np
import ml_dtypes
from contextlib import ExitStack

import concourse.bass as bass
import concourse.mybir as mybir
from concourse.bass_utils import run_bass_kernel_spmd

F32 = mybir.dt.float32
BF16 = mybir.dt.bfloat16
I32 = mybir.dt.int32
AF = mybir.ActivationFunctionType
ALU = mybir.AluOpType
BF = ml_dtypes.bfloat16
AX = mybir.AxisListType.X

NBATCH = 2
SEQ = 4096
DM = 4096
NDC = 32
TPC = 1024
NSLOT = 8
TILE = 128
NST = 16
DTR = 128
HFF = 11008
NHC = 86
EPS = 1e-5
HEADS = 8
NCC = 16
NQKV = 48
DN_BLKS = [(i * 8, min(8, NHC - i * 8)) for i in range(11)]

GQ = [[q, 7 - q, 8 + q, 15 - q, 16 + q, 23 - q, 24 + q, 31 - q] for q in range(4)]
G2QS = {}
for _q in range(4):
    for _s in range(NSLOT):
        G2QS[GQ[_q][_s]] = (_q, _s)

R_INV = float(np.float32(1.0) / np.float32(3484.0))
Q_START, Q_END = 611, 4095



def _prep_shared(inp):
    d = {}
    f32 = lambda a: np.ascontiguousarray(np.asarray(a), dtype=np.float32)
    d["A_log_t"] = f32(inp["L0_attn_A_log"]).reshape(NDC, 128, NST)
    d["D_t"] = np.ascontiguousarray(f32(inp["L0_attn_D"]).reshape(NDC, 128).T)
    d["dtb_t"] = np.ascontiguousarray(f32(inp["L0_attn_dt_proj_b"]).reshape(NDC, 128).T)
    d["xpwT"] = np.ascontiguousarray(f32(inp["L0_attn_x_proj_w"]).T).reshape(NDC, 128, 160)
    dtw = f32(inp["L0_attn_dt_proj_w"])
    d["dtwT"] = np.ascontiguousarray(dtw.T.reshape(DTR, NDC, 128).transpose(1, 0, 2))
    for nm, key in (("qg", "L0_ffn_qw_gate"), ("qu", "L0_ffn_qw_up")):
        qw = np.ascontiguousarray(np.asarray(inp[key]), dtype=np.int32)
        d[nm] = np.ascontiguousarray(
            qw.reshape(NDC, 128, NHC, 128).transpose(2, 1, 0, 3).reshape(NHC, 128, NDC * 128))
    qd = np.ascontiguousarray(np.asarray(inp["L0_ffn_qw_down"]), dtype=np.int32)
    d["qd"] = np.ascontiguousarray(
        qd.reshape(NHC, 128, NDC, 128).transpose(2, 1, 0, 3).reshape(NDC, 128, NHC * 128))
    for nm, key in (("zg", "L0_ffn_z_gate"), ("zu", "L0_ffn_z_up"), ("zd", "L0_ffn_z_down")):
        d[nm] = np.ascontiguousarray(np.asarray(inp[key]), dtype=np.int32)
    for nm, key in (("sg", "L0_ffn_s_gate"), ("su", "L0_ffn_s_up"), ("sd", "L0_ffn_s_down")):
        d[nm] = f32(inp[key])
    d["wqkvT"] = np.ascontiguousarray(f32(inp["L1_attn_w_qkv"]).T).reshape(NDC, 128, 6144)
    d["bqkv_t"] = np.ascontiguousarray(f32(inp["L1_attn_b_qkv"]).reshape(NQKV, 128).T)
    d["woT"] = np.ascontiguousarray(f32(inp["L1_attn_w_o"]).T).reshape(NCC, 128, 4096)
    d["bo_t"] = np.ascontiguousarray(f32(inp["L1_attn_b_o"]).reshape(NDC, 128).T)
    d["WT"] = np.ascontiguousarray(f32(inp["L1_ffn_W"]).T).reshape(NDC, 128, 4096)
    d["bW_t"] = np.ascontiguousarray(f32(inp["L1_ffn_b"]).reshape(NDC, 128).T)
    d["AcatT"] = np.ascontiguousarray(f32(inp["L1_ffn_A"]).reshape(128, DM).T).reshape(NDC, 128, 128)
    d["BcatT"] = np.ascontiguousarray(f32(inp["L1_ffn_B"]).transpose(0, 2, 1).reshape(128, DM))
    d["rWT"] = np.ascontiguousarray(f32(inp["L1_ffn_router_W"]).T).reshape(NDC, 128, 8)
    d["rb_t"] = f32(inp["L1_ffn_router_b"]).reshape(8, 1)
    e32 = np.zeros((128, NDC, 32), np.float32)
    for k in range(NDC):
        e32[:, k, k] = 1.0
    d["E32"] = e32.reshape(128, NDC * 32).astype(BF)
    e16 = np.zeros((16, 16, 128), np.float32)
    for n in range(16):
        e16[n, n, :] = 1.0
    d["E16"] = np.ascontiguousarray(e16.reshape(16, 16 * 128))
    esg = np.zeros((32, 32, 128), np.float32)
    for g in range(32):
        esg[g, g, :] = 1.0
    d["Esg"] = esg.reshape(32, 32 * 128).astype(BF)
    esd = np.zeros((8, 8, 128), np.float32)
    for g in range(8):
        esd[g, g, :] = 1.0
    d["Esd"] = esd.reshape(8, 8 * 128).astype(BF)
    edn = np.zeros((128, NHC, 8), np.float32)
    for k0, bs in DN_BLKS:
        for j in range(bs):
            edn[:, k0 + j, j] = 1.0
    d["Edn"] = edn.reshape(128, NHC * 8).astype(BF)
    erv = np.zeros((8, 128), np.float32)
    for p in range(128):
        erv[p // 16, p] = 1.0
    d["Erv"] = erv
    d["ident"] = np.eye(128).astype(BF)
    d["ones128f"] = np.ones((128, 1), np.float32)
    d["ones1x128f"] = np.ones((1, 128), np.float32)
    d["ones128b"] = np.ones((128, 1), BF)
    d["o1x128b"] = np.ones((1, 128), BF)
    d["onesKM"] = np.ones((128, 32), BF)
    return d


def _prep_core(inp, c):
    b_, q = c // 4, c % 4
    d = {}
    x = np.asarray(inp["L0_attn_x"], dtype=np.float32)
    rows = np.concatenate([x[b_, g * TILE:(g + 1) * TILE, :] for g in GQ[q]], 0)
    d["x_t"] = np.ascontiguousarray(rows.T).reshape(NDC, 128, TPC)
    m = np.zeros((NSLOT, 128, 512), np.float32)
    for s in range(NSLOT):
        g = GQ[q][s]
        keys = s * 512 + np.arange(512)[None, :]
        qpos = g * TILE + np.arange(128)[:, None]
        m[s] = np.where(keys <= qpos, 0.0, -10000.0)
    d["masks"] = m
    wv = np.zeros((TPC,), np.float32)
    for s in range(NSLOT):
        g = GQ[q][s]
        pos = g * TILE + np.arange(TILE)
        wv[s * TILE:(s + 1) * TILE] = np.where((pos >= Q_START) & (pos < Q_END), 1.0, 0.0)
    d["wvb"] = np.ascontiguousarray(np.broadcast_to(wv[None, :], (128, TPC)), dtype=np.float32)
    msk = np.zeros((NSLOT, 128, NST * 32), np.float32)
    for s in range(NSLOT):
        g = GQ[q][s]
        for n in range(NST):
            msk[s, :, n * 32 + g] = 1.0
    d["hmsk"] = msk.astype(BF)
    return d



ENGS = ("sync", "tensor", "vector", "scalar", "gpsimd")
SEMS = ("ld", "gd", "pe", "dv", "ac", "pl", "cc", "so")


class Bld:
    def __init__(self, nc, ctx):
        self.nc = nc
        self.ctx = ctx
        self.ops = {e: [] for e in ENGS}
        self.drain = None
        self.sh = {}
        self.c = {}
        for s in SEMS:
            self.sh[s] = ctx.enter_context(nc.semaphore(s))
            self.c[s] = 0

    def sb(self, name, shape, dt):
        return self.ctx.enter_context(self.nc.sbuf_tensor(name, shape, dt))

    def ps(self, name, shape, dt=F32):
        return self.ctx.enter_context(self.nc.psum_tensor(name, shape, dt))

    def act(self, f, inc="ac", n=1):
        return self.op("scalar", f, inc=inc, n=n)

    def op(self, eng, f, inc=None, n=1, drain=True):
        if inc is None:
            self.ops[eng].append(f)
            return None
        h = self.sh[inc]
        self.c[inc] += n
        if drain and eng in ("vector", "scalar", "gpsimd", "tensor"):
            self.ops[eng].append(f)
            self.ops[eng].append(lambda e, h=h, n=n: e.drain().then_inc(h, n))
        else:
            def g(e, f=f, h=h, n=n):
                f(e).then_inc(h, n)
            self.ops[eng].append(g)
        return self.c[inc]

    def w(self, eng, sem, val=None):
        v = self.c[sem] if val is None else val
        if v <= 0:
            return
        h = self.sh[sem]
        self.ops[eng].append(lambda e, h=h, v=v: e.wait_ge(h, v))

    def bar(self, engs=ENGS, sems=SEMS):
        snap = {s: self.c[s] for s in sems}
        for eng in engs:
            for s, v in snap.items():
                if v > 0:
                    self.w(eng, s, v)

    def dma(self, eng, out, in_, sem=None, **kw):
        if sem is None:
            sem = "gd" if eng == "gpsimd" else "ld"
        return self.op(eng, lambda e, o=out, i=in_, kw=kw: e.dma_start(out=o, in_=i, **kw),
                       inc=sem, n=16, drain=False)

    def finalize(self):
        block = self.ctx.enter_context(self.nc.Block())
        for eng in ENGS:
            lst = self.ops[eng]

            def runner(e, lst=lst):
                for f in lst:
                    f(e)
            getattr(block, eng)(runner)


def _col(t, j):
    return t[:, j:j + 1]



_LN_SEQ = [0]


def _ln_phase(b, P, src, io, with_sg, with_xp, xnb, mb, rstdb):
    _LN_SEQ[0] += 1
    tag = f"{_LN_SEQ[0]}_"
    ph = ExitStack()
    with ph:
        sb = lambda nm, sh, dt: ph.enter_context(b.nc.sbuf_tensor(tag + nm, sh, dt))
        pstn = lambda nm, sh: ph.enter_context(b.nc.psum_tensor(tag + nm, sh, F32))
        if with_xp:
            xpw = sb("ln_xpw", [128, NDC * 160], BF16)
            b.dma("gpsimd", xpw[:].rearrange("p (c r) -> p c r", r=160),
                  io["xpwT"].rearrange("c p r -> p c r"))
        E32s = sb("ln_E32", [128, NDC * 32], BF16)
        b.dma("sync", E32s[:], io["E32"])
        xch = sb("ln_xch", [128, TPC], F32)
        sqb = sb("ln_sqb", [128, TPC], F32)
        hhb = sb("ln_hhb", [128, TPC], BF16)
        hlb = sb("ln_hlb", [128, TPC], BF16)
        shb = sb("ln_shb", [128, TPC], BF16)
        slb = sb("ln_slb", [128, TPC], BF16)
        rsc = sb("ln_rsc", [128, TPC], F32)
        mrow = sb("ln_mrow", [1, TPC], F32)
        vrow = sb("ln_vrow", [1, TPC], F32)
        srow = sb("ln_srow", [1, TPC], F32)
        xnw = sb("ln_xnw", [128, TPC], F32)
        xtr = sb("ln_xtr", [128, TPC], BF16)
        GS = sb("ln_GS", [32, TPC], F32)
        GQ = sb("ln_GQ", [32, TPC], F32)
        phA = ExitStack()
        ps_sum = [phA.enter_context(b.nc.psum_tensor(f"{tag}ln_ps_sum{t}", [32, 512], F32)) for t in range(2)]
        ps_ssq = [phA.enter_context(b.nc.psum_tensor(f"{tag}ln_ps_ssq{t}", [32, 512], F32)) for t in range(2)]
        ps_b1 = phA.enter_context(b.nc.psum_tensor(tag + "ln_ps_b1", [128, 512], F32))
        ps_b2 = phA.enter_context(b.nc.psum_tensor(tag + "ln_ps_b2", [128, 512], F32))

        for dc in range(NDC):
            if dc > 0:
                b.w("sync", "pe", b.c["pe"])
            ldv = b.dma("sync", xch[:], src[dc])
            b.w("scalar", "ld", ldv)
            b.w("scalar", "pe", b.c["pe"])
            acv = b.act(lambda e: e.activation(sqb[:], xch[:], AF.Square))
            b.w("vector", "ld", ldv)
            b.w("vector", "pe", b.c["pe"])
            b.op("vector", lambda e: e.tensor_copy(hhb[:], xch[:]))
            b.op("vector", lambda e: e.tensor_tensor(rsc[:], xch[:], hhb[:], ALU.subtract))
            dvv = b.op("vector", lambda e: e.tensor_copy(hlb[:], rsc[:]), inc="dv")
            b.w("vector", "ac", acv)
            b.op("vector", lambda e: e.tensor_copy(shb[:], sqb[:]))
            b.op("vector", lambda e: e.tensor_tensor(rsc[:], sqb[:], shb[:], ALU.subtract))
            dv2 = b.op("vector", lambda e: e.tensor_copy(slb[:], rsc[:]), inc="dv")
            b.w("tensor", "dv", dv2)
            for t in range(2):
                b.op("tensor", lambda e, t_=t, dc=dc: e.matmul(
                    ps_sum[t_][:], E32s[:, dc * 32:(dc + 1) * 32], hhb[:, t_ * 512:(t_ + 1) * 512],
                    start=(dc == 0), stop=False))
                b.op("tensor", lambda e, t_=t, dc=dc: e.matmul(
                    ps_sum[t_][:], E32s[:, dc * 32:(dc + 1) * 32], hlb[:, t_ * 512:(t_ + 1) * 512],
                    start=False, stop=(dc == NDC - 1)))
            for t in range(2):
                b.op("tensor", lambda e, t_=t, dc=dc: e.matmul(
                    ps_ssq[t_][:], E32s[:, dc * 32:(dc + 1) * 32], shb[:, t_ * 512:(t_ + 1) * 512],
                    start=(dc == 0), stop=False))
                b.op("tensor", lambda e, t_=t, dc=dc: e.matmul(
                    ps_ssq[t_][:], E32s[:, dc * 32:(dc + 1) * 32], slb[:, t_ * 512:(t_ + 1) * 512],
                    start=False, stop=(dc == NDC - 1)),
                    inc="pe" if t == 1 else None)
        b.op("tensor", lambda e: e.drain())
        b.w("vector", "pe")
        for t in range(2):
            b.op("vector", lambda e, t_=t: e.tensor_copy(GS[:, t_ * 512:(t_ + 1) * 512], ps_sum[t_][:]))
            b.op("vector", lambda e, t_=t: e.tensor_copy(GQ[:, t_ * 512:(t_ + 1) * 512], ps_ssq[t_][:]),
                 inc="dv" if t == 1 else None)
        b.w("gpsimd", "dv", b.c["dv"])
        b.op("gpsimd", lambda e: e.reduce_sum(mrow[:], GS[:], axis=mybir.AxisListType.C))
        b.op("gpsimd", lambda e: e.reduce_sum(vrow[:], GQ[:], axis=mybir.AxisListType.C), inc="pl")
        b.w("vector", "pl", b.c["pl"])
        dvv = b.op("vector", lambda e: e.tensor_scalar(mrow[:], mrow[:], 1.0 / DM, None, ALU.mult), inc="dv")
        b.op("vector", lambda e: e.tensor_scalar(vrow[:], vrow[:], 1.0 / DM, None, ALU.mult))
        import os
        if os.environ.get("LN_EARLYTAP") and "d_mrow" in io:
            b.w("gpsimd", "dv", dvv)
            b.dma("gpsimd", io["d_mrow"][:, :], mrow[:])
            b.w("vector", "gd", b.c["gd"])
        b.op("vector", lambda e: e.tensor_tensor(srow[:], mrow[:], mrow[:], ALU.mult))
        b.op("vector", lambda e: e.tensor_tensor(vrow[:], vrow[:], srow[:], ALU.subtract), inc="dv")
        b.op("vector", lambda e: e.tensor_scalar(vrow[:], vrow[:], float(EPS), None, ALU.add),
             inc="dv")
        b.w("scalar", "dv")
        acv0 = b.act(lambda e: e.activation(vrow[:], vrow[:], AF.Ln))
        b.w("vector", "ac", acv0)
        b.op("vector", lambda e: e.tensor_scalar(vrow[:], vrow[:], -0.5, None, ALU.mult), inc="dv")
        b.w("scalar", "dv")
        b.act(lambda e: e.activation(vrow[:], vrow[:], AF.Exp))
        mhi = srow
        b.w("vector", "ac")
        b.op("vector", lambda e: e.tensor_copy(hhb[0:1, 0:TPC], mrow[:]))
        b.op("vector", lambda e: e.tensor_tensor(mhi[:], mrow[:], hhb[0:1, 0:TPC], ALU.subtract))
        b.op("vector", lambda e: e.tensor_copy(hlb[0:1, 0:TPC], mhi[:]))
        b.op("vector", lambda e: e.tensor_copy(shb[0:1, 0:TPC], vrow[:]))
        b.op("vector", lambda e: e.tensor_tensor(mhi[:], vrow[:], shb[0:1, 0:TPC], ALU.subtract))
        b.op("vector", lambda e: e.tensor_copy(slb[0:1, 0:TPC], mhi[:]), inc="dv")
        b.w("tensor", "dv")
        for t in range(2):
            b.op("tensor", lambda e, t_=t: e.matmul(
                ps_b1[:], P["o1x128b"][:], hhb[0:1, t_ * 512:(t_ + 1) * 512], start=True, stop=False))
            b.op("tensor", lambda e, t_=t: e.matmul(
                ps_b1[:], P["o1x128b"][:], hlb[0:1, t_ * 512:(t_ + 1) * 512], start=False, stop=True))
            b.op("tensor", lambda e, t_=t: e.matmul(
                ps_b2[:], P["o1x128b"][:], shb[0:1, t_ * 512:(t_ + 1) * 512], start=True, stop=False))
            b.op("tensor", lambda e, t_=t: e.matmul(
                ps_b2[:], P["o1x128b"][:], slb[0:1, t_ * 512:(t_ + 1) * 512], start=False, stop=True),
                inc="pe")
            b.w("vector", "pe")
            b.op("vector", lambda e, t_=t: e.tensor_copy(mb[:, t_ * 512:(t_ + 1) * 512], ps_b1[:]))
            dvv = b.op("vector", lambda e, t_=t: e.tensor_copy(rstdb[:, t_ * 512:(t_ + 1) * 512], ps_b2[:]), inc="dv")
            b.w("tensor", "dv", dvv)
        b.op("tensor", lambda e: e.drain())
        if "d_mrow" in io:
            b.w("gpsimd", "dv", b.c["dv"])
            b.dma("gpsimd", io["d_mrow"][:, :], mrow[:])
            b.dma("gpsimd", io["d_vrow"][:, :], vrow[:])
            b.dma("gpsimd", io["d_mb"][:, :], mb[:])
            b.dma("gpsimd", io["d_rstdb"][:, :], rstdb[:])
            b.w("vector", "gd", b.c["gd"])
        if True:
            pass

        import os
        if os.environ.get("LN_PASSA_ONLY"):
            if "d_mrow" in io:
                b.w("gpsimd", "dv", b.c["dv"])
                b.dma("gpsimd", io["d_mrow"][:, :], mrow[:])
                b.dma("gpsimd", io["d_vrow"][:, :], vrow[:])
            b.bar()
            return
        phA.close()
        ps_sg = [pstn(f"ln_ps_sg{t}", [32, 512]) for t in range(2)] if with_sg else None
        if with_xp:
            ps_xp1 = [pstn(f"ln_ps_xp1{t}", [128, 512]) for t in range(2)]
            ps_B = [pstn(f"ln_ps_B{t}", [16, 512]) for t in range(2)]
            ps_C = [pstn(f"ln_ps_C{t}", [16, 512]) for t in range(2)]

        pe_ck = []
        dv_ck = b.c["dv"]
        for dc in range(NDC):
            b.w("sync", "dv", dv_ck)
            ldv = b.dma("sync", xch[:], src[dc])
            b.w("vector", "ld", ldv)
            if dc > 0 and pe_ck:
                b.w("vector", "pe", pe_ck[-1])
            b.op("vector", lambda e: e.tensor_tensor(xnw[:], xch[:], mb[:], ALU.subtract))
            xnd = xnb[:, dc * TPC:(dc + 1) * TPC] if xnb is not None else xtr[:]
            dvv = b.op("vector", lambda e, o=xnd: e.tensor_tensor(o, xnw[:], rstdb[:], ALU.mult),
                       inc="dv")
            dv_ck = dvv
            b.w("tensor", "dv", dvv)
            last = None
            for t in range(2):
                sl = (slice(None), slice(t * 512, (t + 1) * 512))
                if with_sg:
                    last = b.op("tensor", lambda e, t_=t, dc=dc, x=xnd: e.matmul(
                        ps_sg[t_][:], E32s[:, dc * 32:(dc + 1) * 32], x[:, t_ * 512:(t_ + 1) * 512],
                        start=(dc == 0), stop=(dc == NDC - 1)), inc="pe")
                if with_xp:
                    b.op("tensor", lambda e, t_=t, dc=dc, x=xnd: e.matmul(
                        ps_xp1[t_][:], xpw[:, dc * 160:dc * 160 + 128],
                        x[:, t_ * 512:(t_ + 1) * 512], start=(dc == 0), stop=(dc == NDC - 1)))
                    b.op("tensor", lambda e, t_=t, dc=dc, x=xnd: e.matmul(
                        ps_B[t_][:], xpw[:, dc * 160 + 128:dc * 160 + 144],
                        x[:, t_ * 512:(t_ + 1) * 512], start=(dc == 0), stop=(dc == NDC - 1)))
                    last = b.op("tensor", lambda e, t_=t, dc=dc, x=xnd: e.matmul(
                        ps_C[t_][:], xpw[:, dc * 160 + 144:dc * 160 + 160],
                        x[:, t_ * 512:(t_ + 1) * 512], start=(dc == 0), stop=(dc == NDC - 1)),
                        inc="pe")
            pe_ck.append(b.c["pe"])
        b.w("vector", "pe")
        for t in range(2):
            sl = slice(t * 512, (t + 1) * 512)
            if with_xp:
                b.op("vector", lambda e, t_=t, sl=sl: e.tensor_copy(P["dtin"][:, sl], ps_xp1[t_][:]))
                b.op("vector", lambda e, t_=t, sl=sl: e.tensor_copy(P["Bf"][:, sl], ps_B[t_][:]))
                b.op("vector", lambda e, t_=t, sl=sl: e.tensor_copy(P["Cf"][:, sl], ps_C[t_][:]))
            if with_sg:
                b.op("vector", lambda e, t_=t, sl=sl: e.tensor_copy(P["Sg"][:, sl], ps_sg[t_][:]))
        b.op("vector", lambda e: e.tensor_copy(srow[:], mrow[:]), inc="dv")
        b.bar()


def _build_bbcb(b, P, Bb, Cb, io_E16, dbg_io=None):
    _LN_SEQ[0] += 1
    tag = f"{_LN_SEQ[0]}_"
    ph = ExitStack()
    with ph:
        E16s = ph.enter_context(b.nc.sbuf_tensor(tag + "bb_E16", [16, 2048], F32))
        dbg_ps = ph.enter_context(b.nc.sbuf_tensor(tag + "bb_dbgps", [128, 512], F32)) if dbg_io is not None else None
        b.dma("sync", E16s[:], io_E16)
        b.bar(engs=("tensor",), sems=("ld",))
        ps1 = ph.enter_context(b.nc.psum_tensor(tag + "bb_ps1", [128, 512], F32))
        ps2 = ph.enter_context(b.nc.psum_tensor(tag + "bb_ps2", [128, 512], F32))
        for n in range(NST):
            for t in range(2):
                b.op("tensor", lambda e, n=n, t_=t: e.matmul(
                    ps1[:], E16s[:, n * 128:(n + 1) * 128],
                    P["Bf"][:, t_ * 512:(t_ + 1) * 512], start=True, stop=True))
                b.op("tensor", lambda e, n=n, t_=t: e.matmul(
                    ps2[:], E16s[:, n * 128:(n + 1) * 128],
                    P["Cf"][:, t_ * 512:(t_ + 1) * 512], start=True, stop=True), inc="pe")
                b.w("vector", "pe")
                o1 = Bb[:, n * TPC + t * 512:n * TPC + (t + 1) * 512]
                o2 = Cb[:, n * TPC + t * 512:n * TPC + (t + 1) * 512]
                b.op("vector", lambda e, o=o1: e.tensor_copy(o, ps1[:]))
                dvv = b.op("vector", lambda e, o=o2: e.tensor_copy(o, ps2[:]), inc="dv")
                b.w("tensor", "dv", dvv)
        b.bar()


def _dt_chunk(b, P, dc, ps_dt, dtf, tA, tB):
    b.w("tensor", "dv", b.c["dv"])
    pev = None
    for t in range(2):
        pev = b.op("tensor", lambda e, t_=t, dc=dc: e.matmul(
            ps_dt[t_][:], P["dtw"][:, dc * 128:(dc + 1) * 128],
            P["dtin"][:, t_ * 512:(t_ + 1) * 512], start=True, stop=True), inc="pe")
    b.w("vector", "pe", pev)
    evv = None
    for t in range(2):
        evv = b.op("vector", lambda e, t_=t, dc=dc: e.tensor_scalar(
            dtf[:, t_ * 512:(t_ + 1) * 512], ps_dt[t_][:], _col(P["dtb"], dc), None, ALU.add),
            inc="dv")
    b.op("vector", lambda e: e.tensor_scalar(tA[:], dtf[:], -1.0, None, ALU.mult))
    dvv = b.op("vector", lambda e: e.tensor_tensor(tA[:], dtf[:], tA[:], ALU.min), inc="dv")
    b.w("scalar", "dv", dvv)
    b.op("scalar", lambda e: e.activation(tA[:], tA[:], AF.Exp))
    acv = b.act(lambda e: e.activation(tA[:], tA[:], AF.Ln, bias=1.0))
    b.op("vector", lambda e: e.tensor_scalar(tB[:], dtf[:], 0.0, None, ALU.max))
    b.w("vector", "ac", acv)
    b.op("vector", lambda e: e.tensor_tensor(dtf[:], tB[:], tA[:], ALU.add))
    return evv


def _mamba_pass1(b, io, P, Bb, Cb, mb, rstdb):
    ph = ExitStack()
    with ph:
        sb = lambda nm, sh, dt: ph.enter_context(b.nc.sbuf_tensor(nm, sh, dt))
        xch = sb("m1_xch", [128, TPC], F32)
        dtf = sb("m1_dtf", [128, TPC], F32)
        tA = sb("m1_tA", [128, TPC], F32)
        dtxb = sb("m1_dtxb", [128, TPC], BF16)
        dAf = sb("m1_dAf", [128, TPC], F32)
        ubf = sb("m1_ubf", [128, TPC], BF16)
        hbf = sb("m1_hbf", [128, TPC], BF16)
        ybf = sb("m1_ybf", [128, TPC], F32)
        ych = [sb(f"m1_ych{i}", [128, TPC], F32) for i in range(2)]
        ssb = [sb(f"m1_ssb{i}", [128, 136], F32) for i in range(2)]
        ps_dt = [ph.enter_context(b.nc.psum_tensor(f"m1_psdt{t}", [128, 512], F32)) for t in range(2)]
        gd_ck = [0, 0]

        for dc in range(NDC):
            yc = ych[dc % 2]
            sc = ssb[dc % 2]
            if dc > 0:
                b.w("sync", "dv", dv_xch)
            ldv = b.dma("sync", xch[:], io["x_t"][dc])
            b.w("vector", "ld", ldv)
            b.w("vector", "gd", gd_ck[dc % 2])
            b.op("vector", lambda e: e.tensor_tensor(xch[:], xch[:], mb[:], ALU.subtract))
            b.op("vector", lambda e: e.tensor_tensor(xch[:], xch[:], rstdb[:], ALU.mult),
                 inc="dv")
            b.op("vector", lambda e, yc=yc, dc=dc: e.tensor_scalar(
                yc[:], xch[:], _col(P["Dg"], dc), None, ALU.mult))
            _dt_chunk(b, P, dc, ps_dt, dtf, tA, dAf)
            b.op("vector", lambda e, sc=sc: e.tensor_reduce(
                sc[:].rearrange("p (s c) -> p s c", c=17)[:, :, 16],
                dtf[:].rearrange("p (s t) -> p s t", t=TILE),
                axis=AX, op=ALU.add))
            dv_xch = b.op("vector", lambda e: e.tensor_tensor(dtxb[:], dtf[:], xch[:], ALU.mult),
                          inc="dv")
            for n in range(NST):
                dvv = b.op("vector", lambda e, dc=dc, n=n: e.tensor_scalar(
                    dAf[:], dtf[:], _col(P["A"], dc * NST + n), None, ALU.mult), inc="dv")
                b.w("scalar", "dv", dvv)
                acv = b.act(lambda e: e.activation(dAf[:], dAf[:], AF.Exp))
                b.op("vector", lambda e, n=n: e.tensor_tensor(
                    ubf[:], dtxb[:], Bb[:, n * TPC:(n + 1) * TPC], ALU.mult))
                b.w("vector", "ac", acv)
                for s in range(NSLOT):
                    dvv = b.op("vector", lambda e, s=s: e.tensor_tensor_scan(
                        hbf[:, s * TILE:(s + 1) * TILE], dAf[:, s * TILE:(s + 1) * TILE],
                        ubf[:, s * TILE:(s + 1) * TILE], 0.0, ALU.mult, ALU.add),
                        inc="dv" if s == NSLOT - 1 else None)
                b.w("vector", "dv", dvv)
                b.op("vector", lambda e, sc=sc, n=n: e.tensor_copy(
                    sc[:].rearrange("p (s c) -> p s c", c=17)[:, :, n],
                    hbf[:].rearrange("p (s t) -> p s t", t=TILE)[:, :, TILE - 1]))
                if DEBUG and dc == 0 and n == 0 and "d_da00" in io:
                    b.op("vector", lambda e: e.tensor_copy(tA[:], dAf[:]), inc="dv")
                    b.w("gpsimd", "dv", b.c["dv"])
                    b.dma("gpsimd", io["d_da00"][:, :], tA[:])
                    b.dma("gpsimd", io["d_u00"][:, :], ubf[:])
                    b.dma("gpsimd", io["d_hb00"][:, :], hbf[:])
                    b.dma("gpsimd", io["d_dtx0"][:, :], dtxb[:])
                    b.w("vector", "gd", b.c["gd"])
                b.w("vector", "pl", b.c["pl"])
                dvv = b.op("vector", lambda e, n=n: e.tensor_tensor(
                    ybf[:], hbf[:], Cb[:, n * TPC:(n + 1) * TPC], ALU.mult), inc="dv")
                b.w("gpsimd", "dv", dvv)
                b.op("gpsimd", lambda e, yc=yc: e.tensor_tensor(yc[:], yc[:], ybf[:], ALU.add),
                     inc="pl")
            b.dma("gpsimd", io["ACC"][dc], yc[:], accum_op=ALU.add)
            b.dma("gpsimd", io["stloc"][dc * 128:(dc + 1) * 128, :], sc[:])
            gd_ck[dc % 2] = b.c["gd"]
        b.bar()


def _ffn0(b, io, P, xnb):
    ph = ExitStack()
    with ph:
        sb = lambda nm, sh, dt: ph.enter_context(b.nc.sbuf_tensor(nm, sh, dt))
        pstn = lambda nm, sh, dt=F32: ph.enter_context(b.nc.psum_tensor(nm, sh, dt))
        Esg = sb("f_Esg", [32, 4096], BF16)
        Esd = sb("f_Esd", [8, 1024], BF16)
        Edn = sb("f_Edn", [128, NHC * 8], BF16)
        b.dma("sync", Esg[:], io["Esg"])
        b.dma("sync", Esd[:], io["Esd"])
        b.dma("sync", Edn[:], io["Edn"])
        qb = {pr: [sb(f"f_q{pr}0", [128, NDC * 128], BF16)] for pr in ("g", "u")}
        wpb = {pr: sb(f"f_w{pr}", [128, NDC * 128], BF16) for pr in ("g", "u")}
        zrow = sb("f_zrow", [32, 1024], F32)
        srow = {pr: sb(f"f_srow{pr}", [32, 1024], F32) for pr in ("g", "u")}
        szb = {pr: sb(f"f_szb{pr}", [32, 1024], BF16) for pr in ("g", "u")}
        ssb = {pr: sb(f"f_ssb{pr}", [32, 1024], BF16) for pr in ("g", "u")}
        hslab = sb("f_hslab", [128, 8 * TPC], BF16)
        silu_t = sb("f_silu", [128, 512], F32)
        shblk = sb("f_shblk", [8, TPC], BF16)
        zdrow = sb("f_zdrow", [8, TPC], F32)
        sdrow = sb("f_sdrow", [8, TPC], F32)
        szd = sb("f_szd", [8, TPC], BF16)
        sdb = sb("f_sdb", [8, TPC], BF16)
        qdb = [sb(f"f_qdb{i}", [128, 8 * 128], BF16) for i in range(2)]
        wdb = sb("f_wdb", [128, 8 * 128], BF16)
        dnf = [sb(f"f_dnf{i}", [128, TPC], F32) for i in range(2)]
        b.bar(engs=("tensor", "vector", "gpsimd"), sems=("ld",))
        if "d_esg_e" in io:
            b.dma("gpsimd", io["d_esg_e"][:, :], Esg[:])
        ps_g = pstn("f_psg", [128, 512])
        ps_u = pstn("f_psu", [128, 512])
        ps_sb = pstn("f_pssb", [128, 1024])
        ps_sh = [pstn(f"f_pssh{t}", [22, 512]) for t in range(2)]
        ps_dn = [pstn(f"f_psdn{t}", [128, 512]) for t in range(2)]

        qsrc = {"g": io["qg"], "u": io["qu"]}
        zsrc = {"g": io["zg"], "u": io["zu"]}
        ssrc = {"g": io["sg"], "u": io["su"]}

        gd_q = {}
        dv_w = 0
        pe_hc = 0

        for blk_i, (k0, bs) in enumerate(DN_BLKS):
            for j in range(bs):
                hc = k0 + j
                par = 0
                for pr in ("g", "u"):
                    b.w("gpsimd", "dv", dv_w)
                    gd_q[(pr, 0)] = b.dma("gpsimd", qb[pr][0][:], qsrc[pr][hc])
                if hc % 8 == 0:
                    hn = min(8, NHC - hc)
                    for pr in ("g", "u"):
                        b.w("gpsimd", "dv", b.c["dv"])
                        b.w("gpsimd", "pe", b.c["pe"])
                        b.w("sync", "dv", b.c["dv"])
                        b.w("sync", "pe", b.c["pe"])
                        ldz = b.dma("gpsimd", zrow[:, 0:hn * 128],
                                    zsrc[pr][:, hc * 128:(hc + hn) * 128])
                        lds = b.dma("sync", srow[pr][:, 0:hn * 128],
                                    ssrc[pr][:, hc * 128:(hc + hn) * 128])
                        b.w("vector", "gd", ldz)
                        b.w("vector", "ld", lds)
                        b.op("vector", lambda e, pr=pr, hn=hn: e.tensor_tensor(
                            zrow[:, 0:hn * 128], zrow[:, 0:hn * 128], srow[pr][:, 0:hn * 128], ALU.mult))
                        b.op("vector", lambda e, pr=pr, hn=hn: e.tensor_scalar(
                            szb[pr][:, 0:hn * 128], zrow[:, 0:hn * 128], -1.0, None, ALU.mult))
                        b.op("vector", lambda e, pr=pr, hn=hn: e.tensor_copy(
                            ssb[pr][:, 0:hn * 128], srow[pr][:, 0:hn * 128]), inc="dv")
                hoff = (hc % 8) * 128
                for pr in ("g", "u"):
                    b.w("tensor", "gd", gd_q[(pr, par)])
                    b.w("tensor", "dv", b.c["dv"])
                    for half in range(4):
                        pev = None
                        for gg in range(8):
                            g_ = half * 8 + gg
                            pev = b.op("tensor", lambda e, g_=g_, gg=gg, pr=pr, hoff=hoff: e.matmul(
                                ps_sb[:, gg * 128:(gg + 1) * 128],
                                Esg[:, g_ * 128:(g_ + 1) * 128],
                                ssb[pr][:, hoff:hoff + 128], start=True, stop=True), inc="pe")
                        b.w("vector", "pe", pev)
                        if hc == 0 and pr == "g" and half == 0 and "d_sb0" in io:
                            dvv = b.op("vector", lambda e: e.tensor_copy(silu_t[:], ps_sb[:, 0:512]), inc="dv")
                            b.w("gpsimd", "dv", dvv)
                            b.dma("gpsimd", io["d_sb0"][:, 0:512], silu_t[:])
                            dvv = b.op("vector", lambda e: e.tensor_copy(silu_t[:], ps_sb[:, 512:1024]), inc="dv")
                            b.w("gpsimd", "dv", dvv)
                            b.dma("gpsimd", io["d_sb0"][:, 512:1024], silu_t[:])
                            b.dma("gpsimd", io["d_ssbg"][:, :], ssb["g"][:])
                            b.dma("gpsimd", io["d_qb0"][:, :], qb["g"][0][:])
                            b.dma("gpsimd", io["d_esg_l"][:, :], Esg[:])
                            b.w("vector", "gd", b.c["gd"])
                        dv_w = b.op("vector", lambda e, pr=pr, half=half: e.tensor_tensor(
                            wpb[pr][:, half * 1024:(half + 1) * 1024],
                            qb[pr][0][:, half * 1024:(half + 1) * 1024],
                            ps_sb[:], ALU.mult), inc="dv")
                        b.w("tensor", "dv", dv_w)
                for t in range(2):
                    for pr, psx in (("g", ps_g), ("u", ps_u)):
                        for k in range(NDC):
                            b.op("tensor", lambda e, pr=pr, psx=psx, k=k, t_=t: e.matmul(
                                psx[:], wpb[pr][:, k * 128:(k + 1) * 128],
                                xnb[:, k * TPC + t_ * 512:k * TPC + (t_ + 1) * 512],
                                start=(k == 0), stop=False))
                        pe_hc = b.op("tensor", lambda e, pr=pr, psx=psx, t_=t, hoff=hoff: e.matmul(
                            psx[:], szb[pr][:, hoff:hoff + 128],
                            P["Sg"][:, t_ * 512:(t_ + 1) * 512], start=False, stop=True), inc="pe")
                    b.w("scalar", "pe", pe_hc)
                    b.w("scalar", "dv", b.c["dv"])
                    acv = b.act(lambda e: e.activation(silu_t[:], ps_g[:], AF.Silu))
                    b.w("vector", "ac", acv)
                    dvv = b.op("vector", lambda e, j=j, t_=t: e.tensor_tensor(
                        hslab[:, j * TPC + t_ * 512:j * TPC + (t_ + 1) * 512],
                        silu_t[:], ps_u[:], ALU.mult), inc="dv")
                    b.w("tensor", "dv", dvv)
                    if DEBUG and hc == 0 and t == 1 and pr == "u" and "d_wg0" in io:
                        b.w("gpsimd", "dv", b.c["dv"])
                        b.dma("gpsimd", io["d_wg0"][:, :], wpb["g"][:])
                        b.dma("gpsimd", io["d_h0"][:, :], hslab[:, 0:TPC])
                    b.op("tensor", lambda e, j=j, t_=t, hc=hc, bs=bs: e.matmul(
                        ps_sh[t_][:bs, :], Edn[:, hc * 8:hc * 8 + bs],
                        hslab[:, j * TPC + t_ * 512:j * TPC + (t_ + 1) * 512],
                        start=(j == 0), stop=(j == bs - 1)),
                        inc="pe" if j == bs - 1 else None)
            b.w("gpsimd", "gd", b.c["gd"])
            b.w("vector", "pe")
            for t in range(2):
                b.op("vector", lambda e, t_=t, bs=bs: e.tensor_copy(
                    shblk[:bs, t_ * 512:(t_ + 1) * 512], ps_sh[t_][:bs, :]), inc="dv")
            gd_dn = [0, 0]
            gd_dn[0] = b.dma("gpsimd", qdb[0][:, 0:bs * 128], io["qd"][0][:, k0 * 128:(k0 + bs) * 128])
            dn_dv = [0, 0]
            for dc in range(NDC):
                if dc % 8 == 0:
                    b.w("gpsimd", "dv", b.c["dv"])
                    b.w("sync", "dv", b.c["dv"])
                    b.w("sync", "pe", b.c["pe"])
                    ldz = b.dma("gpsimd", zdrow[:bs, :], io["zd"][k0:k0 + bs, dc * 128:(dc + 8) * 128])
                    lds = b.dma("sync", sdrow[:bs, :], io["sd"][k0:k0 + bs, dc * 128:(dc + 8) * 128])
                    b.w("vector", "gd", ldz)
                    b.w("vector", "ld", lds)
                    b.w("vector", "pe", b.c["pe"])
                    b.op("vector", lambda e, bs=bs: e.tensor_tensor(
                        zdrow[:bs, :], zdrow[:bs, :], sdrow[:bs, :], ALU.mult))
                    b.op("vector", lambda e, bs=bs: e.tensor_scalar(
                        szd[:bs, :], zdrow[:bs, :], -1.0, None, ALU.mult))
                    b.op("vector", lambda e, bs=bs: e.tensor_copy(sdb[:bs, :], sdrow[:bs, :]), inc="dv")
                if dc + 1 < NDC:
                    b.w("gpsimd", "dv", b.c["dv"])
                    gd_dn[(dc + 1) % 2] = b.dma("gpsimd", qdb[(dc + 1) % 2][:, 0:bs * 128],
                                                io["qd"][dc + 1][:, k0 * 128:(k0 + bs) * 128])
                b.w("tensor", "gd", gd_dn[dc % 2])
                b.w("tensor", "dv", b.c["dv"])
                for c0 in range(0, bs, 8):
                    cnt = min(8, bs - c0)
                    pev = None
                    for kk in range(cnt):
                        jj = c0 + kk
                        pev = b.op("tensor", lambda e, jj=jj, kk=kk, dc=dc, bs=bs: e.matmul(
                            ps_sb[:, kk * 128:(kk + 1) * 128],
                            Esd[:bs, jj * 128:(jj + 1) * 128],
                            sdb[:bs, (dc % 8) * 128:(dc % 8 + 1) * 128], start=True, stop=True), inc="pe")
                    b.w("vector", "pe", pev)
                    dvv = b.op("vector", lambda e, c0=c0, cnt=cnt, dc=dc: e.tensor_tensor(
                        wdb[:, c0 * 128:(c0 + cnt) * 128],
                        qdb[dc % 2][:, c0 * 128:(c0 + cnt) * 128],
                        ps_sb[:, 0:cnt * 128], ALU.mult), inc="dv")
                    b.w("tensor", "dv", dvv)
                if blk_i == 0 and dc in (0, 24) and "d_wd0" in io:
                    b.w("gpsimd", "dv", b.c["dv"])
                    b.dma("gpsimd", io["d_wd0" if dc == 0 else "d_wd24"][:, :], wdb[:, 0:1024])
                    if dc == 0:
                        b.dma("gpsimd", io["d_sdb"][:, 0:1024], sdb[:8, :])
                        b.dma("gpsimd", io["d_szd"][:, 0:1024], szd[:8, :])
                        b.dma("gpsimd", io["d_shblk"][:, :], shblk[:8, :])
                    b.w("tensor", "gd", b.c["gd"])
                pe_dn = 0
                for t in range(2):
                    b.w("tensor", "dv", dn_dv[t])
                    for jj in range(bs):
                        b.op("tensor", lambda e, jj=jj, t_=t: e.matmul(
                            ps_dn[t_][:], wdb[:, jj * 128:(jj + 1) * 128],
                            hslab[:, jj * TPC + t_ * 512:jj * TPC + (t_ + 1) * 512],
                            start=(jj == 0), stop=False))
                    pe_dn = b.op("tensor", lambda e, t_=t, dc=dc, bs=bs: e.matmul(
                        ps_dn[t_][:], szd[:bs, (dc % 8) * 128:(dc % 8 + 1) * 128],
                        shblk[:bs, t_ * 512:(t_ + 1) * 512], start=False, stop=True), inc="pe")
                b.w("vector", "pe", pe_dn)
                b.w("vector", "gd", b.c["gd"] if dc >= 2 else 0)
                dnb = dnf[dc % 2]
                for t in range(2):
                    dn_dv[t] = b.op("vector", lambda e, t_=t, dnb=dnb: e.tensor_copy(
                        dnb[:, t_ * 512:(t_ + 1) * 512], ps_dn[t_][:]), inc="dv")
                b.w("gpsimd", "dv", dn_dv[1])
                b.dma("gpsimd", io["ACC"][dc], dnb[:], accum_op=ALU.add)
        b.bar()


def _combine_fix(b, io, P, Bb, Cb):
    ph = ExitStack()
    with ph:
        sb = lambda nm, sh, dt: ph.enter_context(b.nc.sbuf_tensor(nm, sh, dt))
        stag = sb("c_stag", [128, 544], F32)
        stag2 = sb("c_stag2", [128, 544], F32)
        Sng = sb("c_sng", [128, 512], F32)
        Tg = sb("c_tg", [128, 32], F32)
        Tsh = sb("c_tsh", [128, 32], F32)
        DD = sb("c_dd", [128, 512], F32)
        Ssh = sb("c_ssh", [128, 512], F32)
        Hin = sb("c_hin", [128, 512], F32)
        Hinb = sb("c_hinb", [128, 512], BF16)
        tmpb = sb("c_tmpb", [128, 512], BF16)
        Hsel = sb("c_hsel", [128, 128], F32)
        hmskb = sb("c_hmsk", [128, NSLOT * 512], BF16)
        dtf = sb("c_dtf", [128, TPC], F32)
        tA = sb("c_tA", [128, TPC], F32)
        tB = sb("c_tB", [128, TPC], F32)
        cum = sb("c_cum", [128, TPC], F32)
        Gbf = sb("c_gbf", [128, TPC], BF16)
        Hb = sb("c_hb", [128, TPC], BF16)
        fix1 = sb("c_fix1", [128, TPC], BF16)
        ych = [sb(f"c_ych{i}", [128, TPC], F32) for i in range(2)]
        ps_dt = [ph.enter_context(b.nc.psum_tensor(f"c_psdt{t}", [128, 512], F32)) for t in range(2)]

        for s in range(NSLOT):
            b.dma("sync", hmskb[:, s * 512:(s + 1) * 512], io["hmsk"][s])
        b.bar(engs=("vector", "gpsimd"))

        gd_ck = [0, 0]
        pl_stag = 0
        dv_stag = 0
        for dc in range(NDC):
            b.w("sync", "pl", pl_stag)
            for qq in range(4):
                b.dma("sync", stag[:, qq * 136:(qq + 1) * 136],
                      io["stg"][qq * DM + dc * 128:qq * DM + (dc + 1) * 128, :])
            b.w("gpsimd", "ld", b.c["ld"])
            b.w("gpsimd", "dv", dv_stag)
            plv = None
            for g in range(32):
                qp, sp = G2QS[g]
                plv = b.op("gpsimd", lambda e, g=g, qp=qp, sp=sp: e.tensor_copy(
                    stag2[:, g * 17:(g + 1) * 17],
                    stag[:, qp * 136 + sp * 17:qp * 136 + sp * 17 + 17]), inc="pl")
            pl_stag = plv
            b.w("vector", "pl", plv)
            st3 = stag2[:].rearrange("p (g c) -> p c g", c=17)
            b.op("vector", lambda e, st3=st3: e.tensor_copy(
                Sng[:].rearrange("p (n g) -> p n g", g=32), st3[:, 0:16, :]))
            dv_stag = b.op("vector", lambda e, st3=st3: e.tensor_copy(Tg[:], st3[:, 16, :]), inc="dv")
            if dc == 0 and "d_tg0" in io:
                b.w("sync", "dv", dv_stag)
                b.dma("sync", io["d_stag20"], stag2[:])
                b.dma("sync", io["d_sng0"], Sng[:])
                b.dma("sync", io["d_tg0"], Tg[:])
            b.op("vector", lambda e: e.memset(Tsh[:, 0:1], 0.0))
            b.op("vector", lambda e: e.tensor_copy(Tsh[:, 1:32], Tg[:, 0:31]))
            b.op("vector", lambda e: e.drain())
            for n in range(NST):
                b.op("vector", lambda e, n=n, dc=dc: e.tensor_scalar(
                    DD[:, n * 32:(n + 1) * 32], Tsh[:], _col(P["A"], dc * NST + n), None, ALU.mult),
                    inc="dv" if n == NST - 1 else None)
            b.w("scalar", "dv")
            acv = b.act(lambda e: e.activation(DD[:], DD[:], AF.Exp))
            b.op("vector", lambda e: e.memset(Ssh[:], 0.0))
            b.op("vector", lambda e: e.tensor_copy(
                Ssh[:].rearrange("p (n g) -> p n g", g=32)[:, :, 1:32],
                Sng[:].rearrange("p (n g) -> p n g", g=32)[:, :, 0:31]))
            b.op("vector", lambda e: e.drain())
            b.w("vector", "ac", acv)
            for n in range(NST):
                b.op("vector", lambda e, n=n: e.memset(DD[:, n * 32:n * 32 + 1], 0.0))
            b.op("vector", lambda e: e.drain())
            dvv = b.op("vector", lambda e: e.tensor_tensor_scan(
                Hin[:], DD[:], Ssh[:], 0.0, ALU.mult, ALU.add), inc="dv")
            b.w("vector", "dv", dvv)
            b.op("vector", lambda e: e.tensor_copy(Hinb[:], Hin[:]), inc="dv")
            if dc == 0 and "d_hin0" in io:
                b.w("sync", "dv", b.c["dv"])
                b.dma("sync", io["d_hin0"], Hin[:])
            b.w("gpsimd", "dv", b.c["dv"])
            for s in range(NSLOT):
                b.w("gpsimd", "dv", b.c["dv"])
                plv = b.op("gpsimd", lambda e, s=s: e.tensor_tensor(
                    tmpb[:], Hinb[:], hmskb[:, s * 512:(s + 1) * 512], ALU.mult), inc="pl")
                b.w("vector", "pl", plv)
                b.op("vector", lambda e, s=s: e.tensor_reduce(
                    Hsel[:, s * 16:(s + 1) * 16],
                    tmpb[:].rearrange("p (n g) -> p n g", g=32), axis=AX, op=ALU.add),
                    inc="dv")
            if dc == 0 and "d_hsel0" in io:
                b.w("sync", "dv", b.c["dv"])
                b.dma("sync", io["d_hsel0"], Hsel[:])
            yc = ych[dc % 2]
            b.w("vector", "gd", gd_ck[dc % 2])
            b.w("tensor", "dv", b.c["dv"])
            _dt_chunk(b, P, dc, ps_dt, dtf, tA, tB)
            for s in range(NSLOT):
                dvv = b.op("vector", lambda e, s=s: e.tensor_tensor_scan(
                    cum[:, s * TILE:(s + 1) * TILE], P["ones128t"][:],
                    dtf[:, s * TILE:(s + 1) * TILE], 0.0, ALU.mult, ALU.add),
                    inc="dv" if s == NSLOT - 1 else None)
            b.w("vector", "dv", dvv)
            if dc == 0 and "d_cum0" in io:
                b.w("sync", "dv", b.c["dv"])
                b.dma("sync", io["d_cum0"], cum[:])
                b.dma("sync", io["d_dtf0"], dtf[:])
                b.w("gpsimd", "ld", b.c["ld"])
                b.w("vector", "ld", b.c["ld"])
            pl_fix = 0
            for n in range(NST):
                dvv = b.op("vector", lambda e, n=n, dc=dc: e.tensor_scalar(
                    tA[:], cum[:], _col(P["A"], dc * NST + n), None, ALU.mult), inc="dv")
                b.w("scalar", "dv", dvv)
                b.w("scalar", "pl", pl_fix)
                acv = b.act(lambda e: e.activation(Gbf[:], tA[:], AF.Exp))
                hsrc = Hsel[:].rearrange("p (s n) -> p s n", n=NST)[:, :, n].unsqueeze(2).to_broadcast((128, 8, 128))
                b.w("vector", "pl", pl_fix)
                b.op("vector", lambda e, hsrc=hsrc: e.tensor_copy(
                    Hb[:].rearrange("p (s t) -> p s t", t=TILE), hsrc), inc="dv")
                b.w("gpsimd", "ac", acv)
                b.w("gpsimd", "dv", b.c["dv"])
                pl_fix = b.op("gpsimd", lambda e: e.tensor_tensor(fix1[:], Gbf[:], Hb[:], ALU.mult),
                           inc="pl")
                plv = pl_fix
                b.w("vector", "pl", plv)
                dvv = b.op("vector", lambda e, n=n: e.tensor_tensor(
                    fix1[:], fix1[:], Cb[:, n * TPC:(n + 1) * TPC], ALU.mult), inc="dv")
                b.w("gpsimd", "dv", dvv)
                if n == 0:
                    plv = b.op("gpsimd", lambda e, yc=yc: e.tensor_copy(yc[:], fix1[:]), inc="pl")
                else:
                    plv = b.op("gpsimd", lambda e, yc=yc: e.tensor_tensor(
                        yc[:], yc[:], fix1[:], ALU.add), inc="pl")
                b.w("vector", "pl", plv)
            b.w("gpsimd", "pl", b.c["pl"])
            b.dma("gpsimd", io["ACC"][dc], yc[:], accum_op=ALU.add)
            gd_ck[dc % 2] = b.c["gd"]
        b.bar()


def _l1_phases(b, io, P, mode):
    nc = b.nc
    t1s_g = b.sb("t1s_g", [128, TPC], BF16) if mode in ("L3", "ALL") else None
    if mode in ("L2", "ALL"):
        ph = ExitStack()
        with ph:
            sb = lambda nm, sh, dt: ph.enter_context(nc.sbuf_tensor(nm, sh, dt))
            pstn = lambda nm, sh: ph.enter_context(nc.psum_tensor(nm, sh, F32))
            xnb1 = sb("q_xnb1", [128, NDC * TPC], BF16)
            mb1 = sb("q_mb1", [128, TPC], F32)
            rstdb1 = sb("q_rstdb1", [128, TPC], F32)
            _ln_phase(b, P, io["ACC"], io, with_sg=False, with_xp=False, xnb=xnb1,
                      mb=mb1, rstdb=rstdb1)
            import os
            cut = os.environ.get("K_L2CUT", "") if mode == "L2" else ""
            if cut == "ln1":
                for dc in range(NDC):
                    b.dma("gpsimd", io["xn1o"][dc], xnb1[:, dc * TPC:(dc + 1) * TPC])
                b.w("gpsimd", "gd", b.c["gd"])
                b.w("gpsimd", "ld", b.c["ld"])
                return

            wqb = [sb(f"q_wq{i}", [128, NDC * 128], BF16) for i in range(2)]
            bq = sb("q_bq", [128, NQKV], F32)
            evb = [sb(f"q_ev{i}", [128, TPC], BF16) for i in range(2)]
            qtb = sb("q_qtb", [128, NCC * TPC], BF16)
            arsb = sb("q_arsb", [128, NDC], F32)
            armul = sb("q_armul", [128, TPC], F32)
            wvbs = sb("q_wvb", [128, TPC], F32)
            t1f = sb("q_t1f", [128, TPC], F32)
            acb = sb("q_acb", [128, NDC * 128], BF16)
            ps_q = [[pstn(f"q_psq{i}{t}", [128, 512]) for t in range(2)] for i in range(2)]

            b.dma("sync", bq[:], io["bqkv_t"])
            b.dma("sync", wvbs[:], io["wvb"])
            b.bar(engs=("vector", "gpsimd"))
            for dc in range(NDC):
                b.op("vector", lambda e, dc=dc: e.tensor_tensor(
                    armul[:], xnb1[:, dc * TPC:(dc + 1) * TPC], wvbs[:], ALU.mult))
                b.op("vector", lambda e, dc=dc: e.tensor_reduce(
                    arsb[:, dc:dc + 1], armul[:], axis=AX, op=ALU.add),
                    inc="dv" if dc == NDC - 1 else None)
            if cut == "rt":
                b.w("gpsimd", "dv", b.c["dv"])
                b.dma("gpsimd", io["arpart"][:, :], arsb[:])
                for dc in range(NDC):
                    b.dma("gpsimd", io["xn1o"][dc], xnb1[:, dc * TPC:(dc + 1) * TPC])
                b.w("gpsimd", "gd", b.c["gd"])
                b.w("gpsimd", "ld", b.c["ld"])
                return
            gd_w = [0, 0]
            gd_w[0] = b.dma("gpsimd", wqb[0][:].rearrange("p (c r) -> p c r", r=128),
                            io["wqkvT"].rearrange("c p (o r) -> p c r o", r=128)[:, :, :, 0])
            pe_oc = 0
            dv_oc = [0, 0]
            for oc in range(NQKV):
                if oc + 1 < NQKV:
                    b.w("gpsimd", "pe", pe_oc)
                    gd_w[(oc + 1) % 2] = b.dma(
                        "gpsimd", wqb[(oc + 1) % 2][:].rearrange("p (c r) -> p c r", r=128),
                        io["wqkvT"].rearrange("c p (o r) -> p c r o", r=128)[:, :, :, oc + 1])
                b.w("tensor", "gd", gd_w[oc % 2])
                b.w("tensor", "dv", dv_oc[oc % 2])
                for t in range(2):
                    for k in range(NDC):
                        b.op("tensor", lambda e, t_=t, k=k, oc=oc: e.matmul(
                            ps_q[oc % 2][t_][:], wqb[oc % 2][:, k * 128:(k + 1) * 128],
                            xnb1[:, k * TPC + t_ * 512:k * TPC + (t_ + 1) * 512],
                            start=(k == 0), stop=(k == NDC - 1)),
                            inc="pe" if k == NDC - 1 else None)
                pe_oc = b.c["pe"]
                b.w("vector", "pe", pe_oc)
                if oc < NCC:
                    dst = [qtb[:, oc * TPC + t * 512:oc * TPC + (t + 1) * 512] for t in range(2)]
                else:
                    dst = [evb[oc % 2][:, t * 512:(t + 1) * 512] for t in range(2)]
                    b.w("vector", "gd", b.c["gd"])
                for t in range(2):
                    dvv = b.op("vector", lambda e, t_=t, oc=oc, d=dst[t]: e.tensor_scalar(
                        d, ps_q[oc % 2][t_][:], _col(bq, oc), None, ALU.add), inc="dv")
                dv_oc[oc % 2] = dvv
                if oc >= NCC:
                    b.w("gpsimd", "dv", dvv)
                    b.dma("gpsimd", io["kvloc"][(oc - NCC) * 128:(oc - NCC + 1) * 128, :],
                          evb[oc % 2][:])
            if cut == "qkv":
                b.w("gpsimd", "dv", b.c["dv"])
                for cc in range(NCC):
                    b.dma("gpsimd", io["qto"][cc], qtb[:, cc * TPC:(cc + 1) * TPC])
                for dc in range(NDC):
                    b.dma("gpsimd", io["xn1o"][dc], xnb1[:, dc * TPC:(dc + 1) * TPC])
                b.dma("gpsimd", io["arpart"][:, :], arsb[:])
                b.w("gpsimd", "gd", b.c["gd"])
                b.w("gpsimd", "ld", b.c["ld"])
                return
            b.dma("gpsimd", acb[:].rearrange("p (c r) -> p c r", r=128), io["AcatT"].rearrange("c p r -> p c r"))
            b.w("tensor", "gd", b.c["gd"])
            b.w("tensor", "dv", b.c["dv"])
            for t in range(2):
                for k in range(NDC):
                    b.op("tensor", lambda e, t_=t, k=k: e.matmul(
                        ps_q[0][t_][:], acb[:, k * 128:(k + 1) * 128],
                        xnb1[:, k * TPC + t_ * 512:k * TPC + (t_ + 1) * 512],
                        start=(k == 0), stop=(k == NDC - 1)),
                        inc="pe" if k == NDC - 1 else None)
            b.w("vector", "pe")
            for t in range(2):
                b.op("vector", lambda e, t_=t: e.tensor_copy(
                    t1f[:, t_ * 512:(t_ + 1) * 512], ps_q[0][t_][:]), inc="dv")
            if mode == "L2":
                b.w("gpsimd", "dv", b.c["dv"])
                for dc in range(NDC):
                    b.dma("gpsimd", io["xn1o"][dc], xnb1[:, dc * TPC:(dc + 1) * TPC])
                for cc in range(NCC):
                    b.dma("gpsimd", io["qto"][cc], qtb[:, cc * TPC:(cc + 1) * TPC])
                b.dma("gpsimd", io["arpart"][:, :], arsb[:])
                b.dma("gpsimd", io["t1o"][:, :], t1f[:])
                b.w("gpsimd", "gd", b.c["gd"])
                b.w("gpsimd", "ld", b.c["ld"])
            else:
                b.w("gpsimd", "dv", b.c["dv"])
                for dc in range(NDC):
                    b.dma("gpsimd", io["xn1d"][dc], xnb1[:, dc * TPC:(dc + 1) * TPC])
                b.dma("gpsimd", io["ar_in"][:, :], arsb[:])
                b.w("gpsimd", "gd", b.c["gd"])
                b.op("gpsimd", lambda e: e.collective_compute(
                    "AllGather", ALU.bypass, replica_groups=[[0, 1, 2, 3], [4, 5, 6, 7]],
                    ins=[io["kvloc"].opt()], outs=[io["kvg"].opt()]), inc="cc", drain=False)
                b.op("gpsimd", lambda e: e.collective_compute(
                    "AllReduce", ALU.add, replica_groups=[[0, 1, 2, 3], [4, 5, 6, 7]],
                    ins=[io["ar_in"].opt()], outs=[io["ar_out"].opt()]), inc="cc", drain=False)
                for cc in range(NCC):
                    b.dma("gpsimd", io["qtd"][cc], qtb[:, cc * TPC:(cc + 1) * TPC])
                b.dma("gpsimd", io["t1d"][:, :], t1f[:])
                b.w("gpsimd", "gd", b.c["gd"])
            b.bar()

    if mode not in ("L3", "ALL"):
        return

    import os as _os
    cut3 = _os.environ.get("K_L3CUT", "") if mode == "L3" else ""

    def fin():
        b.dma("gpsimd", io["out_t"].rearrange("c p t -> (c p) t"),
              io["ACC"].rearrange("c p t -> (c p) t"), sem="so")
        b.w("gpsimd", "so", b.c["so"])

    qsrcs = io["qti"] if mode == "L3" else io["qtd"]
    kvg = io["kvg"]
    ph = ExitStack()
    with ph:
        sb = lambda nm, sh, dt: ph.enter_context(nc.sbuf_tensor(nm, sh, dt))
        pstn = lambda nm, sh, dt=F32: ph.enter_context(nc.psum_tensor(nm, sh, dt))
        qtb = sb("a_qtb", [128, NCC * TPC], BF16)
        Oall = sb("a_oall", [128, 64 * 256], F32)
        mall = sb("a_mall", [128, 64], F32)
        lall = sb("a_lall", [128, 64], F32)
        ksl = sb("a_ksl", [128, NCC * 512], BF16)
        vsl = [sb(f"a_vsl{j}", [128, 2048], BF16) for j in range(4)]
        smf = sb("a_smf", [128, 512], F32)
        pbf = sb("a_pbf", [128, 512], BF16)
        ptb = sb("a_ptb", [128, 512], BF16)
        maskb = sb("a_maskb", [128, NSLOT * 512], F32)
        cmax = sb("a_cmax", [128, 1], F32)
        mnew = sb("a_mnew", [128, 1], F32)
        negm = sb("a_negm", [128, 1], F32)
        alph = sb("a_alph", [128, 1], F32)
        rs = sb("a_rs", [128, 1], F32)
        rl = sb("a_rl", [128, 1], F32)
        vvb = sb("a_vvb", [128, 256], BF16)
        arsb2 = sb("a_arsb", [128, NDC], F32)
        ar4 = sb("a_ar4", [128, NDC], F32)
        qcolb = sb("a_qcol", [128, NDC], BF16)
        rwb = sb("a_rwb", [128, NDC * 8], BF16)
        rbc = sb("a_rbc", [8, 1], F32)
        lg = sb("a_lg", [8, 1], F32)
        l8 = sb("a_l8", [1, 8], F32)
        e8 = sb("a_e8", [1, 8], F32)
        sm1 = sb("a_sm1", [1, 1], F32)
        sm1b = sb("a_sm1b", [1, 1], F32)
        e8b = sb("a_e8b", [1, 8], F32)
        rcol = sb("a_rcol", [8, 1], F32)
        rvc = sb("a_rvc", [128, 1], F32)
        ervb = sb("a_ervb", [8, 128], F32)
        t1f = sb("a_t1f", [128, TPC], F32)
        t1s = t1s_g
        dsm = sb("a_dsm", [128, 512], F32) if "d_smf00" in io else None
        dpb = sb("a_dpb", [128, 512], BF16) if "d_smf00" in io else None
        dob = sb("a_dob", [128, 256], F32) if "d_smf00" in io else None
        ps_s = pstn("a_pss", [128, 512])
        ps_pt = pstn("a_pspt", [128, 512], BF16)
        ps_av = pstn("a_psav", [128, 256])
        ps_r = pstn("a_psr", [8, 1])
        ps_rv = pstn("a_psrv", [128, 1])

        for s in range(NSLOT):
            b.dma("sync", maskb[:, s * 512:(s + 1) * 512], io["masks"][s])
        for cc in range(NCC):
            b.dma("sync", qtb[:, cc * TPC:(cc + 1) * TPC], qsrcs[cc])
        b.dma("sync", ervb[:], io["Erv"])
        b.dma("sync", rbc[:], io["rb_t"])
        b.dma("sync", t1f[:], io["t1i"][:, :] if mode == "L3" else io["t1d"][:, :])
        b.dma("gpsimd", rwb[:].rearrange("p (c r) -> p c r", r=8), io["rWT"].rearrange("c p r -> p c r"))
        if mode == "L3":
            b.dma("sync", arsb2[:], io["arall"][0:128, :])
            for qq in range(1, 4):
                b.dma("sync", ar4[:], io["arall"][qq * 128:(qq + 1) * 128, :])
                b.w("vector", "ld", b.c["ld"])
                b.op("vector", lambda e: e.tensor_tensor(arsb2[:], arsb2[:], ar4[:], ALU.add),
                     inc="dv")
                b.w("sync", "dv", b.c["dv"])
        else:
            b.w("sync", "cc", b.c["cc"])
            b.dma("sync", arsb2[:], io["ar_out"][:, :])
        b.bar(engs=("vector", "tensor", "scalar", "gpsimd"))
        b.op("vector", lambda e: e.tensor_scalar(qcolb[:], arsb2[:], R_INV, None, ALU.mult), inc="dv")
        b.w("tensor", "dv")
        for dc in range(NDC):
            b.op("tensor", lambda e, dc=dc: e.matmul(
                ps_r[:], rwb[:, dc * 8:(dc + 1) * 8], qcolb[:, dc:dc + 1],
                start=(dc == 0), stop=(dc == NDC - 1)), inc="pe" if dc == NDC - 1 else None)
        b.w("vector", "pe")
        b.op("vector", lambda e: e.tensor_tensor(lg[:], ps_r[:], rbc[:], ALU.add), inc="dv")
        b.w("gpsimd", "dv", b.c["dv"])
        b.dma("gpsimd", io["bnc2"], lg[:], sem="gd")
        b.w("sync", "gd", b.c["gd"])
        b.dma("sync", l8[:], io["bnc2"].rearrange("a b -> b a"))
        b.w("vector", "ld", b.c["ld"])
        b.op("vector", lambda e: e.reduce_max(sm1[:], l8[:], axis=AX))
        b.op("vector", lambda e: e.drain())
        b.op("vector", lambda e: e.tensor_scalar(sm1[:], sm1[:], -1.0, None, ALU.mult), inc="dv")
        b.w("scalar", "dv", b.c["dv"])
        b.act(lambda e: e.activation(e8[:], l8[:], AF.Exp, bias=sm1[:]))
        b.w("vector", "ac", b.c["ac"])
        b.op("vector", lambda e: e.tensor_reduce(sm1[:], e8[:], axis=AX, op=ALU.add))
        b.op("vector", lambda e: e.drain())
        b.op("vector", lambda e: e.reciprocal(sm1b[:], sm1[:]))
        b.op("vector", lambda e: e.drain())
        b.op("vector", lambda e: e.tensor_scalar(e8b[:], e8[:], sm1b[:], 2.0, ALU.mult, ALU.mult),
             inc="dv")
        b.w("gpsimd", "dv", b.c["dv"])
        b.dma("gpsimd", io["bnc1"], e8b[:], sem="gd")
        b.w("sync", "gd", b.c["gd"])
        b.dma("sync", rcol[:], io["bnc1"].rearrange("a b -> b a"))
        b.w("tensor", "ld", b.c["ld"])
        b.op("tensor", lambda e: e.matmul(ps_rv[:], ervb[:], rcol[:], start=True, stop=True),
             inc="pe")
        b.w("vector", "pe")
        b.op("vector", lambda e: e.tensor_copy(rvc[:], ps_rv[:]))
        b.op("vector", lambda e: e.drain())
        b.op("vector", lambda e: e.tensor_scalar(t1s[:], t1f[:], rvc[:], None, ALU.mult), inc="dv")
        if cut3 == "rt":
            b.w("gpsimd", "dv", b.c["dv"])
            b.w("gpsimd", "ld", b.c["ld"])
            if "d_rvc" in io:
                b.dma("gpsimd", io["d_rvc"], rvc[:])
                b.dma("gpsimd", io["d_lg"], lg[:])
                b.dma("gpsimd", io["d_l8"], l8[:])
                b.dma("gpsimd", io["d_e8"], e8[:])
                b.dma("gpsimd", io["d_e8b"], e8b[:])
                b.dma("gpsimd", io["d_sm1"], sm1[:])
                b.dma("gpsimd", io["d_sm1b"], sm1b[:])
                b.dma("gpsimd", io["d_rcol"], rcol[:])
            b.w("gpsimd", "gd", b.c["gd"])
            fin()
            b.w("gpsimd", "ld", b.c["ld"])
            return

        ld_fc = 0
        for fc in range(NSLOT):
            if fc > 0:
                b.w("sync", "pe", b.c["pe"])
            for j in range(4):
                tg = fc * 4 + j
                qp, sp = G2QS[tg]
                base = qp * DM
                src = kvg[base:base + 2048, sp * 128:(sp + 1) * 128]
                b.dma("sync", ksl[:].rearrange("p (c w) -> p c w", w=512)[:, :, j * 128:(j + 1) * 128],
                      src.rearrange("(c p) f -> p c f", p=128))
                b.dma("sync", vsl[j][:], kvg[base + 2048:base + 4096, sp * 128:(sp + 1) * 128],
                      transpose=True)
            ld_fc = b.c["ld"]
            b.w("tensor", "ld", ld_fc)
            for s in range(fc, NSLOT):
                for h in range(HEADS):
                    ii = s * 8 + h
                    pev = None
                    for kc in range(2):
                        cc = 2 * h + kc
                        pev = b.op("tensor", lambda e, cc=cc, s=s, kc=kc: e.matmul(
                            ps_s[:], qtb[:, cc * TPC + s * 128:cc * TPC + (s + 1) * 128],
                            ksl[:, cc * 512:(cc + 1) * 512], start=(kc == 0), stop=(kc == 1)),
                            inc="pe")
                    b.w("vector", "pe", pev)
                    if fc == s:
                        b.op("vector", lambda e, s=s: e.tensor_tensor(
                            smf[:], ps_s[:], maskb[:, s * 512:(s + 1) * 512], ALU.add))
                    else:
                        b.op("vector", lambda e: e.tensor_copy(smf[:], ps_s[:]))
                    dvv = b.op("vector", lambda e: e.reduce_max(cmax[:], smf[:], axis=AX), inc="dv")
                    if fc == 0 and s == 0 and h == 0 and dsm is not None:
                        b.op("vector", lambda e: e.tensor_copy(dsm[:], smf[:]))
                    if fc == 0:
                        b.op("vector", lambda e, ii=ii: e.tensor_copy(_col(mall, ii), cmax[:]))
                        b.op("vector", lambda e: e.tensor_scalar(
                            negm[:], cmax[:], -1.0, None, ALU.mult), inc="dv")
                    else:
                        b.op("vector", lambda e, ii=ii: e.tensor_tensor(
                            mnew[:], _col(mall, ii), cmax[:], ALU.max))
                        b.op("vector", lambda e: e.drain())
                        b.op("vector", lambda e, ii=ii: e.tensor_tensor(
                            alph[:], _col(mall, ii), mnew[:], ALU.subtract))
                        b.op("vector", lambda e, ii=ii: e.tensor_copy(_col(mall, ii), mnew[:]))
                        b.op("vector", lambda e: e.tensor_scalar(
                            negm[:], mnew[:], -1.0, None, ALU.mult), inc="dv")
                        b.w("scalar", "dv", b.c["dv"])
                        acv = b.act(lambda e: e.activation(alph[:], alph[:], AF.Exp))
                        b.w("vector", "ac", acv)
                        b.op("vector", lambda e, ii=ii: e.tensor_scalar(
                            Oall[:, ii * 256:(ii + 1) * 256], Oall[:, ii * 256:(ii + 1) * 256],
                            alph[:], None, ALU.mult))
                        b.op("vector", lambda e, ii=ii: e.tensor_scalar(
                            _col(lall, ii), _col(lall, ii), alph[:], None, ALU.mult), inc="dv")
                    b.w("scalar", "dv", b.c["dv"])
                    acv = b.act(lambda e: e.activation(
                        pbf[:], smf[:], AF.Exp, bias=negm[:]))
                    b.w("vector", "ac", acv)
                    if fc == 0 and s == 0 and h == 0 and dsm is not None:
                        b.op("vector", lambda e: e.tensor_copy(dpb[:], pbf[:]))
                    b.op("vector", lambda e: e.tensor_reduce(rs[:], pbf[:], axis=AX, op=ALU.add))
                    b.op("vector", lambda e: e.drain())
                    if fc == 0:
                        b.op("vector", lambda e, ii=ii: e.tensor_copy(_col(lall, ii), rs[:]),
                             inc="dv")
                    else:
                        b.op("vector", lambda e, ii=ii: e.tensor_tensor(
                            _col(lall, ii), _col(lall, ii), rs[:], ALU.add), inc="dv")
                    b.w("tensor", "dv", b.c["dv"])
                    pev = None
                    for j in range(4):
                        pev = b.op("tensor", lambda e, j=j: e.transpose(
                            ps_pt[:, j * 128:(j + 1) * 128], pbf[:, j * 128:(j + 1) * 128],
                            P["ident"][:]), inc="pe")
                    b.w("vector", "pe", pev)
                    dvv = b.op("vector", lambda e: e.tensor_copy(ptb[:], ps_pt[:]), inc="dv")
                    b.w("tensor", "dv", dvv)
                    pev = None
                    for j in range(4):
                        pev = b.op("tensor", lambda e, j=j, h=h: e.matmul(
                            ps_av[:], ptb[:, j * 128:(j + 1) * 128],
                            vsl[j][:, h * 256:(h + 1) * 256], start=(j == 0), stop=(j == 3)),
                            inc="pe")
                    b.w("vector", "pe", pev)
                    if fc == 0:
                        b.op("vector", lambda e, ii=ii: e.tensor_copy(
                            Oall[:, ii * 256:(ii + 1) * 256], ps_av[:]), inc="dv")
                        if s == 0 and h == 0 and dsm is not None:
                            b.op("vector", lambda e: e.tensor_copy(dob[:], ps_av[:]))
                    else:
                        b.op("vector", lambda e, ii=ii: e.tensor_tensor(
                            Oall[:, ii * 256:(ii + 1) * 256], Oall[:, ii * 256:(ii + 1) * 256],
                            ps_av[:], ALU.add), inc="dv")
                    b.w("tensor", "dv", b.c["dv"])
        for s in range(NSLOT):
            for h in range(HEADS):
                ii = s * 8 + h
                b.op("vector", lambda e, ii=ii: e.reciprocal(rl[:], _col(lall, ii)))
                b.op("vector", lambda e: e.drain())
                b.op("vector", lambda e, ii=ii: e.tensor_scalar(
                    vvb[:], Oall[:, ii * 256:(ii + 1) * 256], rl[:], None, ALU.mult), inc="dv")
                b.w("tensor", "dv", b.c["dv"])
                for kc in range(2):
                    cc = 2 * h + kc
                    pev = b.op("tensor", lambda e, kc=kc: e.transpose(
                        ps_pt[:, kc * 128:(kc + 1) * 128], vvb[:, kc * 128:(kc + 1) * 128],
                        P["ident"][:]), inc="pe")
                b.w("vector", "pe", b.c["pe"])
                for kc in range(2):
                    cc = 2 * h + kc
                    b.op("vector", lambda e, cc=cc, s=s, kc=kc: e.tensor_copy(
                        qtb[:, cc * TPC + s * 128:cc * TPC + (s + 1) * 128],
                        ps_pt[:, kc * 128:(kc + 1) * 128]), inc="dv")
                b.w("tensor", "dv", b.c["dv"])
        b.bar()
        if cut3 == "fa":
            if "d_vals" in io:
                b.dma("gpsimd", io["d_vals"][:, :], qtb[:])
                b.dma("gpsimd", io["d_smf00"], dsm[:])
                b.dma("gpsimd", io["d_pb00"], dpb[:])
                b.dma("gpsimd", io["d_o00"], dob[:])
                b.dma("gpsimd", io["d_mall"], mall[:])
                b.dma("gpsimd", io["d_lall"], lall[:])
                b.dma("gpsimd", io["d_ksl0"], ksl[:, 0:512])
                b.dma("gpsimd", io["d_vsl0"], vsl[0][:, 0:256])
                b.dma("gpsimd", io["d_q00"], qtb[:, 0:128])
            b.w("gpsimd", "gd", b.c["gd"])
            fin()
            b.w("gpsimd", "ld", b.c["ld"])
            return
        wob = [sb(f"a_wo{i}", [128, NCC * 128], BF16) for i in range(2)]
        bob = sb("a_bob", [128, NDC], F32)
        evo = [sb(f"a_evo{i}", [128, TPC], F32) for i in range(2)]
        ps_o = [pstn(f"a_pso{t}", [128, 512]) for t in range(2)]
        b.dma("sync", bob[:], io["bo_t"])
        gd_w = [0, 0]
        gd_w[0] = b.dma("gpsimd", wob[0][:].rearrange("p (c r) -> p c r", r=128),
                        io["woT"].rearrange("c p (o r) -> p c r o", r=128)[:, :, :, 0])
        pe_oc = 0
        dv_oc = [0, 0]
        for oc in range(NDC):
            if oc + 1 < NDC:
                b.w("gpsimd", "pe", pe_oc)
                gd_w[(oc + 1) % 2] = b.dma(
                    "gpsimd", wob[(oc + 1) % 2][:].rearrange("p (c r) -> p c r", r=128),
                    io["woT"].rearrange("c p (o r) -> p c r o", r=128)[:, :, :, oc + 1])
            b.w("tensor", "gd", gd_w[oc % 2])
            b.w("tensor", "dv", b.c["dv"])
            for t in range(2):
                for k in range(NCC):
                    b.op("tensor", lambda e, t_=t, k=k, oc=oc: e.matmul(
                        ps_o[t_][:], wob[oc % 2][:, k * 128:(k + 1) * 128],
                        qtb[:, k * TPC + t_ * 512:k * TPC + (t_ + 1) * 512],
                        start=(k == 0), stop=(k == NCC - 1)),
                        inc="pe" if k == NCC - 1 else None)
            pe_oc = b.c["pe"]
            b.w("vector", "pe", pe_oc)
            evv = evo[oc % 2]
            for t in range(2):
                dvv = b.op("vector", lambda e, t_=t, oc=oc, ev=evv: e.tensor_scalar(
                    ev[:, t_ * 512:(t_ + 1) * 512], ps_o[t_][:], _col(bob, oc), None, ALU.add),
                    inc="dv")
            dv_oc[oc % 2] = dvv
            b.w("gpsimd", "dv", dvv)
            b.w("gpsimd", "gd", gd_w[oc % 2])
            b.dma("gpsimd", io["ACC"][oc], evv[:], accum_op=ALU.add)
        b.bar()
        if cut3 == "op":
            fin()
            b.w("gpsimd", "ld", b.c["ld"])
            return

    ph2 = ExitStack()
    with ph2:
        sb = lambda nm, sh, dt: ph2.enter_context(nc.sbuf_tensor(nm, sh, dt))
        pstn = lambda nm, sh: ph2.enter_context(nc.psum_tensor(nm, sh, F32))
        xnb2 = sb("w_xnb", [128, NDC * TPC], BF16)
        wwb = [sb(f"w_ww{i}", [128, NDC * 128], BF16) for i in range(2)]
        bcb = sb("w_bcb", [128, DM], BF16)
        bwb = sb("w_bwb", [128, NDC], F32)
        evw = [sb(f"w_evw{i}", [128, TPC], F32) for i in range(2)]
        ps_w = [[pstn(f"w_psw{i}{t}", [128, 512]) for t in range(2)] for i in range(2)]
        xsrc = io["xn1i"] if mode == "L3" else io["xn1d"]
        for dc in range(NDC):
            b.dma("sync", xnb2[:, dc * TPC:(dc + 1) * TPC], xsrc[dc])
        b.dma("sync", bwb[:], io["bW_t"])
        b.dma("gpsimd", bcb[:], io["BcatT"][:, :])
        b.bar(engs=("tensor", "vector", "gpsimd"))
        gd_w = [0, 0]
        gd_w[0] = b.dma("gpsimd", wwb[0][:].rearrange("p (c r) -> p c r", r=128),
                        io["WT"].rearrange("c p (o r) -> p c r o", r=128)[:, :, :, 0])
        pe_oc = 0
        dv_oc = [0, 0]
        for oc in range(NDC):
            if oc + 1 < NDC:
                b.w("gpsimd", "pe", pe_oc)
                gd_w[(oc + 1) % 2] = b.dma(
                    "gpsimd", wwb[(oc + 1) % 2][:].rearrange("p (c r) -> p c r", r=128),
                    io["WT"].rearrange("c p (o r) -> p c r o", r=128)[:, :, :, oc + 1])
            b.w("tensor", "gd", gd_w[oc % 2])
            b.w("tensor", "dv", dv_oc[oc % 2])
            for t in range(2):
                for k in range(NDC):
                    b.op("tensor", lambda e, t_=t, k=k, oc=oc: e.matmul(
                        ps_w[oc % 2][t_][:], wwb[oc % 2][:, k * 128:(k + 1) * 128],
                        xnb2[:, k * TPC + t_ * 512:k * TPC + (t_ + 1) * 512],
                        start=(k == 0), stop=False))
                b.op("tensor", lambda e, t_=t, oc=oc: e.matmul(
                    ps_w[oc % 2][t_][:], bcb[:, oc * 128:(oc + 1) * 128],
                    t1s_g[:, t_ * 512:(t_ + 1) * 512], start=False, stop=True),
                    inc="pe")
            pe_oc = b.c["pe"]
            b.w("vector", "pe", pe_oc)
            evv = evw[oc % 2]
            for t in range(2):
                dvv = b.op("vector", lambda e, t_=t, oc=oc, ev=evv: e.tensor_scalar(
                    ev[:, t_ * 512:(t_ + 1) * 512], ps_w[oc % 2][t_][:], _col(bwb, oc), None, ALU.add),
                    inc="dv")
            dv_oc[oc % 2] = dvv
            b.w("gpsimd", "dv", dvv)
            b.dma("gpsimd", io["ACC"][oc], evv[:], accum_op=ALU.add)
        b.bar()
    fin()



DEBUG = False


def build_program(mode):
    nc = bass.Bass("TRN2", target_bir_lowering=False)
    ctx = ExitStack()
    ET = lambda nm, sh, dt: nc.dram_tensor(nm, sh, dt, kind="ExternalInput").ap()
    OT = lambda nm, sh, dt: nc.dram_tensor(nm, sh, dt, kind="ExternalOutput").ap()
    IT = lambda nm, sh, dt: nc.dram_tensor(nm, sh, dt).ap()

    io = {}
    names = []

    def EI(nm, sh, dt):
        io[nm] = ET(nm, sh, dt)
        names.append(nm)

    EI("x_t", [NDC, 128, TPC], F32)
    for nm, sh, dt in [
        ("A_log_t", [NDC, 128, NST], F32), ("D_t", [128, NDC], F32),
        ("dtb_t", [128, NDC], F32), ("xpwT", [NDC, 128, 160], F32),
        ("dtwT", [NDC, 128, 128], F32),
        ("E32", [128, NDC * 32], BF16), ("E16", [16, 2048], F32),
        ("ident", [128, 128], BF16), ("ones128f", [128, 1], F32),
        ("ones1x128f", [1, 128], F32), ("ones128b", [128, 1], BF16),
        ("o1x128b", [1, 128], BF16), ("onesKM", [128, 32], BF16),
        ("hmsk", [NSLOT, 128, NST * 32], BF16),
    ]:
        EI(nm, sh, dt)
    if mode in ("L1", "ALL"):
        EI("Esg", [32, 4096], BF16)
        EI("Esd", [8, 1024], BF16)
        EI("Edn", [128, NHC * 8], BF16)
        for nm, sh in [("qg", [NHC, 128, NDC * 128]), ("qu", [NHC, 128, NDC * 128]),
                       ("qd", [NDC, 128, NHC * 128])]:
            EI(nm, sh, I32)
        for nm, sh in [("zg", [32, HFF]), ("zu", [32, HFF]), ("zd", [NHC, DM])]:
            EI(nm, sh, I32)
        for nm, sh in [("sg", [32, HFF]), ("su", [32, HFF]), ("sd", [NHC, DM])]:
            EI(nm, sh, F32)
    if mode in ("L2", "ALL"):
        EI("wqkvT", [NDC, 128, 6144], F32)
        EI("bqkv_t", [128, NQKV], F32)
        EI("AcatT", [NDC, 128, 128], F32)
        EI("wvb", [128, TPC], F32)
    if mode in ("L3", "ALL"):
        EI("woT", [NCC, 128, 4096], F32)
        EI("bo_t", [128, NDC], F32)
        EI("WT", [NDC, 128, 4096], F32)
        EI("bW_t", [128, NDC], F32)
        EI("BcatT", [128, DM], F32)
        EI("rWT", [NDC, 128, 8], F32)
        EI("rb_t", [8, 1], F32)
        EI("masks", [NSLOT, 128, 512], F32)
        EI("Erv", [8, 128], F32)

    if mode == "LNDBG":
        io["d_mrow"] = OT("d_mrow", [1, TPC], F32)
        io["d_vrow"] = OT("d_vrow", [1, TPC], F32)
        io["d_mb"] = OT("d_mb", [128, TPC], F32)
        io["d_rstdb"] = OT("d_rstdb", [128, TPC], F32)
    elif mode == "L1":
        io["ACC"] = OT("ACC", [NDC, 128, TPC], F32)
        io["stloc"] = OT("stloc", [DM, 136], F32)
        if DEBUG:
            io["d_xnb"] = OT("d_xnb", [128, NDC * TPC], BF16)
            io["d_sg"] = OT("d_sg", [32, TPC], BF16)
            io["d_bf"] = OT("d_bf", [16, TPC], F32)
            io["d_cf"] = OT("d_cf", [16, TPC], F32)
            io["d_dtin"] = OT("d_dtin", [128, TPC], BF16)
            io["d_bb"] = OT("d_bb", [128, NST * TPC], BF16)
            io["d_wg0"] = OT("d_wg0", [128, NDC * 128], BF16)
            io["d_h0"] = OT("d_h0", [128, TPC], BF16)
            io["d_da00"] = OT("d_da00", [128, TPC], F32)
            io["d_u00"] = OT("d_u00", [128, TPC], BF16)
            io["d_hb00"] = OT("d_hb00", [128, TPC], BF16)
            io["d_dtx0"] = OT("d_dtx0", [128, TPC], BF16)
            io["d_e16"] = OT("d_e16", [16, 2048], F32)
            io["d_ps1"] = OT("d_ps1", [128, 512], F32)
            io["d_ssbg"] = OT("d_ssbg", [32, 1024], BF16)
            io["d_qb0"] = OT("d_qb0", [128, NDC * 128], BF16)
            io["d_sb0"] = OT("d_sb0", [128, 1024], F32)
            io["d_esg_e"] = OT("d_esg_e", [32, 4096], BF16)
            io["d_esg_l"] = OT("d_esg_l", [32, 4096], BF16)
            io["d_wd0"] = OT("d_wd0", [128, 1024], BF16)
            io["d_wd24"] = OT("d_wd24", [128, 1024], BF16)
            io["d_sdb"] = OT("d_sdb", [8, 2048], BF16)
            io["d_szd"] = OT("d_szd", [8, 2048], BF16)
            io["d_shblk"] = OT("d_shblk", [8, TPC], BF16)
            io["d_mrow"] = OT("d_mrow", [1, TPC], F32)
            io["d_vrow"] = OT("d_vrow", [1, TPC], F32)
            io["d_mb"] = OT("d_mb", [128, TPC], F32)
            io["d_rstdb"] = OT("d_rstdb", [128, TPC], F32)
    elif mode == "L2":
        EI("ACCi", [NDC, 128, TPC], F32)
        io["ACC"] = OT("ACC", [NDC, 128, TPC], F32)
        EI("stg", [4 * DM, 136], F32)
        io["xn1o"] = OT("xn1o", [NDC, 128, TPC], BF16)
        io["qto"] = OT("qto", [NCC, 128, TPC], BF16)
        io["kvloc"] = OT("kvloc", [2 * 2048, TPC], BF16)
        io["arpart"] = OT("arpart", [128, NDC], F32)
        io["t1o"] = OT("t1o", [128, TPC], F32)
        import os as _os
        if _os.environ.get("K_DBG2"):
            io["d_afin"] = OT("d_afin", [128, NDC * NST], F32)
            io["d_ones"] = OT("d_ones", [128, 128], F32)
            io["d_dtinf"] = OT("d_dtinf", [128, TPC], BF16)
            io["d_dtbf"] = OT("d_dtbf", [128, NDC], F32)
            io["d_stag20"] = OT("d_stag20", [128, 544], F32)
            io["d_sng0"] = OT("d_sng0", [128, 512], F32)
            io["d_tg0"] = OT("d_tg0", [128, 32], F32)
            io["d_hin0"] = OT("d_hin0", [128, 512], F32)
            io["d_hsel0"] = OT("d_hsel0", [128, 128], F32)
            io["d_cum0"] = OT("d_cum0", [128, TPC], F32)
            io["d_dtf0"] = OT("d_dtf0", [128, TPC], F32)
    elif mode == "L3":
        EI("ACCi", [NDC, 128, TPC], F32)
        io["ACC"] = IT("ACC", [NDC, 128, TPC], F32)
        EI("xn1i", [NDC, 128, TPC], BF16)
        EI("qti", [NCC, 128, TPC], BF16)
        EI("kvg", [4 * DM, TPC], BF16)
        EI("arall", [4 * 128, NDC], F32)
        EI("t1i", [128, TPC], F32)
        io["out_t"] = OT("out_t", [NDC, 128, TPC], F32)
        import os as _os
        if _os.environ.get("K_DBG2"):
            io["d_rvc"] = OT("d_rvc", [128, 1], F32)
            io["d_lg"] = OT("d_lg", [8, 1], F32)
            io["d_l8"] = OT("d_l8", [1, 8], F32)
            io["d_e8"] = OT("d_e8", [1, 8], F32)
            io["d_rcol"] = OT("d_rcol", [8, 1], F32)
            io["d_e8b"] = OT("d_e8b", [1, 8], F32)
            io["d_sm1"] = OT("d_sm1", [1, 1], F32)
            io["d_sm1b"] = OT("d_sm1b", [1, 1], F32)
            io["d_vals"] = OT("d_vals", [128, NCC * TPC], BF16)
            io["d_smf00"] = OT("d_smf00", [128, 512], F32)
            io["d_pb00"] = OT("d_pb00", [128, 512], BF16)
            io["d_o00"] = OT("d_o00", [128, 256], F32)
            io["d_mall"] = OT("d_mall", [128, 64], F32)
            io["d_lall"] = OT("d_lall", [128, 64], F32)
            io["d_ksl0"] = OT("d_ksl0", [128, 512], BF16)
            io["d_vsl0"] = OT("d_vsl0", [128, 256], BF16)
            io["d_q00"] = OT("d_q00", [128, 128], BF16)
    else:
        io["ACC"] = IT("ACC", [NDC, 128, TPC], F32)
        io["stloc"] = IT("stloc", [DM, 136], F32)
        io["stg"] = IT("stg", [4 * DM, 136], F32)
        io["kvloc"] = IT("kvloc", [2 * 2048, TPC], BF16)
        io["kvg"] = IT("kvg", [4 * DM, TPC], BF16)
        io["ar_in"] = IT("ar_in", [128, NDC], F32)
        io["ar_out"] = IT("ar_out", [128, NDC], F32)
        io["xn1d"] = IT("xn1d", [NDC, 128, TPC], BF16)
        io["qtd"] = IT("qtd", [NCC, 128, TPC], BF16)
        io["t1d"] = IT("t1d", [128, TPC], F32)
        io["out_t"] = OT("out_t", [NDC, 128, TPC], F32)
    io["bnc1"] = IT("bnc1", [1, 8], F32)
    io["bnc2"] = IT("bnc2", [8, 1], F32)

    with ctx:
        b = Bld(nc, ctx)
        P = {}
        for nm, sh, dt in [
            ("A", [128, NDC * NST], F32), ("Dg", [128, NDC], F32),
            ("dtb", [128, NDC], F32),
            ("dtw", [128, NDC * 128], BF16),
            ("o128", [128, 1], F32), ("o1x128", [1, 128], F32),
            ("o128b", [128, 1], BF16), ("o1x128b", [1, 128], BF16),
            ("oKM", [128, 32], BF16), ("ident", [128, 128], BF16),
            ("dtin", [128, TPC], BF16), ("Bf", [16, TPC], F32), ("Cf", [16, TPC], F32),
            ("Sg", [32, TPC], BF16), ("ones128t", [128, 128], F32),
        ]:
            P[nm] = b.sb("p_" + nm, sh, dt)

        b.dma("sync", P["o128"][:], io["ones128f"])
        b.dma("sync", P["o1x128"][:], io["ones1x128f"])
        b.dma("sync", P["o128b"][:], io["ones128b"])
        b.dma("sync", P["o1x128b"][:], io["o1x128b"])
        b.dma("sync", P["oKM"][:], io["onesKM"])
        b.dma("sync", P["ident"][:], io["ident"])
        b.dma("sync", P["dtb"][:], io["dtb_t"])
        b.dma("sync", P["Dg"][:], io["D_t"])
        b.dma("sync", P["A"][:].rearrange("p (c r) -> p c r", r=NST), io["A_log_t"].rearrange("c p r -> p c r"))
        b.dma("gpsimd", P["dtw"][:].rearrange("p (c r) -> p c r", r=128), io["dtwT"].rearrange("c p r -> p c r"))
        b.bar(engs=("scalar", "vector", "gpsimd"))
        b.act(lambda e: e.activation(P["A"][:], P["A"][:], AF.Exp))
        b.w("vector", "ac")
        b.op("vector", lambda e: e.tensor_scalar(P["A"][:], P["A"][:], -1.0, None, ALU.mult),
             inc="dv")
        b.op("gpsimd", lambda e: e.memset(P["ones128t"][:], 1.0), inc="pl")
        b.bar(engs=("vector", "scalar", "tensor", "gpsimd"))

        if mode in ("L1", "ALL"):
            b.dma("gpsimd", io["ACC"].rearrange("c p t -> (c p) t"),
                  io["x_t"].rearrange("c p t -> (c p) t"))
        elif mode == "L2":
            b.dma("gpsimd", io["ACC"].rearrange("c p t -> (c p) t"),
                  io["ACCi"].rearrange("c p t -> (c p) t"))
        if mode == "L3":
            b.dma("gpsimd", io["ACC"].rearrange("c p t -> (c p) t"),
                  io["ACCi"].rearrange("c p t -> (c p) t"))
        b.w("gpsimd", "gd", b.c["gd"])
        b.bar(engs=("sync", "vector", "tensor", "scalar"), sems=("gd",))

        if mode == "LNDBG":
            xnb0 = b.sb("xnb0", [128, NDC * TPC], BF16)
            scd = ExitStack()
            with scd:
                mb0 = scd.enter_context(nc.sbuf_tensor("mb0", [128, TPC], F32))
                rstdb0 = scd.enter_context(nc.sbuf_tensor("rstdb0", [128, TPC], F32))
                import os
                _sg = os.environ.get("LNV", "11")
                _ln_phase(b, P, io["x_t"], io, with_sg=_sg[0] == "1", with_xp=_sg[1] == "1",
                          xnb=xnb0 if _sg != "00" else None, mb=mb0, rstdb=rstdb0)
            b.w("gpsimd", "gd", b.c["gd"])
            b.w("gpsimd", "ld", b.c["ld"])
            b.finalize()
            return nc, names
        if mode in ("L1", "ALL"):
            xnb0 = b.sb("xnb0", [128, NDC * TPC], BF16)
            sc1 = ExitStack()
            with sc1:
                mb0 = sc1.enter_context(nc.sbuf_tensor("mb0", [128, TPC], F32))
                rstdb0 = sc1.enter_context(nc.sbuf_tensor("rstdb0", [128, TPC], F32))
                _ln_phase(b, P, io["x_t"], io, with_sg=True, with_xp=True, xnb=xnb0,
                          mb=mb0, rstdb=rstdb0)
                Bb = sc1.enter_context(nc.sbuf_tensor("m_Bb", [128, NST * TPC], BF16))
                Cb = sc1.enter_context(nc.sbuf_tensor("m_Cb", [128, NST * TPC], BF16))
                _build_bbcb(b, P, Bb, Cb, io["E16"],
                            dbg_io=io if (DEBUG and mode == "L1") else None)
                if DEBUG and mode == "L1":
                    b.dma("gpsimd", io["d_xnb"][:, :], xnb0[:])
                    b.dma("gpsimd", io["d_sg"][:, :], P["Sg"][:])
                    b.dma("gpsimd", io["d_bf"][:, :], P["Bf"][:])
                    b.dma("gpsimd", io["d_cf"][:, :], P["Cf"][:])
                    b.dma("gpsimd", io["d_dtin"][:, :], P["dtin"][:])
                    b.dma("gpsimd", io["d_bb"][:, :], Bb[:])
                _mamba_pass1(b, io, P, Bb, Cb, mb0, rstdb0)
            _ffn0(b, io, P, xnb0)
            b.bar()

        if mode == "L1":
            b.w("gpsimd", "gd", b.c["gd"])
            b.w("gpsimd", "ld", b.c["ld"])
            b.finalize()
            return nc, names

        if mode == "ALL":
            b.w("gpsimd", "gd", b.c["gd"])
            b.op("gpsimd", lambda e: e.collective_compute(
                "AllGather", ALU.bypass, replica_groups=[[0, 1, 2, 3], [4, 5, 6, 7]],
                ins=[io["stloc"].opt()], outs=[io["stg"].opt()]), inc="cc", drain=False)
            b.w("gpsimd", "cc")
            b.bar(engs=("sync", "vector", "tensor", "scalar"), sems=("cc",))

        import os
        cut = os.environ.get("K_L2CUT", "") if mode == "L2" else ""
        if mode == "L2":
            sc0 = ExitStack()
            with sc0:
                mbx = sc0.enter_context(nc.sbuf_tensor("mbx", [128, TPC], F32))
                rstdbx = sc0.enter_context(nc.sbuf_tensor("rstdbx", [128, TPC], F32))
                _ln_phase(b, P, io["x_t"], io, with_sg=False, with_xp=True, xnb=None,
                          mb=mbx, rstdb=rstdbx)
        if mode in ("L2", "ALL") and cut != "ln0":
            sc2 = ExitStack()
            with sc2:
                Bb = sc2.enter_context(nc.sbuf_tensor("c_Bb", [128, NST * TPC], BF16))
                Cb = sc2.enter_context(nc.sbuf_tensor("c_Cb", [128, NST * TPC], BF16))
                _build_bbcb(b, P, Bb, Cb, io["E16"])
                _combine_fix(b, io, P, Bb, Cb)

        if cut not in ("ln0", "fix"):
            _l1_phases(b, io, P, mode)
        if mode == "L2":
            if "d_afin" in io:
                b.dma("gpsimd", io["d_afin"][:, :], P["A"][:])
                b.dma("gpsimd", io["d_ones"][:, :], P["ones128t"][:])
                b.dma("gpsimd", io["d_dtinf"][:, :], P["dtin"][:])
                b.dma("gpsimd", io["d_dtbf"][:, :], P["dtb"][:])
            b.w("gpsimd", "gd", b.c["gd"])
            b.w("gpsimd", "ld", b.c["ld"])
        b.finalize()
        return nc, names



_CACHE = {}
FUSED = False


def _get(mode):
    if mode not in _CACHE:
        _CACHE[mode] = build_program(mode)
    return _CACHE[mode]


def _run(mode, shared, cores, extra_core=None):
    nc, names = _get(mode)
    in_maps = []
    for c in range(8):
        m = {}
        for nm in names:
            if nm in shared:
                m[nm] = shared[nm]
            elif nm in cores[c]:
                m[nm] = cores[c][nm]
            elif extra_core is not None and nm in extra_core[c]:
                m[nm] = extra_core[c][nm]
            else:
                raise KeyError(f"missing input {nm} for mode {mode}")
        in_maps.append(m)
    res = run_bass_kernel_spmd(nc, in_maps, core_ids=list(range(8)))
    return res.results


def kernel(**inputs):
    shared = _prep_shared(inputs)
    cores = [_prep_core(inputs, c) for c in range(8)]
    if FUSED:
        res = _run("ALL", shared, cores)
        outs = [np.asarray(r["out_t"], dtype=np.float32) for r in res]
    else:
        r1 = _run("L1", shared, cores)
        ex2 = []
        for c in range(8):
            bb = c // 4
            stg = np.concatenate([np.asarray(r1[bb * 4 + qq]["stloc"]) for qq in range(4)], 0)
            ex2.append({"ACCi": np.asarray(r1[c]["ACC"]), "stg": stg})
        r2 = _run("L2", shared, cores, ex2)
        ex3 = []
        for c in range(8):
            bb = c // 4
            kvg = np.concatenate([np.asarray(r2[bb * 4 + qq]["kvloc"]) for qq in range(4)], 0)
            arall = np.concatenate([np.asarray(r2[bb * 4 + qq]["arpart"]) for qq in range(4)], 0)
            ex3.append({
                "ACCi": np.asarray(r2[c]["ACC"]), "xn1i": np.asarray(r2[c]["xn1o"]),
                "qti": np.asarray(r2[c]["qto"]), "kvg": kvg, "arall": arall,
                "t1i": np.asarray(r2[c]["t1o"]),
            })
        r3 = _run("L3", shared, cores, ex3)
        outs = [np.asarray(r["out_t"], dtype=np.float32) for r in r3]
    out = np.zeros((NBATCH, SEQ, DM), np.float32)
    for c in range(8):
        bb, q = c // 4, c % 4
        o = outs[c].reshape(DM, TPC).T
        for s in range(NSLOT):
            g = GQ[q][s]
            out[bb, g * TILE:(g + 1) * TILE, :] = o[s * TILE:(s + 1) * TILE, :]
    return out


# version 03
# speedup vs baseline: 1.0141x; 1.0141x over previous
import numpy as np
import ml_dtypes
from contextlib import ExitStack

import concourse.bass as bass
import concourse.mybir as mybir
from concourse.bass_utils import run_bass_kernel_spmd

F32 = mybir.dt.float32
BF16 = mybir.dt.bfloat16
I32 = mybir.dt.int32
AF = mybir.ActivationFunctionType
ALU = mybir.AluOpType
BF = ml_dtypes.bfloat16
AX = mybir.AxisListType.X

NBATCH = 2
SEQ = 4096
DM = 4096
NDC = 32
TPC = 1024
NSLOT = 8
TILE = 128
NST = 16
DTR = 128
HFF = 11008
NHC = 86
EPS = 1e-5
HEADS = 8
NCC = 16
NQKV = 48
DN_BLKS = [(i * 8, min(8, NHC - i * 8)) for i in range(11)]

GQ = [[q, 7 - q, 8 + q, 15 - q, 16 + q, 23 - q, 24 + q, 31 - q] for q in range(4)]
G2QS = {}
for _q in range(4):
    for _s in range(NSLOT):
        G2QS[GQ[_q][_s]] = (_q, _s)

R_INV = float(np.float32(1.0) / np.float32(3484.0))
Q_START, Q_END = 611, 4095



def _prep_shared(inp):
    d = {}
    f32 = lambda a: np.ascontiguousarray(np.asarray(a), dtype=np.float32)
    d["A_log_t"] = f32(inp["L0_attn_A_log"]).reshape(NDC, 128, NST)
    d["D_t"] = np.ascontiguousarray(f32(inp["L0_attn_D"]).reshape(NDC, 128).T)
    d["dtb_t"] = np.ascontiguousarray(f32(inp["L0_attn_dt_proj_b"]).reshape(NDC, 128).T)
    d["xpwT"] = np.ascontiguousarray(f32(inp["L0_attn_x_proj_w"]).T).reshape(NDC, 128, 160)
    dtw = f32(inp["L0_attn_dt_proj_w"])
    d["dtwT"] = np.ascontiguousarray(dtw.T.reshape(DTR, NDC, 128).transpose(1, 0, 2))
    for nm, key in (("qg", "L0_ffn_qw_gate"), ("qu", "L0_ffn_qw_up")):
        qw = np.ascontiguousarray(np.asarray(inp[key]), dtype=np.int32)
        d[nm] = np.ascontiguousarray(
            qw.reshape(NDC, 128, NHC, 128).transpose(2, 1, 0, 3).reshape(NHC, 128, NDC * 128))
    qd = np.ascontiguousarray(np.asarray(inp["L0_ffn_qw_down"]), dtype=np.int32)
    d["qd"] = np.ascontiguousarray(
        qd.reshape(NHC, 128, NDC, 128).transpose(2, 1, 0, 3).reshape(NDC, 128, NHC * 128))
    for nm, key in (("zg", "L0_ffn_z_gate"), ("zu", "L0_ffn_z_up"), ("zd", "L0_ffn_z_down")):
        d[nm] = np.ascontiguousarray(np.asarray(inp[key]), dtype=np.int32)
    for nm, key in (("sg", "L0_ffn_s_gate"), ("su", "L0_ffn_s_up"), ("sd", "L0_ffn_s_down")):
        d[nm] = f32(inp[key])
    d["wqkvT"] = np.ascontiguousarray(f32(inp["L1_attn_w_qkv"]).T).reshape(NDC, 128, 6144)
    d["bqkv_t"] = np.ascontiguousarray(f32(inp["L1_attn_b_qkv"]).reshape(NQKV, 128).T)
    d["woT"] = np.ascontiguousarray(f32(inp["L1_attn_w_o"]).T).reshape(NCC, 128, 4096)
    d["bo_t"] = np.ascontiguousarray(f32(inp["L1_attn_b_o"]).reshape(NDC, 128).T)
    d["WT"] = np.ascontiguousarray(f32(inp["L1_ffn_W"]).T).reshape(NDC, 128, 4096)
    d["bW_t"] = np.ascontiguousarray(f32(inp["L1_ffn_b"]).reshape(NDC, 128).T)
    d["AcatT"] = np.ascontiguousarray(f32(inp["L1_ffn_A"]).reshape(128, DM).T).reshape(NDC, 128, 128)
    d["BcatT"] = np.ascontiguousarray(f32(inp["L1_ffn_B"]).transpose(0, 2, 1).reshape(128, DM))
    d["rWT"] = np.ascontiguousarray(f32(inp["L1_ffn_router_W"]).T).reshape(NDC, 128, 8)
    d["rb_t"] = f32(inp["L1_ffn_router_b"]).reshape(8, 1)
    e32 = np.zeros((128, NDC, 32), np.float32)
    for k in range(NDC):
        e32[:, k, k] = 1.0
    d["E32"] = e32.reshape(128, NDC * 32).astype(BF)
    e16 = np.zeros((16, 16, 128), np.float32)
    for n in range(16):
        e16[n, n, :] = 1.0
    d["E16"] = np.ascontiguousarray(e16.reshape(16, 16 * 128))
    esg = np.zeros((32, 32, 128), np.float32)
    for g in range(32):
        esg[g, g, :] = 1.0
    d["Esg"] = esg.reshape(32, 32 * 128).astype(BF)
    esd = np.zeros((8, 8, 128), np.float32)
    for g in range(8):
        esd[g, g, :] = 1.0
    d["Esd"] = esd.reshape(8, 8 * 128).astype(BF)
    edn = np.zeros((128, NHC, 8), np.float32)
    for k0, bs in DN_BLKS:
        for j in range(bs):
            edn[:, k0 + j, j] = 1.0
    d["Edn"] = edn.reshape(128, NHC * 8).astype(BF)
    erv = np.zeros((8, 128), np.float32)
    for p in range(128):
        erv[p // 16, p] = 1.0
    d["Erv"] = erv
    d["ident"] = np.eye(128).astype(BF)
    d["ones128f"] = np.ones((128, 1), np.float32)
    d["ones1x128f"] = np.ones((1, 128), np.float32)
    d["ones128b"] = np.ones((128, 1), BF)
    d["o1x128b"] = np.ones((1, 128), BF)
    d["onesKM"] = np.ones((128, 32), BF)
    return d


def _prep_core(inp, c):
    b_, q = c // 4, c % 4
    d = {}
    x = np.asarray(inp["L0_attn_x"], dtype=np.float32)
    rows = np.concatenate([x[b_, g * TILE:(g + 1) * TILE, :] for g in GQ[q]], 0)
    d["x_t"] = np.ascontiguousarray(rows.T).reshape(NDC, 128, TPC)
    m = np.zeros((NSLOT, 128, 512), np.float32)
    for s in range(NSLOT):
        g = GQ[q][s]
        keys = s * 512 + np.arange(512)[None, :]
        qpos = g * TILE + np.arange(128)[:, None]
        m[s] = np.where(keys <= qpos, 0.0, -10000.0)
    d["masks"] = m
    wv = np.zeros((TPC,), np.float32)
    for s in range(NSLOT):
        g = GQ[q][s]
        pos = g * TILE + np.arange(TILE)
        wv[s * TILE:(s + 1) * TILE] = np.where((pos >= Q_START) & (pos < Q_END), 1.0, 0.0)
    d["wvb"] = np.ascontiguousarray(np.broadcast_to(wv[None, :], (128, TPC)), dtype=np.float32)
    msk = np.zeros((NSLOT, 128, NST * 32), np.float32)
    for s in range(NSLOT):
        g = GQ[q][s]
        for n in range(NST):
            msk[s, :, n * 32 + g] = 1.0
    d["hmsk"] = msk.astype(BF)
    return d



ENGS = ("sync", "tensor", "vector", "scalar", "gpsimd")
SEMS = ("ld", "gd", "pe", "dv", "ac", "pl", "cc", "so")


class Bld:
    def __init__(self, nc, ctx):
        self.nc = nc
        self.ctx = ctx
        self.ops = {e: [] for e in ENGS}
        self.drain = None
        self.sh = {}
        self.c = {}
        for s in SEMS:
            self.sh[s] = ctx.enter_context(nc.semaphore(s))
            self.c[s] = 0

    def sb(self, name, shape, dt):
        return self.ctx.enter_context(self.nc.sbuf_tensor(name, shape, dt))

    def ps(self, name, shape, dt=F32):
        return self.ctx.enter_context(self.nc.psum_tensor(name, shape, dt))

    def act(self, f, inc="ac", n=1):
        return self.op("scalar", f, inc=inc, n=n)

    def op(self, eng, f, inc=None, n=1, drain=True):
        if inc is None:
            self.ops[eng].append(f)
            return None
        h = self.sh[inc]
        self.c[inc] += n
        if drain and eng in ("vector", "scalar", "gpsimd", "tensor"):
            self.ops[eng].append(f)
            self.ops[eng].append(lambda e, h=h, n=n: e.drain().then_inc(h, n))
        else:
            def g(e, f=f, h=h, n=n):
                f(e).then_inc(h, n)
            self.ops[eng].append(g)
        return self.c[inc]

    def w(self, eng, sem, val=None):
        v = self.c[sem] if val is None else val
        if v <= 0:
            return
        h = self.sh[sem]
        self.ops[eng].append(lambda e, h=h, v=v: e.wait_ge(h, v))

    def bar(self, engs=ENGS, sems=SEMS):
        snap = {s: self.c[s] for s in sems}
        for eng in engs:
            for s, v in snap.items():
                if v > 0:
                    self.w(eng, s, v)

    def dma(self, eng, out, in_, sem=None, **kw):
        if sem is None:
            sem = "gd" if eng == "gpsimd" else "ld"
        return self.op(eng, lambda e, o=out, i=in_, kw=kw: e.dma_start(out=o, in_=i, **kw),
                       inc=sem, n=16, drain=False)

    def finalize(self):
        block = self.ctx.enter_context(self.nc.Block())
        for eng in ENGS:
            lst = self.ops[eng]

            def runner(e, lst=lst):
                for f in lst:
                    f(e)
            getattr(block, eng)(runner)


def _col(t, j):
    return t[:, j:j + 1]



_LN_SEQ = [0]


def _ln_phase(b, P, src, io, with_sg, with_xp, xnb, mb, rstdb):
    _LN_SEQ[0] += 1
    tag = f"{_LN_SEQ[0]}_"
    ph = ExitStack()
    with ph:
        sb = lambda nm, sh, dt: ph.enter_context(b.nc.sbuf_tensor(tag + nm, sh, dt))
        pstn = lambda nm, sh: ph.enter_context(b.nc.psum_tensor(tag + nm, sh, F32))
        if with_xp:
            xpw = sb("ln_xpw", [128, NDC * 160], BF16)
            b.dma("gpsimd", xpw[:].rearrange("p (c r) -> p c r", r=160),
                  io["xpwT"].rearrange("c p r -> p c r"))
        E32s = sb("ln_E32", [128, NDC * 32], BF16)
        b.dma("sync", E32s[:], io["E32"])
        xch = sb("ln_xch", [128, TPC], F32)
        sqb = sb("ln_sqb", [128, TPC], F32)
        hhb = sb("ln_hhb", [128, TPC], BF16)
        hlb = sb("ln_hlb", [128, TPC], BF16)
        shb = sb("ln_shb", [128, TPC], BF16)
        slb = sb("ln_slb", [128, TPC], BF16)
        rsc = sb("ln_rsc", [128, TPC], F32)
        mrow = sb("ln_mrow", [1, TPC], F32)
        vrow = sb("ln_vrow", [1, TPC], F32)
        srow = sb("ln_srow", [1, TPC], F32)
        xnw = sb("ln_xnw", [128, TPC], F32)
        xtr = sb("ln_xtr", [128, TPC], BF16)
        GS = sb("ln_GS", [32, TPC], F32)
        GQ = sb("ln_GQ", [32, TPC], F32)
        phA = ExitStack()
        ps_sum = [phA.enter_context(b.nc.psum_tensor(f"{tag}ln_ps_sum{t}", [32, 512], F32)) for t in range(2)]
        ps_ssq = [phA.enter_context(b.nc.psum_tensor(f"{tag}ln_ps_ssq{t}", [32, 512], F32)) for t in range(2)]
        ps_b1 = phA.enter_context(b.nc.psum_tensor(tag + "ln_ps_b1", [128, 512], F32))
        ps_b2 = phA.enter_context(b.nc.psum_tensor(tag + "ln_ps_b2", [128, 512], F32))

        for dc in range(NDC):
            if dc > 0:
                b.w("sync", "pe", b.c["pe"])
            ldv = b.dma("sync", xch[:], src[dc])
            b.w("scalar", "ld", ldv)
            b.w("scalar", "pe", b.c["pe"])
            acv = b.act(lambda e: e.activation(sqb[:], xch[:], AF.Square))
            b.w("vector", "ld", ldv)
            b.w("vector", "pe", b.c["pe"])
            b.op("vector", lambda e: e.tensor_copy(hhb[:], xch[:]))
            b.op("vector", lambda e: e.tensor_tensor(rsc[:], xch[:], hhb[:], ALU.subtract))
            dvv = b.op("vector", lambda e: e.tensor_copy(hlb[:], rsc[:]), inc="dv")
            b.w("vector", "ac", acv)
            b.op("vector", lambda e: e.tensor_copy(shb[:], sqb[:]))
            b.op("vector", lambda e: e.tensor_tensor(rsc[:], sqb[:], shb[:], ALU.subtract))
            dv2 = b.op("vector", lambda e: e.tensor_copy(slb[:], rsc[:]), inc="dv")
            b.w("tensor", "dv", dv2)
            for t in range(2):
                b.op("tensor", lambda e, t_=t, dc=dc: e.matmul(
                    ps_sum[t_][:], E32s[:, dc * 32:(dc + 1) * 32], hhb[:, t_ * 512:(t_ + 1) * 512],
                    start=(dc == 0), stop=False))
                b.op("tensor", lambda e, t_=t, dc=dc: e.matmul(
                    ps_sum[t_][:], E32s[:, dc * 32:(dc + 1) * 32], hlb[:, t_ * 512:(t_ + 1) * 512],
                    start=False, stop=(dc == NDC - 1)))
            for t in range(2):
                b.op("tensor", lambda e, t_=t, dc=dc: e.matmul(
                    ps_ssq[t_][:], E32s[:, dc * 32:(dc + 1) * 32], shb[:, t_ * 512:(t_ + 1) * 512],
                    start=(dc == 0), stop=False))
                b.op("tensor", lambda e, t_=t, dc=dc: e.matmul(
                    ps_ssq[t_][:], E32s[:, dc * 32:(dc + 1) * 32], slb[:, t_ * 512:(t_ + 1) * 512],
                    start=False, stop=(dc == NDC - 1)),
                    inc="pe" if t == 1 else None)
        b.op("tensor", lambda e: e.drain())
        b.w("vector", "pe")
        for t in range(2):
            b.op("vector", lambda e, t_=t: e.tensor_copy(GS[:, t_ * 512:(t_ + 1) * 512], ps_sum[t_][:]))
            b.op("vector", lambda e, t_=t: e.tensor_copy(GQ[:, t_ * 512:(t_ + 1) * 512], ps_ssq[t_][:]),
                 inc="dv" if t == 1 else None)
        b.w("gpsimd", "dv", b.c["dv"])
        b.op("gpsimd", lambda e: e.reduce_sum(mrow[:], GS[:], axis=mybir.AxisListType.C))
        b.op("gpsimd", lambda e: e.reduce_sum(vrow[:], GQ[:], axis=mybir.AxisListType.C), inc="pl")
        b.w("vector", "pl", b.c["pl"])
        dvv = b.op("vector", lambda e: e.tensor_scalar(mrow[:], mrow[:], 1.0 / DM, None, ALU.mult), inc="dv")
        b.op("vector", lambda e: e.tensor_scalar(vrow[:], vrow[:], 1.0 / DM, None, ALU.mult))
        import os
        if os.environ.get("LN_EARLYTAP") and "d_mrow" in io:
            b.w("gpsimd", "dv", dvv)
            b.dma("gpsimd", io["d_mrow"][:, :], mrow[:])
            b.w("vector", "gd", b.c["gd"])
        b.op("vector", lambda e: e.tensor_tensor(srow[:], mrow[:], mrow[:], ALU.mult))
        b.op("vector", lambda e: e.tensor_tensor(vrow[:], vrow[:], srow[:], ALU.subtract), inc="dv")
        b.op("vector", lambda e: e.tensor_scalar(vrow[:], vrow[:], float(EPS), None, ALU.add),
             inc="dv")
        b.w("scalar", "dv")
        acv0 = b.act(lambda e: e.activation(vrow[:], vrow[:], AF.Ln))
        b.w("vector", "ac", acv0)
        b.op("vector", lambda e: e.tensor_scalar(vrow[:], vrow[:], -0.5, None, ALU.mult), inc="dv")
        b.w("scalar", "dv")
        b.act(lambda e: e.activation(vrow[:], vrow[:], AF.Exp))
        mhi = srow
        b.w("vector", "ac")
        b.op("vector", lambda e: e.tensor_copy(hhb[0:1, 0:TPC], mrow[:]))
        b.op("vector", lambda e: e.tensor_tensor(mhi[:], mrow[:], hhb[0:1, 0:TPC], ALU.subtract))
        b.op("vector", lambda e: e.tensor_copy(hlb[0:1, 0:TPC], mhi[:]))
        b.op("vector", lambda e: e.tensor_copy(shb[0:1, 0:TPC], vrow[:]))
        b.op("vector", lambda e: e.tensor_tensor(mhi[:], vrow[:], shb[0:1, 0:TPC], ALU.subtract))
        b.op("vector", lambda e: e.tensor_copy(slb[0:1, 0:TPC], mhi[:]), inc="dv")
        b.w("tensor", "dv")
        for t in range(2):
            b.op("tensor", lambda e, t_=t: e.matmul(
                ps_b1[:], P["o1x128b"][:], hhb[0:1, t_ * 512:(t_ + 1) * 512], start=True, stop=False))
            b.op("tensor", lambda e, t_=t: e.matmul(
                ps_b1[:], P["o1x128b"][:], hlb[0:1, t_ * 512:(t_ + 1) * 512], start=False, stop=True))
            b.op("tensor", lambda e, t_=t: e.matmul(
                ps_b2[:], P["o1x128b"][:], shb[0:1, t_ * 512:(t_ + 1) * 512], start=True, stop=False))
            b.op("tensor", lambda e, t_=t: e.matmul(
                ps_b2[:], P["o1x128b"][:], slb[0:1, t_ * 512:(t_ + 1) * 512], start=False, stop=True),
                inc="pe")
            b.w("vector", "pe")
            b.op("vector", lambda e, t_=t: e.tensor_copy(mb[:, t_ * 512:(t_ + 1) * 512], ps_b1[:]))
            dvv = b.op("vector", lambda e, t_=t: e.tensor_copy(rstdb[:, t_ * 512:(t_ + 1) * 512], ps_b2[:]), inc="dv")
            b.w("tensor", "dv", dvv)
        b.op("tensor", lambda e: e.drain())
        if "d_mrow" in io:
            b.w("gpsimd", "dv", b.c["dv"])
            b.dma("gpsimd", io["d_mrow"][:, :], mrow[:])
            b.dma("gpsimd", io["d_vrow"][:, :], vrow[:])
            b.dma("gpsimd", io["d_mb"][:, :], mb[:])
            b.dma("gpsimd", io["d_rstdb"][:, :], rstdb[:])
            b.w("vector", "gd", b.c["gd"])
        if True:
            pass

        import os
        if os.environ.get("LN_PASSA_ONLY"):
            if "d_mrow" in io:
                b.w("gpsimd", "dv", b.c["dv"])
                b.dma("gpsimd", io["d_mrow"][:, :], mrow[:])
                b.dma("gpsimd", io["d_vrow"][:, :], vrow[:])
            b.bar()
            return
        phA.close()
        ps_sg = [pstn(f"ln_ps_sg{t}", [32, 512]) for t in range(2)] if with_sg else None
        if with_xp:
            ps_xp1 = [pstn(f"ln_ps_xp1{t}", [128, 512]) for t in range(2)]
            ps_B = [pstn(f"ln_ps_B{t}", [16, 512]) for t in range(2)]
            ps_C = [pstn(f"ln_ps_C{t}", [16, 512]) for t in range(2)]

        pe_ck = []
        dv_ck = b.c["dv"]
        for dc in range(NDC):
            b.w("sync", "dv", dv_ck)
            ldv = b.dma("sync", xch[:], src[dc])
            b.w("vector", "ld", ldv)
            if dc > 0 and pe_ck:
                b.w("vector", "pe", pe_ck[-1])
            b.op("vector", lambda e: e.tensor_tensor(xnw[:], xch[:], mb[:], ALU.subtract))
            xnd = xnb[:, dc * TPC:(dc + 1) * TPC] if xnb is not None else xtr[:]
            dvv = b.op("vector", lambda e, o=xnd: e.tensor_tensor(o, xnw[:], rstdb[:], ALU.mult),
                       inc="dv")
            dv_ck = dvv
            b.w("tensor", "dv", dvv)
            last = None
            for t in range(2):
                sl = (slice(None), slice(t * 512, (t + 1) * 512))
                if with_sg:
                    last = b.op("tensor", lambda e, t_=t, dc=dc, x=xnd: e.matmul(
                        ps_sg[t_][:], E32s[:, dc * 32:(dc + 1) * 32], x[:, t_ * 512:(t_ + 1) * 512],
                        start=(dc == 0), stop=(dc == NDC - 1)), inc="pe")
                if with_xp:
                    b.op("tensor", lambda e, t_=t, dc=dc, x=xnd: e.matmul(
                        ps_xp1[t_][:], xpw[:, dc * 160:dc * 160 + 128],
                        x[:, t_ * 512:(t_ + 1) * 512], start=(dc == 0), stop=(dc == NDC - 1)))
                    b.op("tensor", lambda e, t_=t, dc=dc, x=xnd: e.matmul(
                        ps_B[t_][:], xpw[:, dc * 160 + 128:dc * 160 + 144],
                        x[:, t_ * 512:(t_ + 1) * 512], start=(dc == 0), stop=(dc == NDC - 1)))
                    last = b.op("tensor", lambda e, t_=t, dc=dc, x=xnd: e.matmul(
                        ps_C[t_][:], xpw[:, dc * 160 + 144:dc * 160 + 160],
                        x[:, t_ * 512:(t_ + 1) * 512], start=(dc == 0), stop=(dc == NDC - 1)),
                        inc="pe")
            pe_ck.append(b.c["pe"])
        b.w("vector", "pe")
        for t in range(2):
            sl = slice(t * 512, (t + 1) * 512)
            if with_xp:
                b.op("vector", lambda e, t_=t, sl=sl: e.tensor_copy(P["dtin"][:, sl], ps_xp1[t_][:]))
                b.op("vector", lambda e, t_=t, sl=sl: e.tensor_copy(P["Bf"][:, sl], ps_B[t_][:]))
                b.op("vector", lambda e, t_=t, sl=sl: e.tensor_copy(P["Cf"][:, sl], ps_C[t_][:]))
            if with_sg:
                b.op("vector", lambda e, t_=t, sl=sl: e.tensor_copy(P["Sg"][:, sl], ps_sg[t_][:]))
        b.op("vector", lambda e: e.tensor_copy(srow[:], mrow[:]), inc="dv")
        b.bar()


def _build_bbcb(b, P, Bb, Cb, io_E16, dbg_io=None):
    _LN_SEQ[0] += 1
    tag = f"{_LN_SEQ[0]}_"
    ph = ExitStack()
    with ph:
        E16s = ph.enter_context(b.nc.sbuf_tensor(tag + "bb_E16", [16, 2048], F32))
        dbg_ps = ph.enter_context(b.nc.sbuf_tensor(tag + "bb_dbgps", [128, 512], F32)) if dbg_io is not None else None
        b.dma("sync", E16s[:], io_E16)
        b.bar(engs=("tensor",), sems=("ld",))
        ps1 = ph.enter_context(b.nc.psum_tensor(tag + "bb_ps1", [128, 512], F32))
        ps2 = ph.enter_context(b.nc.psum_tensor(tag + "bb_ps2", [128, 512], F32))
        for n in range(NST):
            for t in range(2):
                b.op("tensor", lambda e, n=n, t_=t: e.matmul(
                    ps1[:], E16s[:, n * 128:(n + 1) * 128],
                    P["Bf"][:, t_ * 512:(t_ + 1) * 512], start=True, stop=True))
                b.op("tensor", lambda e, n=n, t_=t: e.matmul(
                    ps2[:], E16s[:, n * 128:(n + 1) * 128],
                    P["Cf"][:, t_ * 512:(t_ + 1) * 512], start=True, stop=True), inc="pe")
                b.w("vector", "pe")
                o1 = Bb[:, n * TPC + t * 512:n * TPC + (t + 1) * 512]
                o2 = Cb[:, n * TPC + t * 512:n * TPC + (t + 1) * 512]
                b.op("vector", lambda e, o=o1: e.tensor_copy(o, ps1[:]))
                dvv = b.op("vector", lambda e, o=o2: e.tensor_copy(o, ps2[:]), inc="dv")
                b.w("tensor", "dv", dvv)
        b.bar()


def _dt_chunk(b, P, dc, ps_dt, dtf, tA, tB):
    b.w("tensor", "dv", b.c["dv"])
    pev = None
    for t in range(2):
        pev = b.op("tensor", lambda e, t_=t, dc=dc: e.matmul(
            ps_dt[t_][:], P["dtw"][:, dc * 128:(dc + 1) * 128],
            P["dtin"][:, t_ * 512:(t_ + 1) * 512], start=True, stop=True), inc="pe")
    b.w("vector", "pe", pev)
    evv = None
    for t in range(2):
        evv = b.op("vector", lambda e, t_=t, dc=dc: e.tensor_scalar(
            dtf[:, t_ * 512:(t_ + 1) * 512], ps_dt[t_][:], _col(P["dtb"], dc), None, ALU.add),
            inc="dv")
    b.op("vector", lambda e: e.tensor_scalar(tA[:], dtf[:], -1.0, None, ALU.mult))
    dvv = b.op("vector", lambda e: e.tensor_tensor(tA[:], dtf[:], tA[:], ALU.min), inc="dv")
    b.w("scalar", "dv", dvv)
    b.op("scalar", lambda e: e.activation(tA[:], tA[:], AF.Exp))
    acv = b.act(lambda e: e.activation(tA[:], tA[:], AF.Ln, bias=1.0))
    b.op("vector", lambda e: e.tensor_scalar(tB[:], dtf[:], 0.0, None, ALU.max))
    b.w("vector", "ac", acv)
    b.op("vector", lambda e: e.tensor_tensor(dtf[:], tB[:], tA[:], ALU.add))
    return evv


def _mamba_pass1(b, io, P, Bb, Cb, mb, rstdb):
    ph = ExitStack()
    with ph:
        sb = lambda nm, sh, dt: ph.enter_context(b.nc.sbuf_tensor(nm, sh, dt))
        xch = sb("m1_xch", [128, TPC], F32)
        dtf = sb("m1_dtf", [128, TPC], F32)
        tA = sb("m1_tA", [128, TPC], F32)
        dtxb = sb("m1_dtxb", [128, TPC], BF16)
        dAf = sb("m1_dAf", [128, TPC], F32)
        ubf = sb("m1_ubf", [128, TPC], BF16)
        hbf = sb("m1_hbf", [128, TPC], BF16)
        ybf = sb("m1_ybf", [128, TPC], F32)
        ych = [sb(f"m1_ych{i}", [128, TPC], F32) for i in range(2)]
        ssb = [sb(f"m1_ssb{i}", [128, 136], F32) for i in range(2)]
        ps_dt = [ph.enter_context(b.nc.psum_tensor(f"m1_psdt{t}", [128, 512], F32)) for t in range(2)]
        gd_ck = [0, 0]

        for dc in range(NDC):
            yc = ych[dc % 2]
            sc = ssb[dc % 2]
            if dc > 0:
                b.w("sync", "dv", dv_xch)
            ldv = b.dma("sync", xch[:], io["x_t"][dc])
            b.w("vector", "ld", ldv)
            b.w("vector", "gd", gd_ck[dc % 2])
            b.op("vector", lambda e: e.tensor_tensor(xch[:], xch[:], mb[:], ALU.subtract))
            b.op("vector", lambda e: e.tensor_tensor(xch[:], xch[:], rstdb[:], ALU.mult),
                 inc="dv")
            b.op("vector", lambda e, yc=yc, dc=dc: e.tensor_scalar(
                yc[:], xch[:], _col(P["Dg"], dc), None, ALU.mult))
            _dt_chunk(b, P, dc, ps_dt, dtf, tA, dAf)
            b.op("vector", lambda e, sc=sc: e.tensor_reduce(
                sc[:].rearrange("p (s c) -> p s c", c=17)[:, :, 16],
                dtf[:].rearrange("p (s t) -> p s t", t=TILE),
                axis=AX, op=ALU.add))
            dv_xch = b.op("vector", lambda e: e.tensor_tensor(dtxb[:], dtf[:], xch[:], ALU.mult),
                          inc="dv")
            for n in range(NST):
                dvv = b.op("vector", lambda e, dc=dc, n=n: e.tensor_scalar(
                    dAf[:], dtf[:], _col(P["A"], dc * NST + n), None, ALU.mult), inc="dv")
                b.w("scalar", "dv", dvv)
                acv = b.act(lambda e: e.activation(dAf[:], dAf[:], AF.Exp))
                b.op("vector", lambda e, n=n: e.tensor_tensor(
                    ubf[:], dtxb[:], Bb[:, n * TPC:(n + 1) * TPC], ALU.mult))
                b.w("vector", "ac", acv)
                for s in range(NSLOT):
                    dvv = b.op("vector", lambda e, s=s: e.tensor_tensor_scan(
                        hbf[:, s * TILE:(s + 1) * TILE], dAf[:, s * TILE:(s + 1) * TILE],
                        ubf[:, s * TILE:(s + 1) * TILE], 0.0, ALU.mult, ALU.add),
                        inc="dv" if s == NSLOT - 1 else None)
                b.w("vector", "dv", dvv)
                b.op("vector", lambda e, sc=sc, n=n: e.tensor_copy(
                    sc[:].rearrange("p (s c) -> p s c", c=17)[:, :, n],
                    hbf[:].rearrange("p (s t) -> p s t", t=TILE)[:, :, TILE - 1]))
                if DEBUG and dc == 0 and n == 0 and "d_da00" in io:
                    b.op("vector", lambda e: e.tensor_copy(tA[:], dAf[:]), inc="dv")
                    b.w("gpsimd", "dv", b.c["dv"])
                    b.dma("gpsimd", io["d_da00"][:, :], tA[:])
                    b.dma("gpsimd", io["d_u00"][:, :], ubf[:])
                    b.dma("gpsimd", io["d_hb00"][:, :], hbf[:])
                    b.dma("gpsimd", io["d_dtx0"][:, :], dtxb[:])
                    b.w("vector", "gd", b.c["gd"])
                b.w("vector", "pl", b.c["pl"])
                dvv = b.op("vector", lambda e, n=n: e.tensor_tensor(
                    ybf[:], hbf[:], Cb[:, n * TPC:(n + 1) * TPC], ALU.mult), inc="dv")
                b.w("gpsimd", "dv", dvv)
                b.op("gpsimd", lambda e, yc=yc: e.tensor_tensor(yc[:], yc[:], ybf[:], ALU.add),
                     inc="pl")
            b.dma("gpsimd", io["ACC"][dc], yc[:], accum_op=ALU.add)
            b.dma("gpsimd", io["stloc"][dc * 128:(dc + 1) * 128, :], sc[:])
            gd_ck[dc % 2] = b.c["gd"]
        b.bar()


def _ffn0(b, io, P, xnb):
    ph = ExitStack()
    with ph:
        sb = lambda nm, sh, dt: ph.enter_context(b.nc.sbuf_tensor(nm, sh, dt))
        pstn = lambda nm, sh, dt=F32: ph.enter_context(b.nc.psum_tensor(nm, sh, dt))
        Esg = sb("f_Esg", [32, 4096], BF16)
        Esd = sb("f_Esd", [8, 1024], BF16)
        Edn = sb("f_Edn", [128, NHC * 8], BF16)
        b.dma("sync", Esg[:], io["Esg"])
        b.dma("sync", Esd[:], io["Esd"])
        b.dma("sync", Edn[:], io["Edn"])
        qb = {pr: [sb(f"f_q{pr}0", [128, NDC * 128], BF16)] for pr in ("g", "u")}
        wpb = {pr: sb(f"f_w{pr}", [128, NDC * 128], BF16) for pr in ("g", "u")}
        zrow = sb("f_zrow", [32, 1024], F32)
        srow = {pr: sb(f"f_srow{pr}", [32, 1024], F32) for pr in ("g", "u")}
        szb = {pr: sb(f"f_szb{pr}", [32, 1024], BF16) for pr in ("g", "u")}
        ssb = {pr: sb(f"f_ssb{pr}", [32, 1024], BF16) for pr in ("g", "u")}
        hslab = sb("f_hslab", [128, 8 * TPC], BF16)
        silu_t = sb("f_silu", [128, 512], F32)
        shblk = sb("f_shblk", [8, TPC], BF16)
        zdrow = sb("f_zdrow", [8, TPC], F32)
        sdrow = sb("f_sdrow", [8, TPC], F32)
        szd = sb("f_szd", [8, TPC], BF16)
        sdb = sb("f_sdb", [8, TPC], BF16)
        qdb = [sb(f"f_qdb{i}", [128, 8 * 128], BF16) for i in range(2)]
        wdb = sb("f_wdb", [128, 8 * 128], BF16)
        dnf = [sb(f"f_dnf{i}", [128, TPC], F32) for i in range(2)]
        b.bar(engs=("tensor", "vector", "gpsimd"), sems=("ld",))
        if "d_esg_e" in io:
            b.dma("gpsimd", io["d_esg_e"][:, :], Esg[:])
        ps_g = pstn("f_psg", [128, 512])
        ps_u = pstn("f_psu", [128, 512])
        ps_sb = pstn("f_pssb", [128, 1024])
        ps_sh = [pstn(f"f_pssh{t}", [22, 512]) for t in range(2)]
        ps_dn = [pstn(f"f_psdn{t}", [128, 512]) for t in range(2)]

        qsrc = {"g": io["qg"], "u": io["qu"]}
        zsrc = {"g": io["zg"], "u": io["zu"]}
        ssrc = {"g": io["sg"], "u": io["su"]}

        gd_q = {}
        dv_w = 0
        pe_hc = 0

        for blk_i, (k0, bs) in enumerate(DN_BLKS):
            for j in range(bs):
                hc = k0 + j
                par = 0
                for pr in ("g", "u"):
                    b.w("gpsimd", "dv", dv_w)
                    gd_q[(pr, 0)] = b.dma("gpsimd", qb[pr][0][:], qsrc[pr][hc])
                if hc % 8 == 0:
                    hn = min(8, NHC - hc)
                    for pr in ("g", "u"):
                        b.w("gpsimd", "dv", b.c["dv"])
                        b.w("gpsimd", "pe", b.c["pe"])
                        b.w("sync", "dv", b.c["dv"])
                        b.w("sync", "pe", b.c["pe"])
                        ldz = b.dma("gpsimd", zrow[:, 0:hn * 128],
                                    zsrc[pr][:, hc * 128:(hc + hn) * 128])
                        lds = b.dma("sync", srow[pr][:, 0:hn * 128],
                                    ssrc[pr][:, hc * 128:(hc + hn) * 128])
                        b.w("vector", "gd", ldz)
                        b.w("vector", "ld", lds)
                        b.op("vector", lambda e, pr=pr, hn=hn: e.tensor_tensor(
                            zrow[:, 0:hn * 128], zrow[:, 0:hn * 128], srow[pr][:, 0:hn * 128], ALU.mult))
                        b.op("vector", lambda e, pr=pr, hn=hn: e.tensor_scalar(
                            szb[pr][:, 0:hn * 128], zrow[:, 0:hn * 128], -1.0, None, ALU.mult))
                        b.op("vector", lambda e, pr=pr, hn=hn: e.tensor_copy(
                            ssb[pr][:, 0:hn * 128], srow[pr][:, 0:hn * 128]), inc="dv")
                hoff = (hc % 8) * 128
                for pr in ("g", "u"):
                    b.w("tensor", "gd", gd_q[(pr, par)])
                    b.w("tensor", "dv", b.c["dv"])
                    for half in range(4):
                        pev = None
                        for gg in range(8):
                            g_ = half * 8 + gg
                            pev = b.op("tensor", lambda e, g_=g_, gg=gg, pr=pr, hoff=hoff: e.matmul(
                                ps_sb[:, gg * 128:(gg + 1) * 128],
                                Esg[:, g_ * 128:(g_ + 1) * 128],
                                ssb[pr][:, hoff:hoff + 128], start=True, stop=True), inc="pe")
                        b.w("vector", "pe", pev)
                        if hc == 0 and pr == "g" and half == 0 and "d_sb0" in io:
                            dvv = b.op("vector", lambda e: e.tensor_copy(silu_t[:], ps_sb[:, 0:512]), inc="dv")
                            b.w("gpsimd", "dv", dvv)
                            b.dma("gpsimd", io["d_sb0"][:, 0:512], silu_t[:])
                            dvv = b.op("vector", lambda e: e.tensor_copy(silu_t[:], ps_sb[:, 512:1024]), inc="dv")
                            b.w("gpsimd", "dv", dvv)
                            b.dma("gpsimd", io["d_sb0"][:, 512:1024], silu_t[:])
                            b.dma("gpsimd", io["d_ssbg"][:, :], ssb["g"][:])
                            b.dma("gpsimd", io["d_qb0"][:, :], qb["g"][0][:])
                            b.dma("gpsimd", io["d_esg_l"][:, :], Esg[:])
                            b.w("vector", "gd", b.c["gd"])
                        dv_w = b.op("vector", lambda e, pr=pr, half=half: e.tensor_tensor(
                            wpb[pr][:, half * 1024:(half + 1) * 1024],
                            qb[pr][0][:, half * 1024:(half + 1) * 1024],
                            ps_sb[:], ALU.mult), inc="dv")
                        b.w("tensor", "dv", dv_w)
                for t in range(2):
                    for pr, psx in (("g", ps_g), ("u", ps_u)):
                        for k in range(NDC):
                            b.op("tensor", lambda e, pr=pr, psx=psx, k=k, t_=t: e.matmul(
                                psx[:], wpb[pr][:, k * 128:(k + 1) * 128],
                                xnb[:, k * TPC + t_ * 512:k * TPC + (t_ + 1) * 512],
                                start=(k == 0), stop=False))
                        pe_hc = b.op("tensor", lambda e, pr=pr, psx=psx, t_=t, hoff=hoff: e.matmul(
                            psx[:], szb[pr][:, hoff:hoff + 128],
                            P["Sg"][:, t_ * 512:(t_ + 1) * 512], start=False, stop=True), inc="pe")
                    b.w("scalar", "pe", pe_hc)
                    b.w("scalar", "dv", b.c["dv"])
                    acv = b.act(lambda e: e.activation(silu_t[:], ps_g[:], AF.Silu))
                    b.w("vector", "ac", acv)
                    dvv = b.op("vector", lambda e, j=j, t_=t: e.tensor_tensor(
                        hslab[:, j * TPC + t_ * 512:j * TPC + (t_ + 1) * 512],
                        silu_t[:], ps_u[:], ALU.mult), inc="dv")
                    b.w("tensor", "dv", dvv)
                    if DEBUG and hc == 0 and t == 1 and pr == "u" and "d_wg0" in io:
                        b.w("gpsimd", "dv", b.c["dv"])
                        b.dma("gpsimd", io["d_wg0"][:, :], wpb["g"][:])
                        b.dma("gpsimd", io["d_h0"][:, :], hslab[:, 0:TPC])
                    b.op("tensor", lambda e, j=j, t_=t, hc=hc, bs=bs: e.matmul(
                        ps_sh[t_][:bs, :], Edn[:, hc * 8:hc * 8 + bs],
                        hslab[:, j * TPC + t_ * 512:j * TPC + (t_ + 1) * 512],
                        start=(j == 0), stop=(j == bs - 1)),
                        inc="pe" if j == bs - 1 else None)
            b.w("gpsimd", "gd", b.c["gd"])
            b.w("vector", "pe")
            for t in range(2):
                b.op("vector", lambda e, t_=t, bs=bs: e.tensor_copy(
                    shblk[:bs, t_ * 512:(t_ + 1) * 512], ps_sh[t_][:bs, :]), inc="dv")
            gd_dn = [0, 0]
            gd_dn[0] = b.dma("gpsimd", qdb[0][:, 0:bs * 128], io["qd"][0][:, k0 * 128:(k0 + bs) * 128])
            dn_dv = [0, 0]
            for dc in range(NDC):
                if dc % 8 == 0:
                    b.w("gpsimd", "dv", b.c["dv"])
                    b.w("sync", "dv", b.c["dv"])
                    b.w("sync", "pe", b.c["pe"])
                    ldz = b.dma("gpsimd", zdrow[:bs, :], io["zd"][k0:k0 + bs, dc * 128:(dc + 8) * 128])
                    lds = b.dma("sync", sdrow[:bs, :], io["sd"][k0:k0 + bs, dc * 128:(dc + 8) * 128])
                    b.w("vector", "gd", ldz)
                    b.w("vector", "ld", lds)
                    b.w("vector", "pe", b.c["pe"])
                    b.op("vector", lambda e, bs=bs: e.tensor_tensor(
                        zdrow[:bs, :], zdrow[:bs, :], sdrow[:bs, :], ALU.mult))
                    b.op("vector", lambda e, bs=bs: e.tensor_scalar(
                        szd[:bs, :], zdrow[:bs, :], -1.0, None, ALU.mult))
                    b.op("vector", lambda e, bs=bs: e.tensor_copy(sdb[:bs, :], sdrow[:bs, :]), inc="dv")
                if dc + 1 < NDC:
                    b.w("gpsimd", "dv", b.c["dv"])
                    gd_dn[(dc + 1) % 2] = b.dma("gpsimd", qdb[(dc + 1) % 2][:, 0:bs * 128],
                                                io["qd"][dc + 1][:, k0 * 128:(k0 + bs) * 128])
                b.w("tensor", "gd", gd_dn[dc % 2])
                b.w("tensor", "dv", b.c["dv"])
                for c0 in range(0, bs, 8):
                    cnt = min(8, bs - c0)
                    pev = None
                    for kk in range(cnt):
                        jj = c0 + kk
                        pev = b.op("tensor", lambda e, jj=jj, kk=kk, dc=dc, bs=bs: e.matmul(
                            ps_sb[:, kk * 128:(kk + 1) * 128],
                            Esd[:bs, jj * 128:(jj + 1) * 128],
                            sdb[:bs, (dc % 8) * 128:(dc % 8 + 1) * 128], start=True, stop=True), inc="pe")
                    b.w("vector", "pe", pev)
                    dvv = b.op("vector", lambda e, c0=c0, cnt=cnt, dc=dc: e.tensor_tensor(
                        wdb[:, c0 * 128:(c0 + cnt) * 128],
                        qdb[dc % 2][:, c0 * 128:(c0 + cnt) * 128],
                        ps_sb[:, 0:cnt * 128], ALU.mult), inc="dv")
                    b.w("tensor", "dv", dvv)
                if blk_i == 0 and dc in (0, 24) and "d_wd0" in io:
                    b.w("gpsimd", "dv", b.c["dv"])
                    b.dma("gpsimd", io["d_wd0" if dc == 0 else "d_wd24"][:, :], wdb[:, 0:1024])
                    if dc == 0:
                        b.dma("gpsimd", io["d_sdb"][:, 0:1024], sdb[:8, :])
                        b.dma("gpsimd", io["d_szd"][:, 0:1024], szd[:8, :])
                        b.dma("gpsimd", io["d_shblk"][:, :], shblk[:8, :])
                    b.w("tensor", "gd", b.c["gd"])
                pe_dn = 0
                for t in range(2):
                    b.w("tensor", "dv", dn_dv[t])
                    for jj in range(bs):
                        b.op("tensor", lambda e, jj=jj, t_=t: e.matmul(
                            ps_dn[t_][:], wdb[:, jj * 128:(jj + 1) * 128],
                            hslab[:, jj * TPC + t_ * 512:jj * TPC + (t_ + 1) * 512],
                            start=(jj == 0), stop=False))
                    pe_dn = b.op("tensor", lambda e, t_=t, dc=dc, bs=bs: e.matmul(
                        ps_dn[t_][:], szd[:bs, (dc % 8) * 128:(dc % 8 + 1) * 128],
                        shblk[:bs, t_ * 512:(t_ + 1) * 512], start=False, stop=True), inc="pe")
                b.w("vector", "pe", pe_dn)
                b.w("vector", "gd", b.c["gd"] if dc >= 2 else 0)
                dnb = dnf[dc % 2]
                for t in range(2):
                    dn_dv[t] = b.op("vector", lambda e, t_=t, dnb=dnb: e.tensor_copy(
                        dnb[:, t_ * 512:(t_ + 1) * 512], ps_dn[t_][:]), inc="dv")
                b.w("gpsimd", "dv", dn_dv[1])
                b.dma("gpsimd", io["ACC"][dc], dnb[:], accum_op=ALU.add)
        b.bar()


def _combine_fix(b, io, P, Bb, Cb):
    ph = ExitStack()
    with ph:
        sb = lambda nm, sh, dt: ph.enter_context(b.nc.sbuf_tensor(nm, sh, dt))
        stag = sb("c_stag", [128, 544], F32)
        stag2 = sb("c_stag2", [128, 544], F32)
        Sng = sb("c_sng", [128, 512], F32)
        Tg = sb("c_tg", [128, 32], F32)
        Tsh = sb("c_tsh", [128, 32], F32)
        DD = sb("c_dd", [128, 512], F32)
        Ssh = sb("c_ssh", [128, 512], F32)
        Hin = sb("c_hin", [128, 512], F32)
        Hinb = sb("c_hinb", [128, 512], BF16)
        tmpb = sb("c_tmpb", [128, 512], BF16)
        Hsel = sb("c_hsel", [128, 128], F32)
        hmskb = sb("c_hmsk", [128, NSLOT * 512], BF16)
        dtf = sb("c_dtf", [128, TPC], F32)
        tA = sb("c_tA", [128, TPC], F32)
        tB = sb("c_tB", [128, TPC], F32)
        cum = sb("c_cum", [128, TPC], F32)
        Gbf = sb("c_gbf", [128, TPC], BF16)
        Hb = sb("c_hb", [128, TPC], BF16)
        fix1 = sb("c_fix1", [128, TPC], BF16)
        ych = [sb(f"c_ych{i}", [128, TPC], F32) for i in range(2)]
        ps_dt = [ph.enter_context(b.nc.psum_tensor(f"c_psdt{t}", [128, 512], F32)) for t in range(2)]

        for s in range(NSLOT):
            b.dma("sync", hmskb[:, s * 512:(s + 1) * 512], io["hmsk"][s])
        b.bar(engs=("vector", "gpsimd"))

        gd_ck = [0, 0]
        pl_stag = 0
        dv_stag = 0
        for dc in range(NDC):
            b.w("sync", "pl", pl_stag)
            for qq in range(4):
                b.dma("sync", stag[:, qq * 136:(qq + 1) * 136],
                      io["stg"][qq * DM + dc * 128:qq * DM + (dc + 1) * 128, :])
            b.w("gpsimd", "ld", b.c["ld"])
            b.w("gpsimd", "dv", dv_stag)
            plv = None
            for g in range(32):
                qp, sp = G2QS[g]
                plv = b.op("gpsimd", lambda e, g=g, qp=qp, sp=sp: e.tensor_copy(
                    stag2[:, g * 17:(g + 1) * 17],
                    stag[:, qp * 136 + sp * 17:qp * 136 + sp * 17 + 17]), inc="pl")
            pl_stag = plv
            b.w("vector", "pl", plv)
            st3 = stag2[:].rearrange("p (g c) -> p c g", c=17)
            b.op("vector", lambda e, st3=st3: e.tensor_copy(
                Sng[:].rearrange("p (n g) -> p n g", g=32), st3[:, 0:16, :]))
            dv_stag = b.op("vector", lambda e, st3=st3: e.tensor_copy(Tg[:], st3[:, 16, :]), inc="dv")
            if dc == 0 and "d_tg0" in io:
                b.w("sync", "dv", dv_stag)
                b.dma("sync", io["d_stag20"], stag2[:])
                b.dma("sync", io["d_sng0"], Sng[:])
                b.dma("sync", io["d_tg0"], Tg[:])
            b.op("vector", lambda e: e.memset(Tsh[:, 0:1], 0.0))
            b.op("vector", lambda e: e.tensor_copy(Tsh[:, 1:32], Tg[:, 0:31]))
            b.op("vector", lambda e: e.drain())
            for n in range(NST):
                b.op("vector", lambda e, n=n, dc=dc: e.tensor_scalar(
                    DD[:, n * 32:(n + 1) * 32], Tsh[:], _col(P["A"], dc * NST + n), None, ALU.mult),
                    inc="dv" if n == NST - 1 else None)
            b.w("scalar", "dv")
            acv = b.act(lambda e: e.activation(DD[:], DD[:], AF.Exp))
            b.op("vector", lambda e: e.memset(Ssh[:], 0.0))
            b.op("vector", lambda e: e.tensor_copy(
                Ssh[:].rearrange("p (n g) -> p n g", g=32)[:, :, 1:32],
                Sng[:].rearrange("p (n g) -> p n g", g=32)[:, :, 0:31]))
            b.op("vector", lambda e: e.drain())
            b.w("vector", "ac", acv)
            for n in range(NST):
                b.op("vector", lambda e, n=n: e.memset(DD[:, n * 32:n * 32 + 1], 0.0))
            b.op("vector", lambda e: e.drain())
            dvv = b.op("vector", lambda e: e.tensor_tensor_scan(
                Hin[:], DD[:], Ssh[:], 0.0, ALU.mult, ALU.add), inc="dv")
            b.w("vector", "dv", dvv)
            b.op("vector", lambda e: e.tensor_copy(Hinb[:], Hin[:]), inc="dv")
            if dc == 0 and "d_hin0" in io:
                b.w("sync", "dv", b.c["dv"])
                b.dma("sync", io["d_hin0"], Hin[:])
            b.w("gpsimd", "dv", b.c["dv"])
            for s in range(NSLOT):
                b.w("gpsimd", "dv", b.c["dv"])
                plv = b.op("gpsimd", lambda e, s=s: e.tensor_tensor(
                    tmpb[:], Hinb[:], hmskb[:, s * 512:(s + 1) * 512], ALU.mult), inc="pl")
                b.w("vector", "pl", plv)
                b.op("vector", lambda e, s=s: e.tensor_reduce(
                    Hsel[:, s * 16:(s + 1) * 16],
                    tmpb[:].rearrange("p (n g) -> p n g", g=32), axis=AX, op=ALU.add),
                    inc="dv")
            if dc == 0 and "d_hsel0" in io:
                b.w("sync", "dv", b.c["dv"])
                b.dma("sync", io["d_hsel0"], Hsel[:])
            yc = ych[dc % 2]
            b.w("vector", "gd", gd_ck[dc % 2])
            b.w("tensor", "dv", b.c["dv"])
            _dt_chunk(b, P, dc, ps_dt, dtf, tA, tB)
            for s in range(NSLOT):
                dvv = b.op("vector", lambda e, s=s: e.tensor_tensor_scan(
                    cum[:, s * TILE:(s + 1) * TILE], P["ones128t"][:],
                    dtf[:, s * TILE:(s + 1) * TILE], 0.0, ALU.mult, ALU.add),
                    inc="dv" if s == NSLOT - 1 else None)
            b.w("vector", "dv", dvv)
            if dc == 0 and "d_cum0" in io:
                b.w("sync", "dv", b.c["dv"])
                b.dma("sync", io["d_cum0"], cum[:])
                b.dma("sync", io["d_dtf0"], dtf[:])
                b.w("gpsimd", "ld", b.c["ld"])
                b.w("vector", "ld", b.c["ld"])
            pl_fix = 0
            for n in range(NST):
                dvv = b.op("vector", lambda e, n=n, dc=dc: e.tensor_scalar(
                    tA[:], cum[:], _col(P["A"], dc * NST + n), None, ALU.mult), inc="dv")
                b.w("scalar", "dv", dvv)
                b.w("scalar", "pl", pl_fix)
                acv = b.act(lambda e: e.activation(Gbf[:], tA[:], AF.Exp))
                hsrc = Hsel[:].rearrange("p (s n) -> p s n", n=NST)[:, :, n].unsqueeze(2).to_broadcast((128, 8, 128))
                b.w("vector", "pl", pl_fix)
                b.op("vector", lambda e, hsrc=hsrc: e.tensor_copy(
                    Hb[:].rearrange("p (s t) -> p s t", t=TILE), hsrc), inc="dv")
                b.w("gpsimd", "ac", acv)
                b.w("gpsimd", "dv", b.c["dv"])
                pl_fix = b.op("gpsimd", lambda e: e.tensor_tensor(fix1[:], Gbf[:], Hb[:], ALU.mult),
                           inc="pl")
                plv = pl_fix
                b.w("vector", "pl", plv)
                dvv = b.op("vector", lambda e, n=n: e.tensor_tensor(
                    fix1[:], fix1[:], Cb[:, n * TPC:(n + 1) * TPC], ALU.mult), inc="dv")
                b.w("gpsimd", "dv", dvv)
                if n == 0:
                    plv = b.op("gpsimd", lambda e, yc=yc: e.tensor_copy(yc[:], fix1[:]), inc="pl")
                else:
                    plv = b.op("gpsimd", lambda e, yc=yc: e.tensor_tensor(
                        yc[:], yc[:], fix1[:], ALU.add), inc="pl")
                b.w("vector", "pl", plv)
            b.w("gpsimd", "pl", b.c["pl"])
            b.dma("gpsimd", io["ACC"][dc], yc[:], accum_op=ALU.add)
            gd_ck[dc % 2] = b.c["gd"]
        b.bar()


def _l1_phases(b, io, P, mode):
    nc = b.nc
    t1s_g = b.sb("t1s_g", [128, TPC], BF16) if mode in ("L3", "ALL") else None
    if mode in ("L2", "ALL"):
        ph = ExitStack()
        with ph:
            sb = lambda nm, sh, dt: ph.enter_context(nc.sbuf_tensor(nm, sh, dt))
            pstn = lambda nm, sh: ph.enter_context(nc.psum_tensor(nm, sh, F32))
            xnb1 = sb("q_xnb1", [128, NDC * TPC], BF16)
            mb1 = sb("q_mb1", [128, TPC], F32)
            rstdb1 = sb("q_rstdb1", [128, TPC], F32)
            _ln_phase(b, P, io["ACC"], io, with_sg=False, with_xp=False, xnb=xnb1,
                      mb=mb1, rstdb=rstdb1)
            import os
            cut = os.environ.get("K_L2CUT", "") if mode == "L2" else ""
            if cut == "ln1":
                for dc in range(NDC):
                    b.dma("gpsimd", io["xn1o"][dc], xnb1[:, dc * TPC:(dc + 1) * TPC])
                b.w("gpsimd", "gd", b.c["gd"])
                b.w("gpsimd", "ld", b.c["ld"])
                return

            wqb = [sb(f"q_wq{i}", [128, NDC * 128], BF16) for i in range(2)]
            bq = sb("q_bq", [128, NQKV], F32)
            evb = [sb(f"q_ev{i}", [128, TPC], BF16) for i in range(2)]
            qtb = sb("q_qtb", [128, NCC * TPC], BF16)
            arsb = sb("q_arsb", [128, NDC], F32)
            armul = sb("q_armul", [128, TPC], F32)
            wvbs = sb("q_wvb", [128, TPC], F32)
            t1f = sb("q_t1f", [128, TPC], F32)
            acb = sb("q_acb", [128, NDC * 128], BF16)
            ps_q = [[pstn(f"q_psq{i}{t}", [128, 512]) for t in range(2)] for i in range(2)]

            b.dma("sync", bq[:], io["bqkv_t"])
            b.dma("sync", wvbs[:], io["wvb"])
            b.bar(engs=("vector", "gpsimd"))
            for dc in range(NDC):
                b.op("vector", lambda e, dc=dc: e.tensor_tensor(
                    armul[:], xnb1[:, dc * TPC:(dc + 1) * TPC], wvbs[:], ALU.mult))
                b.op("vector", lambda e, dc=dc: e.tensor_reduce(
                    arsb[:, dc:dc + 1], armul[:], axis=AX, op=ALU.add),
                    inc="dv" if dc == NDC - 1 else None)
            if cut == "rt":
                b.w("gpsimd", "dv", b.c["dv"])
                b.dma("gpsimd", io["arpart"][:, :], arsb[:])
                for dc in range(NDC):
                    b.dma("gpsimd", io["xn1o"][dc], xnb1[:, dc * TPC:(dc + 1) * TPC])
                b.w("gpsimd", "gd", b.c["gd"])
                b.w("gpsimd", "ld", b.c["ld"])
                return
            gd_w = [0, 0]
            gd_w[0] = b.dma("gpsimd", wqb[0][:].rearrange("p (c r) -> p c r", r=128),
                            io["wqkvT"].rearrange("c p (o r) -> p c r o", r=128)[:, :, :, 0])
            pe_oc = 0
            dv_oc = [0, 0]
            for oc in range(NQKV):
                if oc + 1 < NQKV:
                    b.w("gpsimd", "pe", pe_oc)
                    gd_w[(oc + 1) % 2] = b.dma(
                        "gpsimd", wqb[(oc + 1) % 2][:].rearrange("p (c r) -> p c r", r=128),
                        io["wqkvT"].rearrange("c p (o r) -> p c r o", r=128)[:, :, :, oc + 1])
                b.w("tensor", "gd", gd_w[oc % 2])
                b.w("tensor", "dv", dv_oc[oc % 2])
                for t in range(2):
                    for k in range(NDC):
                        b.op("tensor", lambda e, t_=t, k=k, oc=oc: e.matmul(
                            ps_q[oc % 2][t_][:], wqb[oc % 2][:, k * 128:(k + 1) * 128],
                            xnb1[:, k * TPC + t_ * 512:k * TPC + (t_ + 1) * 512],
                            start=(k == 0), stop=(k == NDC - 1)),
                            inc="pe" if k == NDC - 1 else None)
                pe_oc = b.c["pe"]
                b.w("vector", "pe", pe_oc)
                if oc < NCC:
                    dst = [qtb[:, oc * TPC + t * 512:oc * TPC + (t + 1) * 512] for t in range(2)]
                else:
                    dst = [evb[oc % 2][:, t * 512:(t + 1) * 512] for t in range(2)]
                    b.w("vector", "gd", b.c["gd"])
                for t in range(2):
                    dvv = b.op("vector", lambda e, t_=t, oc=oc, d=dst[t]: e.tensor_scalar(
                        d, ps_q[oc % 2][t_][:], _col(bq, oc), None, ALU.add), inc="dv")
                dv_oc[oc % 2] = dvv
                if oc >= NCC:
                    b.w("gpsimd", "dv", dvv)
                    b.dma("gpsimd", io["kvloc"][(oc - NCC) * 128:(oc - NCC + 1) * 128, :],
                          evb[oc % 2][:])
            if cut == "qkv":
                b.w("gpsimd", "dv", b.c["dv"])
                for cc in range(NCC):
                    b.dma("gpsimd", io["qto"][cc], qtb[:, cc * TPC:(cc + 1) * TPC])
                for dc in range(NDC):
                    b.dma("gpsimd", io["xn1o"][dc], xnb1[:, dc * TPC:(dc + 1) * TPC])
                b.dma("gpsimd", io["arpart"][:, :], arsb[:])
                b.w("gpsimd", "gd", b.c["gd"])
                b.w("gpsimd", "ld", b.c["ld"])
                return
            b.dma("gpsimd", acb[:].rearrange("p (c r) -> p c r", r=128), io["AcatT"].rearrange("c p r -> p c r"))
            b.w("tensor", "gd", b.c["gd"])
            b.w("tensor", "dv", b.c["dv"])
            for t in range(2):
                for k in range(NDC):
                    b.op("tensor", lambda e, t_=t, k=k: e.matmul(
                        ps_q[0][t_][:], acb[:, k * 128:(k + 1) * 128],
                        xnb1[:, k * TPC + t_ * 512:k * TPC + (t_ + 1) * 512],
                        start=(k == 0), stop=(k == NDC - 1)),
                        inc="pe" if k == NDC - 1 else None)
            b.w("vector", "pe")
            for t in range(2):
                b.op("vector", lambda e, t_=t: e.tensor_copy(
                    t1f[:, t_ * 512:(t_ + 1) * 512], ps_q[0][t_][:]), inc="dv")
            if mode == "L2":
                b.w("gpsimd", "dv", b.c["dv"])
                for dc in range(NDC):
                    b.dma("gpsimd", io["xn1o"][dc], xnb1[:, dc * TPC:(dc + 1) * TPC])
                for cc in range(NCC):
                    b.dma("gpsimd", io["qto"][cc], qtb[:, cc * TPC:(cc + 1) * TPC])
                b.dma("gpsimd", io["arpart"][:, :], arsb[:])
                b.dma("gpsimd", io["t1o"][:, :], t1f[:])
                b.w("gpsimd", "gd", b.c["gd"])
                b.w("gpsimd", "ld", b.c["ld"])
            else:
                b.w("gpsimd", "dv", b.c["dv"])
                for dc in range(NDC):
                    b.dma("gpsimd", io["xn1d"][dc], xnb1[:, dc * TPC:(dc + 1) * TPC])
                b.dma("gpsimd", io["ar_in"][:, :], arsb[:])
                b.w("gpsimd", "gd", b.c["gd"])
                b.op("gpsimd", lambda e: e.collective_compute(
                    "AllGather", ALU.bypass, replica_groups=[[0, 1, 2, 3], [4, 5, 6, 7]],
                    ins=[io["kvloc"].opt()], outs=[io["kvg"].opt()]), inc="cc", drain=False)
                b.op("gpsimd", lambda e: e.collective_compute(
                    "AllReduce", ALU.add, replica_groups=[[0, 1, 2, 3], [4, 5, 6, 7]],
                    ins=[io["ar_in"].opt()], outs=[io["ar_out"].opt()]), inc="cc", drain=False)
                for cc in range(NCC):
                    b.dma("gpsimd", io["qtd"][cc], qtb[:, cc * TPC:(cc + 1) * TPC])
                b.dma("gpsimd", io["t1d"][:, :], t1f[:])
                b.w("gpsimd", "gd", b.c["gd"])
            b.bar()

    if mode not in ("L3", "ALL"):
        return

    import os as _os
    cut3 = _os.environ.get("K_L3CUT", "") if mode == "L3" else ""

    def fin():
        b.dma("gpsimd", io["out_t"].rearrange("c p t -> (c p) t"),
              io["ACC"].rearrange("c p t -> (c p) t"), sem="so")
        b.w("gpsimd", "so", b.c["so"])

    qsrcs = io["qti"] if mode == "L3" else io["qtd"]
    kvg = io["kvg"]
    ph = ExitStack()
    with ph:
        sb = lambda nm, sh, dt: ph.enter_context(nc.sbuf_tensor(nm, sh, dt))
        pstn = lambda nm, sh, dt=F32: ph.enter_context(nc.psum_tensor(nm, sh, dt))
        qtb = sb("a_qtb", [128, NCC * TPC], BF16)
        Oall = sb("a_oall", [128, 64 * 256], F32)
        mall = sb("a_mall", [128, 64], F32)
        lall = sb("a_lall", [128, 64], F32)
        ksl = sb("a_ksl", [128, NCC * 512], BF16)
        vsl = [sb(f"a_vsl{j}", [128, 2048], BF16) for j in range(4)]
        smf = sb("a_smf", [128, 512], F32)
        pbf = sb("a_pbf", [128, 512], BF16)
        ptb = sb("a_ptb", [128, 512], BF16)
        maskb = sb("a_maskb", [128, NSLOT * 512], F32)
        cmax = sb("a_cmax", [128, 1], F32)
        mnew = sb("a_mnew", [128, 1], F32)
        negm = sb("a_negm", [128, 1], F32)
        alph = sb("a_alph", [128, 1], F32)
        rs = sb("a_rs", [128, 1], F32)
        rl = sb("a_rl", [128, 1], F32)
        vvb = sb("a_vvb", [128, 256], BF16)
        arsb2 = sb("a_arsb", [128, NDC], F32)
        ar4 = sb("a_ar4", [128, NDC], F32)
        qcolb = sb("a_qcol", [128, NDC], BF16)
        rwb = sb("a_rwb", [128, NDC * 8], BF16)
        rbc = sb("a_rbc", [8, 1], F32)
        lg = sb("a_lg", [8, 1], F32)
        l8 = sb("a_l8", [1, 8], F32)
        e8 = sb("a_e8", [1, 8], F32)
        sm1 = sb("a_sm1", [1, 1], F32)
        sm1b = sb("a_sm1b", [1, 1], F32)
        e8b = sb("a_e8b", [1, 8], F32)
        rcol = sb("a_rcol", [8, 1], F32)
        rvc = sb("a_rvc", [128, 1], F32)
        ervb = sb("a_ervb", [8, 128], F32)
        t1f = sb("a_t1f", [128, TPC], F32)
        t1s = t1s_g
        dsm = sb("a_dsm", [128, 512], F32) if "d_smf00" in io else None
        dpb = sb("a_dpb", [128, 512], BF16) if "d_smf00" in io else None
        dob = sb("a_dob", [128, 256], F32) if "d_smf00" in io else None
        ps_s = pstn("a_pss", [128, 512])
        ps_pt = pstn("a_pspt", [128, 512], BF16)
        ps_av = pstn("a_psav", [128, 256])
        ps_r = pstn("a_psr", [8, 1])
        ps_rv = pstn("a_psrv", [128, 1])

        for s in range(NSLOT):
            b.dma("sync", maskb[:, s * 512:(s + 1) * 512], io["masks"][s])
        for cc in range(NCC):
            b.dma("sync", qtb[:, cc * TPC:(cc + 1) * TPC], qsrcs[cc])
        b.dma("sync", ervb[:], io["Erv"])
        b.dma("sync", rbc[:], io["rb_t"])
        b.dma("sync", t1f[:], io["t1i"][:, :] if mode == "L3" else io["t1d"][:, :])
        b.dma("gpsimd", rwb[:].rearrange("p (c r) -> p c r", r=8), io["rWT"].rearrange("c p r -> p c r"))
        if mode == "L3":
            b.dma("sync", arsb2[:], io["arall"][0:128, :])
            for qq in range(1, 4):
                b.dma("sync", ar4[:], io["arall"][qq * 128:(qq + 1) * 128, :])
                b.w("vector", "ld", b.c["ld"])
                b.op("vector", lambda e: e.tensor_tensor(arsb2[:], arsb2[:], ar4[:], ALU.add),
                     inc="dv")
                b.w("sync", "dv", b.c["dv"])
        else:
            b.w("sync", "cc", b.c["cc"])
            b.dma("sync", arsb2[:], io["ar_out"][:, :])
        b.bar(engs=("vector", "tensor", "scalar", "gpsimd"))
        b.op("vector", lambda e: e.tensor_scalar(qcolb[:], arsb2[:], R_INV, None, ALU.mult), inc="dv")
        b.w("tensor", "dv")
        for dc in range(NDC):
            b.op("tensor", lambda e, dc=dc: e.matmul(
                ps_r[:], rwb[:, dc * 8:(dc + 1) * 8], qcolb[:, dc:dc + 1],
                start=(dc == 0), stop=(dc == NDC - 1)), inc="pe" if dc == NDC - 1 else None)
        b.w("vector", "pe")
        b.op("vector", lambda e: e.tensor_tensor(lg[:], ps_r[:], rbc[:], ALU.add), inc="dv")
        b.w("gpsimd", "dv", b.c["dv"])
        b.dma("gpsimd", io["bnc2"], lg[:], sem="gd")
        b.w("sync", "gd", b.c["gd"])
        b.dma("sync", l8[:], io["bnc2"].rearrange("a b -> b a"))
        b.w("vector", "ld", b.c["ld"])
        b.op("vector", lambda e: e.reduce_max(sm1[:], l8[:], axis=AX))
        b.op("vector", lambda e: e.drain())
        b.op("vector", lambda e: e.tensor_scalar(sm1[:], sm1[:], -1.0, None, ALU.mult), inc="dv")
        b.w("scalar", "dv", b.c["dv"])
        b.act(lambda e: e.activation(e8[:], l8[:], AF.Exp, bias=sm1[:]))
        b.w("vector", "ac", b.c["ac"])
        b.op("vector", lambda e: e.tensor_reduce(sm1[:], e8[:], axis=AX, op=ALU.add))
        b.op("vector", lambda e: e.drain())
        b.op("vector", lambda e: e.reciprocal(sm1b[:], sm1[:]))
        b.op("vector", lambda e: e.drain())
        b.op("vector", lambda e: e.tensor_scalar(e8b[:], e8[:], sm1b[:], 2.0, ALU.mult, ALU.mult),
             inc="dv")
        b.w("gpsimd", "dv", b.c["dv"])
        b.dma("gpsimd", io["bnc1"], e8b[:], sem="gd")
        b.w("sync", "gd", b.c["gd"])
        b.dma("sync", rcol[:], io["bnc1"].rearrange("a b -> b a"))
        b.w("tensor", "ld", b.c["ld"])
        b.op("tensor", lambda e: e.matmul(ps_rv[:], ervb[:], rcol[:], start=True, stop=True),
             inc="pe")
        b.w("vector", "pe")
        b.op("vector", lambda e: e.tensor_copy(rvc[:], ps_rv[:]))
        b.op("vector", lambda e: e.drain())
        b.op("vector", lambda e: e.tensor_scalar(t1s[:], t1f[:], rvc[:], None, ALU.mult), inc="dv")
        if cut3 == "rt":
            b.w("gpsimd", "dv", b.c["dv"])
            b.w("gpsimd", "ld", b.c["ld"])
            if "d_rvc" in io:
                b.dma("gpsimd", io["d_rvc"], rvc[:])
                b.dma("gpsimd", io["d_lg"], lg[:])
                b.dma("gpsimd", io["d_l8"], l8[:])
                b.dma("gpsimd", io["d_e8"], e8[:])
                b.dma("gpsimd", io["d_e8b"], e8b[:])
                b.dma("gpsimd", io["d_sm1"], sm1[:])
                b.dma("gpsimd", io["d_sm1b"], sm1b[:])
                b.dma("gpsimd", io["d_rcol"], rcol[:])
            b.w("gpsimd", "gd", b.c["gd"])
            fin()
            b.w("gpsimd", "ld", b.c["ld"])
            return

        ld_fc = 0
        for fc in range(NSLOT):
            if fc > 0:
                b.w("sync", "pe", b.c["pe"])
            for j in range(4):
                tg = fc * 4 + j
                qp, sp = G2QS[tg]
                base = qp * DM
                src = kvg[base:base + 2048, sp * 128:(sp + 1) * 128]
                b.dma("sync", ksl[:].rearrange("p (c w) -> p c w", w=512)[:, :, j * 128:(j + 1) * 128],
                      src.rearrange("(c p) f -> p c f", p=128))
                b.dma("sync", vsl[j][:], kvg[base + 2048:base + 4096, sp * 128:(sp + 1) * 128],
                      transpose=True)
            ld_fc = b.c["ld"]
            b.w("tensor", "ld", ld_fc)
            for s in range(fc, NSLOT):
                for h in range(HEADS):
                    ii = s * 8 + h
                    pev = None
                    for kc in range(2):
                        cc = 2 * h + kc
                        pev = b.op("tensor", lambda e, cc=cc, s=s, kc=kc: e.matmul(
                            ps_s[:], qtb[:, cc * TPC + s * 128:cc * TPC + (s + 1) * 128],
                            ksl[:, cc * 512:(cc + 1) * 512], start=(kc == 0), stop=(kc == 1)),
                            inc="pe")
                    b.w("vector", "pe", pev)
                    if fc == s:
                        b.op("vector", lambda e, s=s: e.tensor_tensor(
                            smf[:], ps_s[:], maskb[:, s * 512:(s + 1) * 512], ALU.add))
                    else:
                        b.op("vector", lambda e: e.tensor_copy(smf[:], ps_s[:]))
                    dvv = b.op("vector", lambda e: e.reduce_max(cmax[:], smf[:], axis=AX), inc="dv")
                    if fc == 0 and s == 0 and h == 0 and dsm is not None:
                        b.op("vector", lambda e: e.tensor_copy(dsm[:], smf[:]))
                    if fc == 0:
                        b.op("vector", lambda e, ii=ii: e.tensor_copy(_col(mall, ii), cmax[:]))
                        b.op("vector", lambda e: e.tensor_scalar(
                            negm[:], cmax[:], -1.0, None, ALU.mult), inc="dv")
                    else:
                        b.op("vector", lambda e, ii=ii: e.tensor_tensor(
                            mnew[:], _col(mall, ii), cmax[:], ALU.max))
                        b.op("vector", lambda e: e.drain())
                        b.op("vector", lambda e, ii=ii: e.tensor_tensor(
                            alph[:], _col(mall, ii), mnew[:], ALU.subtract))
                        b.op("vector", lambda e, ii=ii: e.tensor_copy(_col(mall, ii), mnew[:]))
                        b.op("vector", lambda e: e.tensor_scalar(
                            negm[:], mnew[:], -1.0, None, ALU.mult), inc="dv")
                        b.w("scalar", "dv", b.c["dv"])
                        acv = b.act(lambda e: e.activation(alph[:], alph[:], AF.Exp))
                        b.w("vector", "ac", acv)
                        b.op("vector", lambda e, ii=ii: e.tensor_scalar(
                            Oall[:, ii * 256:(ii + 1) * 256], Oall[:, ii * 256:(ii + 1) * 256],
                            alph[:], None, ALU.mult))
                        b.op("vector", lambda e, ii=ii: e.tensor_scalar(
                            _col(lall, ii), _col(lall, ii), alph[:], None, ALU.mult), inc="dv")
                    b.w("scalar", "dv", b.c["dv"])
                    acv = b.act(lambda e: e.activation(
                        pbf[:], smf[:], AF.Exp, bias=negm[:]))
                    b.w("vector", "ac", acv)
                    if fc == 0 and s == 0 and h == 0 and dsm is not None:
                        b.op("vector", lambda e: e.tensor_copy(dpb[:], pbf[:]))
                    b.op("vector", lambda e: e.tensor_reduce(rs[:], pbf[:], axis=AX, op=ALU.add))
                    b.op("vector", lambda e: e.drain())
                    if fc == 0:
                        b.op("vector", lambda e, ii=ii: e.tensor_copy(_col(lall, ii), rs[:]),
                             inc="dv")
                    else:
                        b.op("vector", lambda e, ii=ii: e.tensor_tensor(
                            _col(lall, ii), _col(lall, ii), rs[:], ALU.add), inc="dv")
                    b.w("tensor", "dv", b.c["dv"])
                    pev = None
                    for j in range(4):
                        pev = b.op("tensor", lambda e, j=j: e.transpose(
                            ps_pt[:, j * 128:(j + 1) * 128], pbf[:, j * 128:(j + 1) * 128],
                            P["ident"][:]), inc="pe")
                    b.w("vector", "pe", pev)
                    dvv = b.op("vector", lambda e: e.tensor_copy(ptb[:], ps_pt[:]), inc="dv")
                    b.w("tensor", "dv", dvv)
                    pev = None
                    for j in range(4):
                        pev = b.op("tensor", lambda e, j=j, h=h: e.matmul(
                            ps_av[:], ptb[:, j * 128:(j + 1) * 128],
                            vsl[j][:, h * 256:(h + 1) * 256], start=(j == 0), stop=(j == 3)),
                            inc="pe")
                    b.w("vector", "pe", pev)
                    if fc == 0:
                        b.op("vector", lambda e, ii=ii: e.tensor_copy(
                            Oall[:, ii * 256:(ii + 1) * 256], ps_av[:]), inc="dv")
                        if s == 0 and h == 0 and dsm is not None:
                            b.op("vector", lambda e: e.tensor_copy(dob[:], ps_av[:]))
                    else:
                        b.op("vector", lambda e, ii=ii: e.tensor_tensor(
                            Oall[:, ii * 256:(ii + 1) * 256], Oall[:, ii * 256:(ii + 1) * 256],
                            ps_av[:], ALU.add), inc="dv")
                    b.w("tensor", "dv", b.c["dv"])
        for s in range(NSLOT):
            for h in range(HEADS):
                ii = s * 8 + h
                b.op("vector", lambda e, ii=ii: e.reciprocal(rl[:], _col(lall, ii)))
                b.op("vector", lambda e: e.drain())
                b.op("vector", lambda e, ii=ii: e.tensor_scalar(
                    vvb[:], Oall[:, ii * 256:(ii + 1) * 256], rl[:], None, ALU.mult), inc="dv")
                b.w("tensor", "dv", b.c["dv"])
                for kc in range(2):
                    cc = 2 * h + kc
                    pev = b.op("tensor", lambda e, kc=kc: e.transpose(
                        ps_pt[:, kc * 128:(kc + 1) * 128], vvb[:, kc * 128:(kc + 1) * 128],
                        P["ident"][:]), inc="pe")
                b.w("vector", "pe", b.c["pe"])
                for kc in range(2):
                    cc = 2 * h + kc
                    b.op("vector", lambda e, cc=cc, s=s, kc=kc: e.tensor_copy(
                        qtb[:, cc * TPC + s * 128:cc * TPC + (s + 1) * 128],
                        ps_pt[:, kc * 128:(kc + 1) * 128]), inc="dv")
                b.w("tensor", "dv", b.c["dv"])
        b.bar()
        if cut3 == "fa":
            if "d_vals" in io:
                b.dma("gpsimd", io["d_vals"][:, :], qtb[:])
                b.dma("gpsimd", io["d_smf00"], dsm[:])
                b.dma("gpsimd", io["d_pb00"], dpb[:])
                b.dma("gpsimd", io["d_o00"], dob[:])
                b.dma("gpsimd", io["d_mall"], mall[:])
                b.dma("gpsimd", io["d_lall"], lall[:])
                b.dma("gpsimd", io["d_ksl0"], ksl[:, 0:512])
                b.dma("gpsimd", io["d_vsl0"], vsl[0][:, 0:256])
                b.dma("gpsimd", io["d_q00"], qtb[:, 0:128])
            b.w("gpsimd", "gd", b.c["gd"])
            fin()
            b.w("gpsimd", "ld", b.c["ld"])
            return
        wob = [sb(f"a_wo{i}", [128, NCC * 128], BF16) for i in range(2)]
        bob = sb("a_bob", [128, NDC], F32)
        evo = [sb(f"a_evo{i}", [128, TPC], F32) for i in range(2)]
        ps_o = [pstn(f"a_pso{t}", [128, 512]) for t in range(2)]
        b.dma("sync", bob[:], io["bo_t"])
        gd_w = [0, 0]
        gd_w[0] = b.dma("gpsimd", wob[0][:].rearrange("p (c r) -> p c r", r=128),
                        io["woT"].rearrange("c p (o r) -> p c r o", r=128)[:, :, :, 0])
        pe_oc = 0
        dv_oc = [0, 0]
        for oc in range(NDC):
            if oc + 1 < NDC:
                b.w("gpsimd", "pe", pe_oc)
                gd_w[(oc + 1) % 2] = b.dma(
                    "gpsimd", wob[(oc + 1) % 2][:].rearrange("p (c r) -> p c r", r=128),
                    io["woT"].rearrange("c p (o r) -> p c r o", r=128)[:, :, :, oc + 1])
            b.w("tensor", "gd", gd_w[oc % 2])
            b.w("tensor", "dv", b.c["dv"])
            for t in range(2):
                for k in range(NCC):
                    b.op("tensor", lambda e, t_=t, k=k, oc=oc: e.matmul(
                        ps_o[t_][:], wob[oc % 2][:, k * 128:(k + 1) * 128],
                        qtb[:, k * TPC + t_ * 512:k * TPC + (t_ + 1) * 512],
                        start=(k == 0), stop=(k == NCC - 1)),
                        inc="pe" if k == NCC - 1 else None)
            pe_oc = b.c["pe"]
            b.w("vector", "pe", pe_oc)
            evv = evo[oc % 2]
            for t in range(2):
                dvv = b.op("vector", lambda e, t_=t, oc=oc, ev=evv: e.tensor_scalar(
                    ev[:, t_ * 512:(t_ + 1) * 512], ps_o[t_][:], _col(bob, oc), None, ALU.add),
                    inc="dv")
            dv_oc[oc % 2] = dvv
            b.w("gpsimd", "dv", dvv)
            b.w("gpsimd", "gd", gd_w[oc % 2])
            b.dma("gpsimd", io["ACC"][oc], evv[:], accum_op=ALU.add)
        b.bar()
        if cut3 == "op":
            fin()
            b.w("gpsimd", "ld", b.c["ld"])
            return

    ph2 = ExitStack()
    with ph2:
        sb = lambda nm, sh, dt: ph2.enter_context(nc.sbuf_tensor(nm, sh, dt))
        pstn = lambda nm, sh: ph2.enter_context(nc.psum_tensor(nm, sh, F32))
        xnb2 = sb("w_xnb", [128, NDC * TPC], BF16)
        wwb = [sb(f"w_ww{i}", [128, NDC * 128], BF16) for i in range(2)]
        bcb = sb("w_bcb", [128, DM], BF16)
        bwb = sb("w_bwb", [128, NDC], F32)
        evw = [sb(f"w_evw{i}", [128, TPC], F32) for i in range(2)]
        ps_w = [[pstn(f"w_psw{i}{t}", [128, 512]) for t in range(2)] for i in range(2)]
        xsrc = io["xn1i"] if mode == "L3" else io["xn1d"]
        for dc in range(NDC):
            b.dma("sync", xnb2[:, dc * TPC:(dc + 1) * TPC], xsrc[dc])
        b.dma("sync", bwb[:], io["bW_t"])
        b.dma("gpsimd", bcb[:], io["BcatT"][:, :])
        b.bar(engs=("tensor", "vector", "gpsimd"))
        gd_w = [0, 0]
        gd_w[0] = b.dma("gpsimd", wwb[0][:].rearrange("p (c r) -> p c r", r=128),
                        io["WT"].rearrange("c p (o r) -> p c r o", r=128)[:, :, :, 0])
        pe_oc = 0
        dv_oc = [0, 0]
        for oc in range(NDC):
            if oc + 1 < NDC:
                b.w("gpsimd", "pe", pe_oc)
                gd_w[(oc + 1) % 2] = b.dma(
                    "gpsimd", wwb[(oc + 1) % 2][:].rearrange("p (c r) -> p c r", r=128),
                    io["WT"].rearrange("c p (o r) -> p c r o", r=128)[:, :, :, oc + 1])
            b.w("tensor", "gd", gd_w[oc % 2])
            b.w("tensor", "dv", dv_oc[oc % 2])
            for t in range(2):
                for k in range(NDC):
                    b.op("tensor", lambda e, t_=t, k=k, oc=oc: e.matmul(
                        ps_w[oc % 2][t_][:], wwb[oc % 2][:, k * 128:(k + 1) * 128],
                        xnb2[:, k * TPC + t_ * 512:k * TPC + (t_ + 1) * 512],
                        start=(k == 0), stop=False))
                b.op("tensor", lambda e, t_=t, oc=oc: e.matmul(
                    ps_w[oc % 2][t_][:], bcb[:, oc * 128:(oc + 1) * 128],
                    t1s_g[:, t_ * 512:(t_ + 1) * 512], start=False, stop=True),
                    inc="pe")
            pe_oc = b.c["pe"]
            b.w("vector", "pe", pe_oc)
            evv = evw[oc % 2]
            for t in range(2):
                dvv = b.op("vector", lambda e, t_=t, oc=oc, ev=evv: e.tensor_scalar(
                    ev[:, t_ * 512:(t_ + 1) * 512], ps_w[oc % 2][t_][:], _col(bwb, oc), None, ALU.add),
                    inc="dv")
            dv_oc[oc % 2] = dvv
            b.w("gpsimd", "dv", dvv)
            b.dma("gpsimd", io["ACC"][oc], evv[:], accum_op=ALU.add)
        b.bar()
    fin()



DEBUG = False


def build_program(mode):
    nc = bass.Bass("TRN2", target_bir_lowering=False)
    ctx = ExitStack()
    ET = lambda nm, sh, dt: nc.dram_tensor(nm, sh, dt, kind="ExternalInput").ap()
    OT = lambda nm, sh, dt: nc.dram_tensor(nm, sh, dt, kind="ExternalOutput").ap()
    IT = lambda nm, sh, dt: nc.dram_tensor(nm, sh, dt).ap()

    io = {}
    names = []

    def EI(nm, sh, dt):
        io[nm] = ET(nm, sh, dt)
        names.append(nm)

    EI("x_t", [NDC, 128, TPC], F32)
    for nm, sh, dt in [
        ("A_log_t", [NDC, 128, NST], F32), ("D_t", [128, NDC], F32),
        ("dtb_t", [128, NDC], F32), ("xpwT", [NDC, 128, 160], F32),
        ("dtwT", [NDC, 128, 128], F32),
        ("E32", [128, NDC * 32], BF16), ("E16", [16, 2048], F32),
        ("ident", [128, 128], BF16), ("ones128f", [128, 1], F32),
        ("ones1x128f", [1, 128], F32), ("ones128b", [128, 1], BF16),
        ("o1x128b", [1, 128], BF16), ("onesKM", [128, 32], BF16),
        ("hmsk", [NSLOT, 128, NST * 32], BF16),
    ]:
        EI(nm, sh, dt)
    if mode in ("L1", "ALL"):
        EI("Esg", [32, 4096], BF16)
        EI("Esd", [8, 1024], BF16)
        EI("Edn", [128, NHC * 8], BF16)
        for nm, sh in [("qg", [NHC, 128, NDC * 128]), ("qu", [NHC, 128, NDC * 128]),
                       ("qd", [NDC, 128, NHC * 128])]:
            EI(nm, sh, I32)
        for nm, sh in [("zg", [32, HFF]), ("zu", [32, HFF]), ("zd", [NHC, DM])]:
            EI(nm, sh, I32)
        for nm, sh in [("sg", [32, HFF]), ("su", [32, HFF]), ("sd", [NHC, DM])]:
            EI(nm, sh, F32)
    if mode in ("L2", "ALL"):
        EI("wqkvT", [NDC, 128, 6144], F32)
        EI("bqkv_t", [128, NQKV], F32)
        EI("AcatT", [NDC, 128, 128], F32)
        EI("wvb", [128, TPC], F32)
    if mode in ("L3", "ALL"):
        EI("woT", [NCC, 128, 4096], F32)
        EI("bo_t", [128, NDC], F32)
        EI("WT", [NDC, 128, 4096], F32)
        EI("bW_t", [128, NDC], F32)
        EI("BcatT", [128, DM], F32)
        EI("rWT", [NDC, 128, 8], F32)
        EI("rb_t", [8, 1], F32)
        EI("masks", [NSLOT, 128, 512], F32)
        EI("Erv", [8, 128], F32)

    if mode == "LNDBG":
        io["d_mrow"] = OT("d_mrow", [1, TPC], F32)
        io["d_vrow"] = OT("d_vrow", [1, TPC], F32)
        io["d_mb"] = OT("d_mb", [128, TPC], F32)
        io["d_rstdb"] = OT("d_rstdb", [128, TPC], F32)
    elif mode == "L1":
        io["ACC"] = OT("ACC", [NDC, 128, TPC], F32)
        io["stloc"] = OT("stloc", [DM, 136], F32)
        if DEBUG:
            io["d_xnb"] = OT("d_xnb", [128, NDC * TPC], BF16)
            io["d_sg"] = OT("d_sg", [32, TPC], BF16)
            io["d_bf"] = OT("d_bf", [16, TPC], F32)
            io["d_cf"] = OT("d_cf", [16, TPC], F32)
            io["d_dtin"] = OT("d_dtin", [128, TPC], BF16)
            io["d_bb"] = OT("d_bb", [128, NST * TPC], BF16)
            io["d_wg0"] = OT("d_wg0", [128, NDC * 128], BF16)
            io["d_h0"] = OT("d_h0", [128, TPC], BF16)
            io["d_da00"] = OT("d_da00", [128, TPC], F32)
            io["d_u00"] = OT("d_u00", [128, TPC], BF16)
            io["d_hb00"] = OT("d_hb00", [128, TPC], BF16)
            io["d_dtx0"] = OT("d_dtx0", [128, TPC], BF16)
            io["d_e16"] = OT("d_e16", [16, 2048], F32)
            io["d_ps1"] = OT("d_ps1", [128, 512], F32)
            io["d_ssbg"] = OT("d_ssbg", [32, 1024], BF16)
            io["d_qb0"] = OT("d_qb0", [128, NDC * 128], BF16)
            io["d_sb0"] = OT("d_sb0", [128, 1024], F32)
            io["d_esg_e"] = OT("d_esg_e", [32, 4096], BF16)
            io["d_esg_l"] = OT("d_esg_l", [32, 4096], BF16)
            io["d_wd0"] = OT("d_wd0", [128, 1024], BF16)
            io["d_wd24"] = OT("d_wd24", [128, 1024], BF16)
            io["d_sdb"] = OT("d_sdb", [8, 2048], BF16)
            io["d_szd"] = OT("d_szd", [8, 2048], BF16)
            io["d_shblk"] = OT("d_shblk", [8, TPC], BF16)
            io["d_mrow"] = OT("d_mrow", [1, TPC], F32)
            io["d_vrow"] = OT("d_vrow", [1, TPC], F32)
            io["d_mb"] = OT("d_mb", [128, TPC], F32)
            io["d_rstdb"] = OT("d_rstdb", [128, TPC], F32)
    elif mode == "L2":
        EI("ACCi", [NDC, 128, TPC], F32)
        io["ACC"] = OT("ACC", [NDC, 128, TPC], F32)
        EI("stg", [4 * DM, 136], F32)
        io["xn1o"] = OT("xn1o", [NDC, 128, TPC], BF16)
        io["qto"] = OT("qto", [NCC, 128, TPC], BF16)
        io["kvloc"] = OT("kvloc", [2 * 2048, TPC], BF16)
        io["arpart"] = OT("arpart", [128, NDC], F32)
        io["t1o"] = OT("t1o", [128, TPC], F32)
        import os as _os
        if _os.environ.get("K_DBG2"):
            io["d_afin"] = OT("d_afin", [128, NDC * NST], F32)
            io["d_ones"] = OT("d_ones", [128, 128], F32)
            io["d_dtinf"] = OT("d_dtinf", [128, TPC], BF16)
            io["d_dtbf"] = OT("d_dtbf", [128, NDC], F32)
            io["d_stag20"] = OT("d_stag20", [128, 544], F32)
            io["d_sng0"] = OT("d_sng0", [128, 512], F32)
            io["d_tg0"] = OT("d_tg0", [128, 32], F32)
            io["d_hin0"] = OT("d_hin0", [128, 512], F32)
            io["d_hsel0"] = OT("d_hsel0", [128, 128], F32)
            io["d_cum0"] = OT("d_cum0", [128, TPC], F32)
            io["d_dtf0"] = OT("d_dtf0", [128, TPC], F32)
    elif mode == "L3":
        EI("ACCi", [NDC, 128, TPC], F32)
        io["ACC"] = IT("ACC", [NDC, 128, TPC], F32)
        EI("xn1i", [NDC, 128, TPC], BF16)
        EI("qti", [NCC, 128, TPC], BF16)
        EI("kvg", [4 * DM, TPC], BF16)
        EI("arall", [4 * 128, NDC], F32)
        EI("t1i", [128, TPC], F32)
        io["out_t"] = OT("out_t", [NDC, 128, TPC], F32)
        import os as _os
        if _os.environ.get("K_DBG2"):
            io["d_rvc"] = OT("d_rvc", [128, 1], F32)
            io["d_lg"] = OT("d_lg", [8, 1], F32)
            io["d_l8"] = OT("d_l8", [1, 8], F32)
            io["d_e8"] = OT("d_e8", [1, 8], F32)
            io["d_rcol"] = OT("d_rcol", [8, 1], F32)
            io["d_e8b"] = OT("d_e8b", [1, 8], F32)
            io["d_sm1"] = OT("d_sm1", [1, 1], F32)
            io["d_sm1b"] = OT("d_sm1b", [1, 1], F32)
            io["d_vals"] = OT("d_vals", [128, NCC * TPC], BF16)
            io["d_smf00"] = OT("d_smf00", [128, 512], F32)
            io["d_pb00"] = OT("d_pb00", [128, 512], BF16)
            io["d_o00"] = OT("d_o00", [128, 256], F32)
            io["d_mall"] = OT("d_mall", [128, 64], F32)
            io["d_lall"] = OT("d_lall", [128, 64], F32)
            io["d_ksl0"] = OT("d_ksl0", [128, 512], BF16)
            io["d_vsl0"] = OT("d_vsl0", [128, 256], BF16)
            io["d_q00"] = OT("d_q00", [128, 128], BF16)
    else:
        io["ACC"] = IT("ACC", [NDC, 128, TPC], F32)
        io["stloc"] = IT("stloc", [DM, 136], F32)
        io["stg"] = IT("stg", [4 * DM, 136], F32)
        io["kvloc"] = IT("kvloc", [2 * 2048, TPC], BF16)
        io["kvg"] = IT("kvg", [4 * DM, TPC], BF16)
        io["ar_in"] = IT("ar_in", [128, NDC], F32)
        io["ar_out"] = IT("ar_out", [128, NDC], F32)
        io["xn1d"] = IT("xn1d", [NDC, 128, TPC], BF16)
        io["qtd"] = IT("qtd", [NCC, 128, TPC], BF16)
        io["t1d"] = IT("t1d", [128, TPC], F32)
        io["out_t"] = OT("out_t", [NDC, 128, TPC], F32)
    io["bnc1"] = IT("bnc1", [1, 8], F32)
    io["bnc2"] = IT("bnc2", [8, 1], F32)

    with ctx:
        b = Bld(nc, ctx)
        P = {}
        for nm, sh, dt in [
            ("A", [128, NDC * NST], F32), ("Dg", [128, NDC], F32),
            ("dtb", [128, NDC], F32),
            ("dtw", [128, NDC * 128], BF16),
            ("o128", [128, 1], F32), ("o1x128", [1, 128], F32),
            ("o128b", [128, 1], BF16), ("o1x128b", [1, 128], BF16),
            ("oKM", [128, 32], BF16), ("ident", [128, 128], BF16),
            ("dtin", [128, TPC], BF16), ("Bf", [16, TPC], F32), ("Cf", [16, TPC], F32),
            ("Sg", [32, TPC], BF16), ("ones128t", [128, 128], F32),
        ]:
            P[nm] = b.sb("p_" + nm, sh, dt)

        b.dma("sync", P["o128"][:], io["ones128f"])
        b.dma("sync", P["o1x128"][:], io["ones1x128f"])
        b.dma("sync", P["o128b"][:], io["ones128b"])
        b.dma("sync", P["o1x128b"][:], io["o1x128b"])
        b.dma("sync", P["oKM"][:], io["onesKM"])
        b.dma("sync", P["ident"][:], io["ident"])
        b.dma("sync", P["dtb"][:], io["dtb_t"])
        b.dma("sync", P["Dg"][:], io["D_t"])
        b.dma("sync", P["A"][:].rearrange("p (c r) -> p c r", r=NST), io["A_log_t"].rearrange("c p r -> p c r"))
        b.dma("gpsimd", P["dtw"][:].rearrange("p (c r) -> p c r", r=128), io["dtwT"].rearrange("c p r -> p c r"))
        b.bar(engs=("scalar", "vector", "gpsimd"))
        b.act(lambda e: e.activation(P["A"][:], P["A"][:], AF.Exp))
        b.w("vector", "ac")
        b.op("vector", lambda e: e.tensor_scalar(P["A"][:], P["A"][:], -1.0, None, ALU.mult),
             inc="dv")
        b.op("gpsimd", lambda e: e.memset(P["ones128t"][:], 1.0), inc="pl")
        b.bar(engs=("vector", "scalar", "tensor", "gpsimd"))

        if mode in ("L1", "ALL"):
            b.dma("gpsimd", io["ACC"].rearrange("c p t -> (c p) t"),
                  io["x_t"].rearrange("c p t -> (c p) t"))
        elif mode == "L2":
            b.dma("gpsimd", io["ACC"].rearrange("c p t -> (c p) t"),
                  io["ACCi"].rearrange("c p t -> (c p) t"))
        if mode == "L3":
            b.dma("gpsimd", io["ACC"].rearrange("c p t -> (c p) t"),
                  io["ACCi"].rearrange("c p t -> (c p) t"))
        b.w("gpsimd", "gd", b.c["gd"])
        b.bar(engs=("sync", "vector", "tensor", "scalar"), sems=("gd",))

        if mode == "LNDBG":
            xnb0 = b.sb("xnb0", [128, NDC * TPC], BF16)
            scd = ExitStack()
            with scd:
                mb0 = scd.enter_context(nc.sbuf_tensor("mb0", [128, TPC], F32))
                rstdb0 = scd.enter_context(nc.sbuf_tensor("rstdb0", [128, TPC], F32))
                import os
                _sg = os.environ.get("LNV", "11")
                _ln_phase(b, P, io["x_t"], io, with_sg=_sg[0] == "1", with_xp=_sg[1] == "1",
                          xnb=xnb0 if _sg != "00" else None, mb=mb0, rstdb=rstdb0)
            b.w("gpsimd", "gd", b.c["gd"])
            b.w("gpsimd", "ld", b.c["ld"])
            b.finalize()
            return nc, names
        if mode in ("L1", "ALL"):
            sc0x = ExitStack()
            with sc0x:
                xnb0 = sc0x.enter_context(nc.sbuf_tensor("xnb0", [128, NDC * TPC], BF16))
                sc1 = ExitStack()
                with sc1:
                    mb0 = sc1.enter_context(nc.sbuf_tensor("mb0", [128, TPC], F32))
                    rstdb0 = sc1.enter_context(nc.sbuf_tensor("rstdb0", [128, TPC], F32))
                    _ln_phase(b, P, io["x_t"], io, with_sg=True, with_xp=True, xnb=xnb0,
                              mb=mb0, rstdb=rstdb0)
                    Bb = sc1.enter_context(nc.sbuf_tensor("m_Bb", [128, NST * TPC], BF16))
                    Cb = sc1.enter_context(nc.sbuf_tensor("m_Cb", [128, NST * TPC], BF16))
                    _build_bbcb(b, P, Bb, Cb, io["E16"],
                                dbg_io=io if (DEBUG and mode == "L1") else None)
                    if DEBUG and mode == "L1":
                        b.dma("gpsimd", io["d_xnb"][:, :], xnb0[:])
                        b.dma("gpsimd", io["d_sg"][:, :], P["Sg"][:])
                        b.dma("gpsimd", io["d_bf"][:, :], P["Bf"][:])
                        b.dma("gpsimd", io["d_cf"][:, :], P["Cf"][:])
                        b.dma("gpsimd", io["d_dtin"][:, :], P["dtin"][:])
                        b.dma("gpsimd", io["d_bb"][:, :], Bb[:])
                    _mamba_pass1(b, io, P, Bb, Cb, mb0, rstdb0)
                _ffn0(b, io, P, xnb0)
                b.bar()

        if mode == "L1":
            b.w("gpsimd", "gd", b.c["gd"])
            b.w("gpsimd", "ld", b.c["ld"])
            b.finalize()
            return nc, names

        if mode == "ALL":
            b.w("gpsimd", "gd", b.c["gd"])
            b.op("gpsimd", lambda e: e.collective_compute(
                "AllGather", ALU.bypass, replica_groups=[[0, 1, 2, 3], [4, 5, 6, 7]],
                ins=[io["stloc"].opt()], outs=[io["stg"].opt()]), inc="cc", drain=False)
            b.w("gpsimd", "cc")
            b.bar(engs=("sync", "vector", "tensor", "scalar"), sems=("cc",))

        import os
        cut = os.environ.get("K_L2CUT", "") if mode == "L2" else ""
        if mode == "L2":
            sc0 = ExitStack()
            with sc0:
                mbx = sc0.enter_context(nc.sbuf_tensor("mbx", [128, TPC], F32))
                rstdbx = sc0.enter_context(nc.sbuf_tensor("rstdbx", [128, TPC], F32))
                _ln_phase(b, P, io["x_t"], io, with_sg=False, with_xp=True, xnb=None,
                          mb=mbx, rstdb=rstdbx)
        if mode in ("L2", "ALL") and cut != "ln0":
            sc2 = ExitStack()
            with sc2:
                Bb = sc2.enter_context(nc.sbuf_tensor("c_Bb", [128, NST * TPC], BF16))
                Cb = sc2.enter_context(nc.sbuf_tensor("c_Cb", [128, NST * TPC], BF16))
                _build_bbcb(b, P, Bb, Cb, io["E16"])
                _combine_fix(b, io, P, Bb, Cb)

        if cut not in ("ln0", "fix"):
            _l1_phases(b, io, P, mode)
        if mode == "L2":
            if "d_afin" in io:
                b.dma("gpsimd", io["d_afin"][:, :], P["A"][:])
                b.dma("gpsimd", io["d_ones"][:, :], P["ones128t"][:])
                b.dma("gpsimd", io["d_dtinf"][:, :], P["dtin"][:])
                b.dma("gpsimd", io["d_dtbf"][:, :], P["dtb"][:])
            b.w("gpsimd", "gd", b.c["gd"])
            b.w("gpsimd", "ld", b.c["ld"])
        b.finalize()
        return nc, names



_CACHE = {}
FUSED = True


def _get(mode):
    if mode not in _CACHE:
        _CACHE[mode] = build_program(mode)
    return _CACHE[mode]


def _run(mode, shared, cores, extra_core=None):
    nc, names = _get(mode)
    in_maps = []
    for c in range(8):
        m = {}
        for nm in names:
            if nm in shared:
                m[nm] = shared[nm]
            elif nm in cores[c]:
                m[nm] = cores[c][nm]
            elif extra_core is not None and nm in extra_core[c]:
                m[nm] = extra_core[c][nm]
            else:
                raise KeyError(f"missing input {nm} for mode {mode}")
        in_maps.append(m)
    res = run_bass_kernel_spmd(nc, in_maps, core_ids=list(range(8)))
    return res.results


def kernel(**inputs):
    shared = _prep_shared(inputs)
    cores = [_prep_core(inputs, c) for c in range(8)]
    if FUSED:
        res = _run("ALL", shared, cores)
        outs = [np.asarray(r["out_t"], dtype=np.float32) for r in res]
    else:
        r1 = _run("L1", shared, cores)
        ex2 = []
        for c in range(8):
            bb = c // 4
            stg = np.concatenate([np.asarray(r1[bb * 4 + qq]["stloc"]) for qq in range(4)], 0)
            ex2.append({"ACCi": np.asarray(r1[c]["ACC"]), "stg": stg})
        r2 = _run("L2", shared, cores, ex2)
        ex3 = []
        for c in range(8):
            bb = c // 4
            kvg = np.concatenate([np.asarray(r2[bb * 4 + qq]["kvloc"]) for qq in range(4)], 0)
            arall = np.concatenate([np.asarray(r2[bb * 4 + qq]["arpart"]) for qq in range(4)], 0)
            ex3.append({
                "ACCi": np.asarray(r2[c]["ACC"]), "xn1i": np.asarray(r2[c]["xn1o"]),
                "qti": np.asarray(r2[c]["qto"]), "kvg": kvg, "arall": arall,
                "t1i": np.asarray(r2[c]["t1o"]),
            })
        r3 = _run("L3", shared, cores, ex3)
        outs = [np.asarray(r["out_t"], dtype=np.float32) for r in r3]
    out = np.zeros((NBATCH, SEQ, DM), np.float32)
    for c in range(8):
        bb, q = c // 4, c % 4
        o = outs[c].reshape(DM, TPC).T
        for s in range(NSLOT):
            g = GQ[q][s]
            out[bb, g * TILE:(g + 1) * TILE, :] = o[s * TILE:(s + 1) * TILE, :]
    return out


# version 04 (final)
# speedup vs baseline: 1.1171x; 1.1015x over previous
import numpy as np
import ml_dtypes
from contextlib import ExitStack

import concourse.bass as bass
import concourse.mybir as mybir
from concourse.bass_utils import run_bass_kernel_spmd

F32 = mybir.dt.float32
BF16 = mybir.dt.bfloat16
I32 = mybir.dt.int32
AF = mybir.ActivationFunctionType
ALU = mybir.AluOpType
BF = ml_dtypes.bfloat16
AX = mybir.AxisListType.X

NBATCH = 2
SEQ = 4096
DM = 4096
NDC = 32
TPC = 1024
NSLOT = 8
TILE = 128
NST = 16
DTR = 128
HFF = 11008
NHC = 86
EPS = 1e-5
HEADS = 8
NCC = 16
NQKV = 48
DN_BLKS = [(i * 8, min(8, NHC - i * 8)) for i in range(11)]

GQ = [[q, 7 - q, 8 + q, 15 - q, 16 + q, 23 - q, 24 + q, 31 - q] for q in range(4)]
G2QS = {}
for _q in range(4):
    for _s in range(NSLOT):
        G2QS[GQ[_q][_s]] = (_q, _s)

R_INV = float(np.float32(1.0) / np.float32(3484.0))
Q_START, Q_END = 611, 4095



def _prep_shared(inp):
    d = {}
    f32 = lambda a: np.ascontiguousarray(np.asarray(a), dtype=np.float32)
    d["A_log_t"] = f32(inp["L0_attn_A_log"]).reshape(NDC, 128, NST)
    d["D_t"] = np.ascontiguousarray(f32(inp["L0_attn_D"]).reshape(NDC, 128).T)
    d["dtb_t"] = np.ascontiguousarray(f32(inp["L0_attn_dt_proj_b"]).reshape(NDC, 128).T)
    d["xpwT"] = np.ascontiguousarray(f32(inp["L0_attn_x_proj_w"]).T).reshape(NDC, 128, 160)
    dtw = f32(inp["L0_attn_dt_proj_w"])
    d["dtwT"] = np.ascontiguousarray(dtw.T.reshape(DTR, NDC, 128).transpose(1, 0, 2))
    for nm, key in (("qg", "L0_ffn_qw_gate"), ("qu", "L0_ffn_qw_up")):
        qw = np.ascontiguousarray(np.asarray(inp[key]), dtype=np.int32)
        d[nm] = np.ascontiguousarray(
            qw.reshape(NDC, 128, NHC, 128).transpose(2, 1, 0, 3).reshape(NHC, 128, NDC * 128))
    qd = np.ascontiguousarray(np.asarray(inp["L0_ffn_qw_down"]), dtype=np.int32)
    d["qd"] = np.ascontiguousarray(
        qd.reshape(NHC, 128, NDC, 128).transpose(2, 1, 0, 3).reshape(NDC, 128, NHC * 128))
    for nm, key in (("zg", "L0_ffn_z_gate"), ("zu", "L0_ffn_z_up"), ("zd", "L0_ffn_z_down")):
        d[nm] = np.ascontiguousarray(np.asarray(inp[key]), dtype=np.int32)
    for nm, key in (("sg", "L0_ffn_s_gate"), ("su", "L0_ffn_s_up"), ("sd", "L0_ffn_s_down")):
        d[nm] = f32(inp[key])
    d["wqkvT"] = np.ascontiguousarray(f32(inp["L1_attn_w_qkv"]).T).reshape(NDC, 128, 6144)
    d["bqkv_t"] = np.ascontiguousarray(f32(inp["L1_attn_b_qkv"]).reshape(NQKV, 128).T)
    d["woT"] = np.ascontiguousarray(f32(inp["L1_attn_w_o"]).T).reshape(NCC, 128, 4096)
    d["bo_t"] = np.ascontiguousarray(f32(inp["L1_attn_b_o"]).reshape(NDC, 128).T)
    d["WT"] = np.ascontiguousarray(f32(inp["L1_ffn_W"]).T).reshape(NDC, 128, 4096)
    d["bW_t"] = np.ascontiguousarray(f32(inp["L1_ffn_b"]).reshape(NDC, 128).T)
    d["AcatT"] = np.ascontiguousarray(f32(inp["L1_ffn_A"]).reshape(128, DM).T).reshape(NDC, 128, 128)
    d["BcatT"] = np.ascontiguousarray(f32(inp["L1_ffn_B"]).transpose(0, 2, 1).reshape(128, DM))
    d["rWT"] = np.ascontiguousarray(f32(inp["L1_ffn_router_W"]).T).reshape(NDC, 128, 8)
    d["rb_t"] = f32(inp["L1_ffn_router_b"]).reshape(8, 1)
    e32 = np.zeros((128, NDC, 32), np.float32)
    for k in range(NDC):
        e32[:, k, k] = 1.0
    d["E32"] = e32.reshape(128, NDC * 32).astype(BF)
    e16 = np.zeros((16, 16, 128), np.float32)
    for n in range(16):
        e16[n, n, :] = 1.0
    d["E16"] = np.ascontiguousarray(e16.reshape(16, 16 * 128))
    esg = np.zeros((32, 32, 128), np.float32)
    for g in range(32):
        esg[g, g, :] = 1.0
    d["Esg"] = esg.reshape(32, 32 * 128).astype(BF)
    esd = np.zeros((8, 8, 128), np.float32)
    for g in range(8):
        esd[g, g, :] = 1.0
    d["Esd"] = esd.reshape(8, 8 * 128).astype(BF)
    edn = np.zeros((128, NHC, 8), np.float32)
    for k0, bs in DN_BLKS:
        for j in range(bs):
            edn[:, k0 + j, j] = 1.0
    d["Edn"] = edn.reshape(128, NHC * 8).astype(BF)
    erv = np.zeros((8, 128), np.float32)
    for p in range(128):
        erv[p // 16, p] = 1.0
    d["Erv"] = erv
    d["ident"] = np.eye(128).astype(BF)
    d["ones128f"] = np.ones((128, 1), np.float32)
    d["ones1x128f"] = np.ones((1, 128), np.float32)
    d["ones128b"] = np.ones((128, 1), BF)
    d["o1x128b"] = np.ones((1, 128), BF)
    d["onesKM"] = np.ones((128, 32), BF)
    return d


def _prep_core(inp, c):
    b_, q = c // 4, c % 4
    d = {}
    x = np.asarray(inp["L0_attn_x"], dtype=np.float32)
    rows = np.concatenate([x[b_, g * TILE:(g + 1) * TILE, :] for g in GQ[q]], 0)
    d["x_t"] = np.ascontiguousarray(rows.T).reshape(NDC, 128, TPC)
    m = np.zeros((NSLOT, 128, 512), np.float32)
    for s in range(NSLOT):
        g = GQ[q][s]
        keys = s * 512 + np.arange(512)[None, :]
        qpos = g * TILE + np.arange(128)[:, None]
        m[s] = np.where(keys <= qpos, 0.0, -10000.0)
    d["masks"] = m
    wv = np.zeros((TPC,), np.float32)
    for s in range(NSLOT):
        g = GQ[q][s]
        pos = g * TILE + np.arange(TILE)
        wv[s * TILE:(s + 1) * TILE] = np.where((pos >= Q_START) & (pos < Q_END), 1.0, 0.0)
    d["wvb"] = np.ascontiguousarray(np.broadcast_to(wv[None, :], (128, TPC)), dtype=np.float32)
    msk = np.zeros((NSLOT, 128, NST * 32), np.float32)
    for s in range(NSLOT):
        g = GQ[q][s]
        for n in range(NST):
            msk[s, :, n * 32 + g] = 1.0
    d["hmsk"] = msk.astype(BF)
    return d



ENGS = ("sync", "tensor", "vector", "scalar", "gpsimd")
SEMS = ("ld", "gd", "pe", "dv", "ac", "pl", "cc", "so")


class Bld:
    def __init__(self, nc, ctx):
        self.nc = nc
        self.ctx = ctx
        self.ops = {e: [] for e in ENGS}
        self.drain = None
        self.sh = {}
        self.c = {}
        for s in SEMS:
            self.sh[s] = ctx.enter_context(nc.semaphore(s))
            self.c[s] = 0

    def sb(self, name, shape, dt):
        return self.ctx.enter_context(self.nc.sbuf_tensor(name, shape, dt))

    def ps(self, name, shape, dt=F32):
        return self.ctx.enter_context(self.nc.psum_tensor(name, shape, dt))

    def act(self, f, inc="ac", n=1):
        return self.op("scalar", f, inc=inc, n=n)

    def op(self, eng, f, inc=None, n=1, drain=True):
        if inc is None:
            self.ops[eng].append(f)
            return None
        h = self.sh[inc]
        self.c[inc] += n
        if drain and eng in ("vector", "scalar", "gpsimd"):
            self.ops[eng].append(f)
            self.ops[eng].append(lambda e, h=h, n=n: e.drain().then_inc(h, n))
        else:
            def g(e, f=f, h=h, n=n):
                f(e).then_inc(h, n)
            self.ops[eng].append(g)
        return self.c[inc]

    def w(self, eng, sem, val=None):
        v = self.c[sem] if val is None else val
        if v <= 0:
            return
        h = self.sh[sem]
        self.ops[eng].append(lambda e, h=h, v=v: e.wait_ge(h, v))

    def bar(self, engs=ENGS, sems=SEMS):
        snap = {s: self.c[s] for s in sems}
        for eng in engs:
            for s, v in snap.items():
                if v > 0:
                    self.w(eng, s, v)

    def dma(self, eng, out, in_, sem=None, **kw):
        if sem is None:
            sem = "gd" if eng == "gpsimd" else "ld"
        return self.op(eng, lambda e, o=out, i=in_, kw=kw: e.dma_start(out=o, in_=i, **kw),
                       inc=sem, n=16, drain=False)

    def finalize(self):
        block = self.ctx.enter_context(self.nc.Block())
        for eng in ENGS:
            lst = self.ops[eng]

            def runner(e, lst=lst):
                for f in lst:
                    f(e)
            getattr(block, eng)(runner)


def _col(t, j):
    return t[:, j:j + 1]



_LN_SEQ = [0]


def _ln_phase(b, P, src, io, with_sg, with_xp, xnb, mb, rstdb):
    _LN_SEQ[0] += 1
    tag = f"{_LN_SEQ[0]}_"
    ph = ExitStack()
    with ph:
        sb = lambda nm, sh, dt: ph.enter_context(b.nc.sbuf_tensor(tag + nm, sh, dt))
        pstn = lambda nm, sh: ph.enter_context(b.nc.psum_tensor(tag + nm, sh, F32))
        if with_xp:
            xpw = sb("ln_xpw", [128, NDC * 160], BF16)
            b.dma("gpsimd", xpw[:].rearrange("p (c r) -> p c r", r=160),
                  io["xpwT"].rearrange("c p r -> p c r"))
        E32s = sb("ln_E32", [128, NDC * 32], BF16)
        b.dma("sync", E32s[:], io["E32"])
        xch = sb("ln_xch", [128, TPC], F32)
        sqb = sb("ln_sqb", [128, TPC], F32)
        hhb = sb("ln_hhb", [128, TPC], BF16)
        hlb = sb("ln_hlb", [128, TPC], BF16)
        shb = sb("ln_shb", [128, TPC], BF16)
        slb = sb("ln_slb", [128, TPC], BF16)
        rsc = sb("ln_rsc", [128, TPC], F32)
        mrow = sb("ln_mrow", [1, TPC], F32)
        vrow = sb("ln_vrow", [1, TPC], F32)
        srow = sb("ln_srow", [1, TPC], F32)
        xnw = sb("ln_xnw", [128, TPC], F32)
        xtr = sb("ln_xtr", [128, TPC], BF16)
        GS = sb("ln_GS", [32, TPC], F32)
        GQ = sb("ln_GQ", [32, TPC], F32)
        phA = ExitStack()
        ps_sum = [phA.enter_context(b.nc.psum_tensor(f"{tag}ln_ps_sum{t}", [32, 512], F32)) for t in range(2)]
        ps_ssq = [phA.enter_context(b.nc.psum_tensor(f"{tag}ln_ps_ssq{t}", [32, 512], F32)) for t in range(2)]
        ps_b1 = phA.enter_context(b.nc.psum_tensor(tag + "ln_ps_b1", [128, 512], F32))
        ps_b2 = phA.enter_context(b.nc.psum_tensor(tag + "ln_ps_b2", [128, 512], F32))

        for dc in range(NDC):
            if dc > 0:
                b.w("sync", "pe", b.c["pe"])
            ldv = b.dma("sync", xch[:], src[dc])
            b.w("scalar", "ld", ldv)
            b.w("scalar", "pe", b.c["pe"])
            acv = b.act(lambda e: e.activation(sqb[:], xch[:], AF.Square))
            b.w("vector", "ld", ldv)
            b.w("vector", "pe", b.c["pe"])
            b.op("vector", lambda e: e.tensor_copy(hhb[:], xch[:]))
            b.op("vector", lambda e: e.tensor_tensor(rsc[:], xch[:], hhb[:], ALU.subtract))
            dvv = b.op("vector", lambda e: e.tensor_copy(hlb[:], rsc[:]), inc="dv")
            b.w("vector", "ac", acv)
            b.op("vector", lambda e: e.tensor_copy(shb[:], sqb[:]))
            b.op("vector", lambda e: e.tensor_tensor(rsc[:], sqb[:], shb[:], ALU.subtract))
            dv2 = b.op("vector", lambda e: e.tensor_copy(slb[:], rsc[:]), inc="dv")
            b.w("tensor", "dv", dv2)
            for t in range(2):
                b.op("tensor", lambda e, t_=t, dc=dc: e.matmul(
                    ps_sum[t_][:], E32s[:, dc * 32:(dc + 1) * 32], hhb[:, t_ * 512:(t_ + 1) * 512],
                    start=(dc == 0), stop=False))
                b.op("tensor", lambda e, t_=t, dc=dc: e.matmul(
                    ps_sum[t_][:], E32s[:, dc * 32:(dc + 1) * 32], hlb[:, t_ * 512:(t_ + 1) * 512],
                    start=False, stop=(dc == NDC - 1)))
            for t in range(2):
                b.op("tensor", lambda e, t_=t, dc=dc: e.matmul(
                    ps_ssq[t_][:], E32s[:, dc * 32:(dc + 1) * 32], shb[:, t_ * 512:(t_ + 1) * 512],
                    start=(dc == 0), stop=False))
                b.op("tensor", lambda e, t_=t, dc=dc: e.matmul(
                    ps_ssq[t_][:], E32s[:, dc * 32:(dc + 1) * 32], slb[:, t_ * 512:(t_ + 1) * 512],
                    start=False, stop=(dc == NDC - 1)),
                    inc="pe" if t == 1 else None)
        b.op("tensor", lambda e: e.drain())
        b.w("vector", "pe")
        for t in range(2):
            b.op("vector", lambda e, t_=t: e.tensor_copy(GS[:, t_ * 512:(t_ + 1) * 512], ps_sum[t_][:]))
            b.op("vector", lambda e, t_=t: e.tensor_copy(GQ[:, t_ * 512:(t_ + 1) * 512], ps_ssq[t_][:]),
                 inc="dv" if t == 1 else None)
        b.w("gpsimd", "dv", b.c["dv"])
        b.op("gpsimd", lambda e: e.reduce_sum(mrow[:], GS[:], axis=mybir.AxisListType.C))
        b.op("gpsimd", lambda e: e.reduce_sum(vrow[:], GQ[:], axis=mybir.AxisListType.C), inc="pl")
        b.w("vector", "pl", b.c["pl"])
        dvv = b.op("vector", lambda e: e.tensor_scalar(mrow[:], mrow[:], 1.0 / DM, None, ALU.mult), inc="dv")
        b.op("vector", lambda e: e.tensor_scalar(vrow[:], vrow[:], 1.0 / DM, None, ALU.mult))
        import os
        if os.environ.get("LN_EARLYTAP") and "d_mrow" in io:
            b.w("gpsimd", "dv", dvv)
            b.dma("gpsimd", io["d_mrow"][:, :], mrow[:])
            b.w("vector", "gd", b.c["gd"])
        b.op("vector", lambda e: e.tensor_tensor(srow[:], mrow[:], mrow[:], ALU.mult))
        b.op("vector", lambda e: e.tensor_tensor(vrow[:], vrow[:], srow[:], ALU.subtract), inc="dv")
        b.op("vector", lambda e: e.tensor_scalar(vrow[:], vrow[:], float(EPS), None, ALU.add),
             inc="dv")
        b.w("scalar", "dv")
        acv0 = b.act(lambda e: e.activation(vrow[:], vrow[:], AF.Ln))
        b.w("vector", "ac", acv0)
        b.op("vector", lambda e: e.tensor_scalar(vrow[:], vrow[:], -0.5, None, ALU.mult), inc="dv")
        b.w("scalar", "dv")
        b.act(lambda e: e.activation(vrow[:], vrow[:], AF.Exp))
        mhi = srow
        b.w("vector", "ac")
        b.op("vector", lambda e: e.tensor_copy(hhb[0:1, 0:TPC], mrow[:]))
        b.op("vector", lambda e: e.tensor_tensor(mhi[:], mrow[:], hhb[0:1, 0:TPC], ALU.subtract))
        b.op("vector", lambda e: e.tensor_copy(hlb[0:1, 0:TPC], mhi[:]))
        b.op("vector", lambda e: e.tensor_copy(shb[0:1, 0:TPC], vrow[:]))
        b.op("vector", lambda e: e.tensor_tensor(mhi[:], vrow[:], shb[0:1, 0:TPC], ALU.subtract))
        b.op("vector", lambda e: e.tensor_copy(slb[0:1, 0:TPC], mhi[:]), inc="dv")
        b.w("tensor", "dv")
        for t in range(2):
            b.op("tensor", lambda e, t_=t: e.matmul(
                ps_b1[:], P["o1x128b"][:], hhb[0:1, t_ * 512:(t_ + 1) * 512], start=True, stop=False))
            b.op("tensor", lambda e, t_=t: e.matmul(
                ps_b1[:], P["o1x128b"][:], hlb[0:1, t_ * 512:(t_ + 1) * 512], start=False, stop=True))
            b.op("tensor", lambda e, t_=t: e.matmul(
                ps_b2[:], P["o1x128b"][:], shb[0:1, t_ * 512:(t_ + 1) * 512], start=True, stop=False))
            b.op("tensor", lambda e, t_=t: e.matmul(
                ps_b2[:], P["o1x128b"][:], slb[0:1, t_ * 512:(t_ + 1) * 512], start=False, stop=True),
                inc="pe")
            b.w("vector", "pe")
            b.op("vector", lambda e, t_=t: e.tensor_copy(mb[:, t_ * 512:(t_ + 1) * 512], ps_b1[:]))
            dvv = b.op("vector", lambda e, t_=t: e.tensor_copy(rstdb[:, t_ * 512:(t_ + 1) * 512], ps_b2[:]), inc="dv")
            b.w("tensor", "dv", dvv)
        b.op("tensor", lambda e: e.drain())
        if "d_mrow" in io:
            b.w("gpsimd", "dv", b.c["dv"])
            b.dma("gpsimd", io["d_mrow"][:, :], mrow[:])
            b.dma("gpsimd", io["d_vrow"][:, :], vrow[:])
            b.dma("gpsimd", io["d_mb"][:, :], mb[:])
            b.dma("gpsimd", io["d_rstdb"][:, :], rstdb[:])
            b.w("vector", "gd", b.c["gd"])
        if True:
            pass

        import os
        if os.environ.get("LN_PASSA_ONLY"):
            if "d_mrow" in io:
                b.w("gpsimd", "dv", b.c["dv"])
                b.dma("gpsimd", io["d_mrow"][:, :], mrow[:])
                b.dma("gpsimd", io["d_vrow"][:, :], vrow[:])
            b.bar()
            return
        phA.close()
        ps_sg = [pstn(f"ln_ps_sg{t}", [32, 512]) for t in range(2)] if with_sg else None
        if with_xp:
            ps_xp1 = [pstn(f"ln_ps_xp1{t}", [128, 512]) for t in range(2)]
            ps_B = [pstn(f"ln_ps_B{t}", [16, 512]) for t in range(2)]
            ps_C = [pstn(f"ln_ps_C{t}", [16, 512]) for t in range(2)]

        pe_ck = []
        dv_ck = b.c["dv"]
        for dc in range(NDC):
            b.w("sync", "dv", dv_ck)
            ldv = b.dma("sync", xch[:], src[dc])
            b.w("vector", "ld", ldv)
            if dc > 0 and pe_ck:
                b.w("vector", "pe", pe_ck[-1])
            b.op("vector", lambda e: e.tensor_tensor(xnw[:], xch[:], mb[:], ALU.subtract))
            xnd = xnb[:, dc * TPC:(dc + 1) * TPC] if xnb is not None else xtr[:]
            dvv = b.op("vector", lambda e, o=xnd: e.tensor_tensor(o, xnw[:], rstdb[:], ALU.mult),
                       inc="dv")
            dv_ck = dvv
            b.w("tensor", "dv", dvv)
            last = None
            for t in range(2):
                sl = (slice(None), slice(t * 512, (t + 1) * 512))
                if with_sg:
                    last = b.op("tensor", lambda e, t_=t, dc=dc, x=xnd: e.matmul(
                        ps_sg[t_][:], E32s[:, dc * 32:(dc + 1) * 32], x[:, t_ * 512:(t_ + 1) * 512],
                        start=(dc == 0), stop=(dc == NDC - 1)), inc="pe")
                if with_xp:
                    b.op("tensor", lambda e, t_=t, dc=dc, x=xnd: e.matmul(
                        ps_xp1[t_][:], xpw[:, dc * 160:dc * 160 + 128],
                        x[:, t_ * 512:(t_ + 1) * 512], start=(dc == 0), stop=(dc == NDC - 1)))
                    b.op("tensor", lambda e, t_=t, dc=dc, x=xnd: e.matmul(
                        ps_B[t_][:], xpw[:, dc * 160 + 128:dc * 160 + 144],
                        x[:, t_ * 512:(t_ + 1) * 512], start=(dc == 0), stop=(dc == NDC - 1)))
                    last = b.op("tensor", lambda e, t_=t, dc=dc, x=xnd: e.matmul(
                        ps_C[t_][:], xpw[:, dc * 160 + 144:dc * 160 + 160],
                        x[:, t_ * 512:(t_ + 1) * 512], start=(dc == 0), stop=(dc == NDC - 1)),
                        inc="pe")
            pe_ck.append(b.c["pe"])
        b.w("vector", "pe")
        for t in range(2):
            sl = slice(t * 512, (t + 1) * 512)
            if with_xp:
                b.op("vector", lambda e, t_=t, sl=sl: e.tensor_copy(P["dtin"][:, sl], ps_xp1[t_][:]))
                b.op("vector", lambda e, t_=t, sl=sl: e.tensor_copy(P["Bf"][:, sl], ps_B[t_][:]))
                b.op("vector", lambda e, t_=t, sl=sl: e.tensor_copy(P["Cf"][:, sl], ps_C[t_][:]))
            if with_sg:
                b.op("vector", lambda e, t_=t, sl=sl: e.tensor_copy(P["Sg"][:, sl], ps_sg[t_][:]))
        b.op("vector", lambda e: e.tensor_copy(srow[:], mrow[:]), inc="dv")
        b.bar()


def _build_bbcb(b, P, Bb, Cb, io_E16, dbg_io=None):
    _LN_SEQ[0] += 1
    tag = f"{_LN_SEQ[0]}_"
    ph = ExitStack()
    with ph:
        E16s = ph.enter_context(b.nc.sbuf_tensor(tag + "bb_E16", [16, 2048], F32))
        dbg_ps = ph.enter_context(b.nc.sbuf_tensor(tag + "bb_dbgps", [128, 512], F32)) if dbg_io is not None else None
        b.dma("sync", E16s[:], io_E16)
        b.bar(engs=("tensor",), sems=("ld",))
        ps1 = ph.enter_context(b.nc.psum_tensor(tag + "bb_ps1", [128, 512], F32))
        ps2 = ph.enter_context(b.nc.psum_tensor(tag + "bb_ps2", [128, 512], F32))
        for n in range(NST):
            for t in range(2):
                b.op("tensor", lambda e, n=n, t_=t: e.matmul(
                    ps1[:], E16s[:, n * 128:(n + 1) * 128],
                    P["Bf"][:, t_ * 512:(t_ + 1) * 512], start=True, stop=True))
                b.op("tensor", lambda e, n=n, t_=t: e.matmul(
                    ps2[:], E16s[:, n * 128:(n + 1) * 128],
                    P["Cf"][:, t_ * 512:(t_ + 1) * 512], start=True, stop=True), inc="pe")
                b.w("vector", "pe")
                o1 = Bb[:, n * TPC + t * 512:n * TPC + (t + 1) * 512]
                o2 = Cb[:, n * TPC + t * 512:n * TPC + (t + 1) * 512]
                b.op("vector", lambda e, o=o1: e.tensor_copy(o, ps1[:]))
                dvv = b.op("vector", lambda e, o=o2: e.tensor_copy(o, ps2[:]), inc="dv")
                b.w("tensor", "dv", dvv)
        b.bar()


def _dt_chunk(b, P, dc, ps_dt, dtf, tA, tB):
    b.w("tensor", "dv", b.c["dv"])
    pev = None
    for t in range(2):
        pev = b.op("tensor", lambda e, t_=t, dc=dc: e.matmul(
            ps_dt[t_][:], P["dtw"][:, dc * 128:(dc + 1) * 128],
            P["dtin"][:, t_ * 512:(t_ + 1) * 512], start=True, stop=True), inc="pe")
    b.w("vector", "pe", pev)
    evv = None
    for t in range(2):
        evv = b.op("vector", lambda e, t_=t, dc=dc: e.tensor_scalar(
            dtf[:, t_ * 512:(t_ + 1) * 512], ps_dt[t_][:], _col(P["dtb"], dc), None, ALU.add),
            inc="dv")
    b.op("vector", lambda e: e.tensor_scalar(tA[:], dtf[:], -1.0, None, ALU.mult))
    dvv = b.op("vector", lambda e: e.tensor_tensor(tA[:], dtf[:], tA[:], ALU.min), inc="dv")
    b.w("scalar", "dv", dvv)
    b.op("scalar", lambda e: e.activation(tA[:], tA[:], AF.Exp))
    acv = b.act(lambda e: e.activation(tA[:], tA[:], AF.Ln, bias=1.0))
    b.op("vector", lambda e: e.tensor_scalar(tB[:], dtf[:], 0.0, None, ALU.max))
    b.w("vector", "ac", acv)
    b.op("vector", lambda e: e.tensor_tensor(dtf[:], tB[:], tA[:], ALU.add))
    return evv


def _mamba_pass1(b, io, P, Bb, Cb, mb, rstdb):
    ph = ExitStack()
    with ph:
        sb = lambda nm, sh, dt: ph.enter_context(b.nc.sbuf_tensor(nm, sh, dt))
        xch = sb("m1_xch", [128, TPC], F32)
        dtf = sb("m1_dtf", [128, TPC], F32)
        tA = sb("m1_tA", [128, TPC], F32)
        dtxb = sb("m1_dtxb", [128, TPC], BF16)
        dAf = sb("m1_dAf", [128, TPC], F32)
        ubf = sb("m1_ubf", [128, TPC], BF16)
        hbf = sb("m1_hbf", [128, TPC], BF16)
        ybf = sb("m1_ybf", [128, TPC], F32)
        ych = [sb(f"m1_ych{i}", [128, TPC], F32) for i in range(2)]
        ssb = [sb(f"m1_ssb{i}", [128, 136], F32) for i in range(2)]
        ps_dt = [ph.enter_context(b.nc.psum_tensor(f"m1_psdt{t}", [128, 512], F32)) for t in range(2)]
        gd_ck = [0, 0]

        for dc in range(NDC):
            yc = ych[dc % 2]
            sc = ssb[dc % 2]
            if dc > 0:
                b.w("sync", "dv", dv_xch)
            ldv = b.dma("sync", xch[:], io["x_t"][dc])
            b.w("vector", "ld", ldv)
            b.w("vector", "gd", gd_ck[dc % 2])
            b.op("vector", lambda e: e.tensor_tensor(xch[:], xch[:], mb[:], ALU.subtract))
            b.op("vector", lambda e: e.tensor_tensor(xch[:], xch[:], rstdb[:], ALU.mult),
                 inc="dv")
            b.op("vector", lambda e, yc=yc, dc=dc: e.tensor_scalar(
                yc[:], xch[:], _col(P["Dg"], dc), None, ALU.mult))
            _dt_chunk(b, P, dc, ps_dt, dtf, tA, dAf)
            b.op("vector", lambda e, sc=sc: e.tensor_reduce(
                sc[:].rearrange("p (s c) -> p s c", c=17)[:, :, 16],
                dtf[:].rearrange("p (s t) -> p s t", t=TILE),
                axis=AX, op=ALU.add))
            dv_xch = b.op("vector", lambda e: e.tensor_tensor(dtxb[:], dtf[:], xch[:], ALU.mult),
                          inc="dv")
            for n in range(NST):
                dvv = b.op("vector", lambda e, dc=dc, n=n: e.tensor_scalar(
                    dAf[:], dtf[:], _col(P["A"], dc * NST + n), None, ALU.mult), inc="dv")
                b.w("scalar", "dv", dvv)
                acv = b.act(lambda e: e.activation(dAf[:], dAf[:], AF.Exp))
                b.op("vector", lambda e, n=n: e.tensor_tensor(
                    ubf[:], dtxb[:], Bb[:, n * TPC:(n + 1) * TPC], ALU.mult))
                b.w("vector", "ac", acv)
                for s in range(NSLOT):
                    dvv = b.op("vector", lambda e, s=s: e.tensor_tensor_scan(
                        hbf[:, s * TILE:(s + 1) * TILE], dAf[:, s * TILE:(s + 1) * TILE],
                        ubf[:, s * TILE:(s + 1) * TILE], 0.0, ALU.mult, ALU.add),
                        inc="dv" if s == NSLOT - 1 else None)
                b.w("vector", "dv", dvv)
                b.op("vector", lambda e, sc=sc, n=n: e.tensor_copy(
                    sc[:].rearrange("p (s c) -> p s c", c=17)[:, :, n],
                    hbf[:].rearrange("p (s t) -> p s t", t=TILE)[:, :, TILE - 1]))
                if DEBUG and dc == 0 and n == 0 and "d_da00" in io:
                    b.op("vector", lambda e: e.tensor_copy(tA[:], dAf[:]), inc="dv")
                    b.w("gpsimd", "dv", b.c["dv"])
                    b.dma("gpsimd", io["d_da00"][:, :], tA[:])
                    b.dma("gpsimd", io["d_u00"][:, :], ubf[:])
                    b.dma("gpsimd", io["d_hb00"][:, :], hbf[:])
                    b.dma("gpsimd", io["d_dtx0"][:, :], dtxb[:])
                    b.w("vector", "gd", b.c["gd"])
                b.w("vector", "pl", b.c["pl"])
                dvv = b.op("vector", lambda e, n=n: e.tensor_tensor(
                    ybf[:], hbf[:], Cb[:, n * TPC:(n + 1) * TPC], ALU.mult), inc="dv")
                b.w("gpsimd", "dv", dvv)
                b.op("gpsimd", lambda e, yc=yc: e.tensor_tensor(yc[:], yc[:], ybf[:], ALU.add),
                     inc="pl")
            b.dma("gpsimd", io["ACC"][dc], yc[:], accum_op=ALU.add)
            b.dma("gpsimd", io["stloc"][dc * 128:(dc + 1) * 128, :], sc[:])
            gd_ck[dc % 2] = b.c["gd"]
        b.bar()


def _ffn0(b, io, P, xnb):
    ph = ExitStack()
    with ph:
        sb = lambda nm, sh, dt: ph.enter_context(b.nc.sbuf_tensor(nm, sh, dt))
        pstn = lambda nm, sh, dt=F32: ph.enter_context(b.nc.psum_tensor(nm, sh, dt))
        Esg = sb("f_Esg", [32, 4096], BF16)
        Esd = sb("f_Esd", [8, 1024], BF16)
        Edn = sb("f_Edn", [128, NHC * 8], BF16)
        b.dma("sync", Esg[:], io["Esg"])
        b.dma("sync", Esd[:], io["Esd"])
        b.dma("sync", Edn[:], io["Edn"])
        qb = {pr: [sb(f"f_q{pr}0", [128, NDC * 128], BF16)] for pr in ("g", "u")}
        wpb = {pr: sb(f"f_w{pr}", [128, NDC * 128], BF16) for pr in ("g", "u")}
        zrow = sb("f_zrow", [32, 1024], F32)
        srow = {pr: sb(f"f_srow{pr}", [32, 1024], F32) for pr in ("g", "u")}
        szb = {pr: sb(f"f_szb{pr}", [32, 1024], BF16) for pr in ("g", "u")}
        ssb = {pr: sb(f"f_ssb{pr}", [32, 1024], BF16) for pr in ("g", "u")}
        hslab = sb("f_hslab", [128, 8 * TPC], BF16)
        silu_t = sb("f_silu", [128, 512], F32)
        shblk = sb("f_shblk", [8, TPC], BF16)
        zdrow = sb("f_zdrow", [8, TPC], F32)
        sdrow = sb("f_sdrow", [8, TPC], F32)
        szd = sb("f_szd", [8, TPC], BF16)
        sdb = sb("f_sdb", [8, TPC], BF16)
        qdb = [sb(f"f_qdb{i}", [128, 8 * 128], BF16) for i in range(2)]
        wdb = sb("f_wdb", [128, 8 * 128], BF16)
        dnf = [sb(f"f_dnf{i}", [128, TPC], F32) for i in range(2)]
        b.bar(engs=("tensor", "vector", "gpsimd"), sems=("ld",))
        if "d_esg_e" in io:
            b.dma("gpsimd", io["d_esg_e"][:, :], Esg[:])
        ps_g = pstn("f_psg", [128, 512])
        ps_u = pstn("f_psu", [128, 512])
        ps_sb = pstn("f_pssb", [128, 1024])
        ps_sh = [pstn(f"f_pssh{t}", [22, 512]) for t in range(2)]
        ps_dn = [pstn(f"f_psdn{t}", [128, 512]) for t in range(2)]

        qsrc = {"g": io["qg"], "u": io["qu"]}
        zsrc = {"g": io["zg"], "u": io["zu"]}
        ssrc = {"g": io["sg"], "u": io["su"]}

        gd_q = {}
        dv_w = 0
        pe_hc = 0

        for blk_i, (k0, bs) in enumerate(DN_BLKS):
            for j in range(bs):
                hc = k0 + j
                par = 0
                for pr in ("g", "u"):
                    b.w("gpsimd", "dv", dv_w)
                    gd_q[(pr, 0)] = b.dma("gpsimd", qb[pr][0][:], qsrc[pr][hc])
                if hc % 8 == 0:
                    hn = min(8, NHC - hc)
                    for pr in ("g", "u"):
                        b.w("gpsimd", "dv", b.c["dv"])
                        b.w("gpsimd", "pe", b.c["pe"])
                        b.w("sync", "dv", b.c["dv"])
                        b.w("sync", "pe", b.c["pe"])
                        ldz = b.dma("gpsimd", zrow[:, 0:hn * 128],
                                    zsrc[pr][:, hc * 128:(hc + hn) * 128])
                        lds = b.dma("sync", srow[pr][:, 0:hn * 128],
                                    ssrc[pr][:, hc * 128:(hc + hn) * 128])
                        b.w("vector", "gd", ldz)
                        b.w("vector", "ld", lds)
                        b.op("vector", lambda e, pr=pr, hn=hn: e.tensor_tensor(
                            zrow[:, 0:hn * 128], zrow[:, 0:hn * 128], srow[pr][:, 0:hn * 128], ALU.mult))
                        b.op("vector", lambda e, pr=pr, hn=hn: e.tensor_scalar(
                            szb[pr][:, 0:hn * 128], zrow[:, 0:hn * 128], -1.0, None, ALU.mult))
                        b.op("vector", lambda e, pr=pr, hn=hn: e.tensor_copy(
                            ssb[pr][:, 0:hn * 128], srow[pr][:, 0:hn * 128]), inc="dv")
                hoff = (hc % 8) * 128
                for pr in ("g", "u"):
                    b.w("tensor", "gd", gd_q[(pr, par)])
                    b.w("tensor", "dv", b.c["dv"])
                    for half in range(4):
                        pev = None
                        for gg in range(8):
                            g_ = half * 8 + gg
                            pev = b.op("tensor", lambda e, g_=g_, gg=gg, pr=pr, hoff=hoff: e.matmul(
                                ps_sb[:, gg * 128:(gg + 1) * 128],
                                Esg[:, g_ * 128:(g_ + 1) * 128],
                                ssb[pr][:, hoff:hoff + 128], start=True, stop=True), inc="pe")
                        b.w("vector", "pe", pev)
                        if hc == 0 and pr == "g" and half == 0 and "d_sb0" in io:
                            dvv = b.op("vector", lambda e: e.tensor_copy(silu_t[:], ps_sb[:, 0:512]), inc="dv")
                            b.w("gpsimd", "dv", dvv)
                            b.dma("gpsimd", io["d_sb0"][:, 0:512], silu_t[:])
                            dvv = b.op("vector", lambda e: e.tensor_copy(silu_t[:], ps_sb[:, 512:1024]), inc="dv")
                            b.w("gpsimd", "dv", dvv)
                            b.dma("gpsimd", io["d_sb0"][:, 512:1024], silu_t[:])
                            b.dma("gpsimd", io["d_ssbg"][:, :], ssb["g"][:])
                            b.dma("gpsimd", io["d_qb0"][:, :], qb["g"][0][:])
                            b.dma("gpsimd", io["d_esg_l"][:, :], Esg[:])
                            b.w("vector", "gd", b.c["gd"])
                        dv_w = b.op("vector", lambda e, pr=pr, half=half: e.tensor_tensor(
                            wpb[pr][:, half * 1024:(half + 1) * 1024],
                            qb[pr][0][:, half * 1024:(half + 1) * 1024],
                            ps_sb[:], ALU.mult), inc="dv")
                        b.w("tensor", "dv", dv_w)
                for t in range(2):
                    for pr, psx in (("g", ps_g), ("u", ps_u)):
                        for k in range(NDC):
                            b.op("tensor", lambda e, pr=pr, psx=psx, k=k, t_=t: e.matmul(
                                psx[:], wpb[pr][:, k * 128:(k + 1) * 128],
                                xnb[:, k * TPC + t_ * 512:k * TPC + (t_ + 1) * 512],
                                start=(k == 0), stop=False))
                        pe_hc = b.op("tensor", lambda e, pr=pr, psx=psx, t_=t, hoff=hoff: e.matmul(
                            psx[:], szb[pr][:, hoff:hoff + 128],
                            P["Sg"][:, t_ * 512:(t_ + 1) * 512], start=False, stop=True), inc="pe")
                    b.w("scalar", "pe", pe_hc)
                    b.w("scalar", "dv", b.c["dv"])
                    acv = b.act(lambda e: e.activation(silu_t[:], ps_g[:], AF.Silu))
                    b.w("vector", "ac", acv)
                    dvv = b.op("vector", lambda e, j=j, t_=t: e.tensor_tensor(
                        hslab[:, j * TPC + t_ * 512:j * TPC + (t_ + 1) * 512],
                        silu_t[:], ps_u[:], ALU.mult), inc="dv")
                    b.w("tensor", "dv", dvv)
                    if DEBUG and hc == 0 and t == 1 and pr == "u" and "d_wg0" in io:
                        b.w("gpsimd", "dv", b.c["dv"])
                        b.dma("gpsimd", io["d_wg0"][:, :], wpb["g"][:])
                        b.dma("gpsimd", io["d_h0"][:, :], hslab[:, 0:TPC])
                    b.op("tensor", lambda e, j=j, t_=t, hc=hc, bs=bs: e.matmul(
                        ps_sh[t_][:bs, :], Edn[:, hc * 8:hc * 8 + bs],
                        hslab[:, j * TPC + t_ * 512:j * TPC + (t_ + 1) * 512],
                        start=(j == 0), stop=(j == bs - 1)),
                        inc="pe" if j == bs - 1 else None)
            b.w("gpsimd", "gd", b.c["gd"])
            b.w("vector", "pe")
            for t in range(2):
                b.op("vector", lambda e, t_=t, bs=bs: e.tensor_copy(
                    shblk[:bs, t_ * 512:(t_ + 1) * 512], ps_sh[t_][:bs, :]), inc="dv")
            gd_dn = [0, 0]
            gd_dn[0] = b.dma("gpsimd", qdb[0][:, 0:bs * 128], io["qd"][0][:, k0 * 128:(k0 + bs) * 128])
            dn_dv = [0, 0]
            for dc in range(NDC):
                if dc % 8 == 0:
                    b.w("gpsimd", "dv", b.c["dv"])
                    b.w("sync", "dv", b.c["dv"])
                    b.w("sync", "pe", b.c["pe"])
                    ldz = b.dma("gpsimd", zdrow[:bs, :], io["zd"][k0:k0 + bs, dc * 128:(dc + 8) * 128])
                    lds = b.dma("sync", sdrow[:bs, :], io["sd"][k0:k0 + bs, dc * 128:(dc + 8) * 128])
                    b.w("vector", "gd", ldz)
                    b.w("vector", "ld", lds)
                    b.w("vector", "pe", b.c["pe"])
                    b.op("vector", lambda e, bs=bs: e.tensor_tensor(
                        zdrow[:bs, :], zdrow[:bs, :], sdrow[:bs, :], ALU.mult))
                    b.op("vector", lambda e, bs=bs: e.tensor_scalar(
                        szd[:bs, :], zdrow[:bs, :], -1.0, None, ALU.mult))
                    b.op("vector", lambda e, bs=bs: e.tensor_copy(sdb[:bs, :], sdrow[:bs, :]), inc="dv")
                if dc + 1 < NDC:
                    b.w("gpsimd", "dv", b.c["dv"])
                    gd_dn[(dc + 1) % 2] = b.dma("gpsimd", qdb[(dc + 1) % 2][:, 0:bs * 128],
                                                io["qd"][dc + 1][:, k0 * 128:(k0 + bs) * 128])
                b.w("tensor", "gd", gd_dn[dc % 2])
                b.w("tensor", "dv", b.c["dv"])
                for c0 in range(0, bs, 8):
                    cnt = min(8, bs - c0)
                    pev = None
                    for kk in range(cnt):
                        jj = c0 + kk
                        pev = b.op("tensor", lambda e, jj=jj, kk=kk, dc=dc, bs=bs: e.matmul(
                            ps_sb[:, kk * 128:(kk + 1) * 128],
                            Esd[:bs, jj * 128:(jj + 1) * 128],
                            sdb[:bs, (dc % 8) * 128:(dc % 8 + 1) * 128], start=True, stop=True), inc="pe")
                    b.w("vector", "pe", pev)
                    dvv = b.op("vector", lambda e, c0=c0, cnt=cnt, dc=dc: e.tensor_tensor(
                        wdb[:, c0 * 128:(c0 + cnt) * 128],
                        qdb[dc % 2][:, c0 * 128:(c0 + cnt) * 128],
                        ps_sb[:, 0:cnt * 128], ALU.mult), inc="dv")
                    b.w("tensor", "dv", dvv)
                if blk_i == 0 and dc in (0, 24) and "d_wd0" in io:
                    b.w("gpsimd", "dv", b.c["dv"])
                    b.dma("gpsimd", io["d_wd0" if dc == 0 else "d_wd24"][:, :], wdb[:, 0:1024])
                    if dc == 0:
                        b.dma("gpsimd", io["d_sdb"][:, 0:1024], sdb[:8, :])
                        b.dma("gpsimd", io["d_szd"][:, 0:1024], szd[:8, :])
                        b.dma("gpsimd", io["d_shblk"][:, :], shblk[:8, :])
                    b.w("tensor", "gd", b.c["gd"])
                pe_dn = 0
                for t in range(2):
                    b.w("tensor", "dv", dn_dv[t])
                    for jj in range(bs):
                        b.op("tensor", lambda e, jj=jj, t_=t: e.matmul(
                            ps_dn[t_][:], wdb[:, jj * 128:(jj + 1) * 128],
                            hslab[:, jj * TPC + t_ * 512:jj * TPC + (t_ + 1) * 512],
                            start=(jj == 0), stop=False))
                    pe_dn = b.op("tensor", lambda e, t_=t, dc=dc, bs=bs: e.matmul(
                        ps_dn[t_][:], szd[:bs, (dc % 8) * 128:(dc % 8 + 1) * 128],
                        shblk[:bs, t_ * 512:(t_ + 1) * 512], start=False, stop=True), inc="pe")
                b.w("vector", "pe", pe_dn)
                b.w("vector", "gd", b.c["gd"] if dc >= 2 else 0)
                dnb = dnf[dc % 2]
                for t in range(2):
                    dn_dv[t] = b.op("vector", lambda e, t_=t, dnb=dnb: e.tensor_copy(
                        dnb[:, t_ * 512:(t_ + 1) * 512], ps_dn[t_][:]), inc="dv")
                b.w("gpsimd", "dv", dn_dv[1])
                b.dma("gpsimd", io["ACC"][dc], dnb[:], accum_op=ALU.add)
        b.bar()


def _combine_fix(b, io, P, Bb, Cb):
    ph = ExitStack()
    with ph:
        sb = lambda nm, sh, dt: ph.enter_context(b.nc.sbuf_tensor(nm, sh, dt))
        stag = sb("c_stag", [128, 544], F32)
        stag2 = sb("c_stag2", [128, 544], F32)
        Sng = sb("c_sng", [128, 512], F32)
        Tg = sb("c_tg", [128, 32], F32)
        Tsh = sb("c_tsh", [128, 32], F32)
        DD = sb("c_dd", [128, 512], F32)
        Ssh = sb("c_ssh", [128, 512], F32)
        Hin = sb("c_hin", [128, 512], F32)
        Hinb = sb("c_hinb", [128, 512], BF16)
        tmpb = sb("c_tmpb", [128, 512], BF16)
        Hsel = sb("c_hsel", [128, 128], F32)
        hmskb = sb("c_hmsk", [128, NSLOT * 512], BF16)
        dtf = sb("c_dtf", [128, TPC], F32)
        tA = sb("c_tA", [128, TPC], F32)
        tB = sb("c_tB", [128, TPC], F32)
        cum = sb("c_cum", [128, TPC], F32)
        Gbf = sb("c_gbf", [128, TPC], BF16)
        Hb = sb("c_hb", [128, TPC], BF16)
        fix1 = sb("c_fix1", [128, TPC], BF16)
        ych = [sb(f"c_ych{i}", [128, TPC], F32) for i in range(2)]
        ps_dt = [ph.enter_context(b.nc.psum_tensor(f"c_psdt{t}", [128, 512], F32)) for t in range(2)]

        for s in range(NSLOT):
            b.dma("sync", hmskb[:, s * 512:(s + 1) * 512], io["hmsk"][s])
        b.bar(engs=("vector", "gpsimd"))

        gd_ck = [0, 0]
        pl_stag = 0
        dv_stag = 0
        for dc in range(NDC):
            b.w("sync", "pl", pl_stag)
            for qq in range(4):
                b.dma("sync", stag[:, qq * 136:(qq + 1) * 136],
                      io["stg"][qq * DM + dc * 128:qq * DM + (dc + 1) * 128, :])
            b.w("gpsimd", "ld", b.c["ld"])
            b.w("gpsimd", "dv", dv_stag)
            plv = None
            for g in range(32):
                qp, sp = G2QS[g]
                plv = b.op("gpsimd", lambda e, g=g, qp=qp, sp=sp: e.tensor_copy(
                    stag2[:, g * 17:(g + 1) * 17],
                    stag[:, qp * 136 + sp * 17:qp * 136 + sp * 17 + 17]), inc="pl")
            pl_stag = plv
            b.w("vector", "pl", plv)
            st3 = stag2[:].rearrange("p (g c) -> p c g", c=17)
            b.op("vector", lambda e, st3=st3: e.tensor_copy(
                Sng[:].rearrange("p (n g) -> p n g", g=32), st3[:, 0:16, :]))
            dv_stag = b.op("vector", lambda e, st3=st3: e.tensor_copy(Tg[:], st3[:, 16, :]), inc="dv")
            if dc == 0 and "d_tg0" in io:
                b.w("sync", "dv", dv_stag)
                b.dma("sync", io["d_stag20"], stag2[:])
                b.dma("sync", io["d_sng0"], Sng[:])
                b.dma("sync", io["d_tg0"], Tg[:])
            b.op("vector", lambda e: e.memset(Tsh[:, 0:1], 0.0))
            b.op("vector", lambda e: e.tensor_copy(Tsh[:, 1:32], Tg[:, 0:31]))
            b.op("vector", lambda e: e.drain())
            for n in range(NST):
                b.op("vector", lambda e, n=n, dc=dc: e.tensor_scalar(
                    DD[:, n * 32:(n + 1) * 32], Tsh[:], _col(P["A"], dc * NST + n), None, ALU.mult),
                    inc="dv" if n == NST - 1 else None)
            b.w("scalar", "dv")
            acv = b.act(lambda e: e.activation(DD[:], DD[:], AF.Exp))
            b.op("vector", lambda e: e.memset(Ssh[:], 0.0))
            b.op("vector", lambda e: e.tensor_copy(
                Ssh[:].rearrange("p (n g) -> p n g", g=32)[:, :, 1:32],
                Sng[:].rearrange("p (n g) -> p n g", g=32)[:, :, 0:31]))
            b.op("vector", lambda e: e.drain())
            b.w("vector", "ac", acv)
            for n in range(NST):
                b.op("vector", lambda e, n=n: e.memset(DD[:, n * 32:n * 32 + 1], 0.0))
            b.op("vector", lambda e: e.drain())
            dvv = b.op("vector", lambda e: e.tensor_tensor_scan(
                Hin[:], DD[:], Ssh[:], 0.0, ALU.mult, ALU.add), inc="dv")
            b.w("vector", "dv", dvv)
            b.op("vector", lambda e: e.tensor_copy(Hinb[:], Hin[:]), inc="dv")
            if dc == 0 and "d_hin0" in io:
                b.w("sync", "dv", b.c["dv"])
                b.dma("sync", io["d_hin0"], Hin[:])
            b.w("gpsimd", "dv", b.c["dv"])
            for s in range(NSLOT):
                b.w("gpsimd", "dv", b.c["dv"])
                plv = b.op("gpsimd", lambda e, s=s: e.tensor_tensor(
                    tmpb[:], Hinb[:], hmskb[:, s * 512:(s + 1) * 512], ALU.mult), inc="pl")
                b.w("vector", "pl", plv)
                b.op("vector", lambda e, s=s: e.tensor_reduce(
                    Hsel[:, s * 16:(s + 1) * 16],
                    tmpb[:].rearrange("p (n g) -> p n g", g=32), axis=AX, op=ALU.add),
                    inc="dv")
            if dc == 0 and "d_hsel0" in io:
                b.w("sync", "dv", b.c["dv"])
                b.dma("sync", io["d_hsel0"], Hsel[:])
            yc = ych[dc % 2]
            b.w("vector", "gd", gd_ck[dc % 2])
            b.w("tensor", "dv", b.c["dv"])
            _dt_chunk(b, P, dc, ps_dt, dtf, tA, tB)
            for s in range(NSLOT):
                dvv = b.op("vector", lambda e, s=s: e.tensor_tensor_scan(
                    cum[:, s * TILE:(s + 1) * TILE], P["ones128t"][:],
                    dtf[:, s * TILE:(s + 1) * TILE], 0.0, ALU.mult, ALU.add),
                    inc="dv" if s == NSLOT - 1 else None)
            b.w("vector", "dv", dvv)
            if dc == 0 and "d_cum0" in io:
                b.w("sync", "dv", b.c["dv"])
                b.dma("sync", io["d_cum0"], cum[:])
                b.dma("sync", io["d_dtf0"], dtf[:])
                b.w("gpsimd", "ld", b.c["ld"])
                b.w("vector", "ld", b.c["ld"])
            pl_fix = 0
            for n in range(NST):
                dvv = b.op("vector", lambda e, n=n, dc=dc: e.tensor_scalar(
                    tA[:], cum[:], _col(P["A"], dc * NST + n), None, ALU.mult), inc="dv")
                b.w("scalar", "dv", dvv)
                b.w("scalar", "pl", pl_fix)
                acv = b.act(lambda e: e.activation(Gbf[:], tA[:], AF.Exp))
                hsrc = Hsel[:].rearrange("p (s n) -> p s n", n=NST)[:, :, n].unsqueeze(2).to_broadcast((128, 8, 128))
                b.w("vector", "pl", pl_fix)
                b.op("vector", lambda e, hsrc=hsrc: e.tensor_copy(
                    Hb[:].rearrange("p (s t) -> p s t", t=TILE), hsrc), inc="dv")
                b.w("gpsimd", "ac", acv)
                b.w("gpsimd", "dv", b.c["dv"])
                pl_fix = b.op("gpsimd", lambda e: e.tensor_tensor(fix1[:], Gbf[:], Hb[:], ALU.mult),
                           inc="pl")
                plv = pl_fix
                b.w("vector", "pl", plv)
                dvv = b.op("vector", lambda e, n=n: e.tensor_tensor(
                    fix1[:], fix1[:], Cb[:, n * TPC:(n + 1) * TPC], ALU.mult), inc="dv")
                b.w("gpsimd", "dv", dvv)
                if n == 0:
                    plv = b.op("gpsimd", lambda e, yc=yc: e.tensor_copy(yc[:], fix1[:]), inc="pl")
                else:
                    plv = b.op("gpsimd", lambda e, yc=yc: e.tensor_tensor(
                        yc[:], yc[:], fix1[:], ALU.add), inc="pl")
                b.w("vector", "pl", plv)
            b.w("gpsimd", "pl", b.c["pl"])
            b.dma("gpsimd", io["ACC"][dc], yc[:], accum_op=ALU.add)
            gd_ck[dc % 2] = b.c["gd"]
        b.bar()


def _l1_phases(b, io, P, mode):
    nc = b.nc
    t1s_g = b.sb("t1s_g", [128, TPC], BF16) if mode in ("L3", "ALL") else None
    if mode in ("L2", "ALL"):
        ph = ExitStack()
        with ph:
            sb = lambda nm, sh, dt: ph.enter_context(nc.sbuf_tensor(nm, sh, dt))
            pstn = lambda nm, sh: ph.enter_context(nc.psum_tensor(nm, sh, F32))
            xnb1 = sb("q_xnb1", [128, NDC * TPC], BF16)
            mb1 = sb("q_mb1", [128, TPC], F32)
            rstdb1 = sb("q_rstdb1", [128, TPC], F32)
            _ln_phase(b, P, io["ACC"], io, with_sg=False, with_xp=False, xnb=xnb1,
                      mb=mb1, rstdb=rstdb1)
            import os
            cut = os.environ.get("K_L2CUT", "") if mode == "L2" else ""
            if cut == "ln1":
                for dc in range(NDC):
                    b.dma("gpsimd", io["xn1o"][dc], xnb1[:, dc * TPC:(dc + 1) * TPC])
                b.w("gpsimd", "gd", b.c["gd"])
                b.w("gpsimd", "ld", b.c["ld"])
                return

            wqb = [sb(f"q_wq{i}", [128, NDC * 128], BF16) for i in range(2)]
            bq = sb("q_bq", [128, NQKV], F32)
            evb = [sb(f"q_ev{i}", [128, TPC], BF16) for i in range(2)]
            qtb = sb("q_qtb", [128, NCC * TPC], BF16)
            arsb = sb("q_arsb", [128, NDC], F32)
            armul = sb("q_armul", [128, TPC], F32)
            wvbs = sb("q_wvb", [128, TPC], F32)
            t1f = sb("q_t1f", [128, TPC], F32)
            acb = sb("q_acb", [128, NDC * 128], BF16)
            ps_q = [[pstn(f"q_psq{i}{t}", [128, 512]) for t in range(2)] for i in range(2)]

            b.dma("sync", bq[:], io["bqkv_t"])
            b.dma("sync", wvbs[:], io["wvb"])
            b.bar(engs=("vector", "gpsimd"))
            for dc in range(NDC):
                b.op("vector", lambda e, dc=dc: e.tensor_tensor(
                    armul[:], xnb1[:, dc * TPC:(dc + 1) * TPC], wvbs[:], ALU.mult))
                b.op("vector", lambda e, dc=dc: e.tensor_reduce(
                    arsb[:, dc:dc + 1], armul[:], axis=AX, op=ALU.add),
                    inc="dv" if dc == NDC - 1 else None)
            if cut == "rt":
                b.w("gpsimd", "dv", b.c["dv"])
                b.dma("gpsimd", io["arpart"][:, :], arsb[:])
                for dc in range(NDC):
                    b.dma("gpsimd", io["xn1o"][dc], xnb1[:, dc * TPC:(dc + 1) * TPC])
                b.w("gpsimd", "gd", b.c["gd"])
                b.w("gpsimd", "ld", b.c["ld"])
                return
            gd_w = [0, 0]
            gd_w[0] = b.dma("gpsimd", wqb[0][:].rearrange("p (c r) -> p c r", r=128),
                            io["wqkvT"].rearrange("c p (o r) -> p c r o", r=128)[:, :, :, 0])
            pe_oc = 0
            dv_oc = [0, 0]
            for oc in range(NQKV):
                if oc + 1 < NQKV:
                    b.w("gpsimd", "pe", pe_oc)
                    gd_w[(oc + 1) % 2] = b.dma(
                        "gpsimd", wqb[(oc + 1) % 2][:].rearrange("p (c r) -> p c r", r=128),
                        io["wqkvT"].rearrange("c p (o r) -> p c r o", r=128)[:, :, :, oc + 1])
                b.w("tensor", "gd", gd_w[oc % 2])
                b.w("tensor", "dv", dv_oc[oc % 2])
                for t in range(2):
                    for k in range(NDC):
                        b.op("tensor", lambda e, t_=t, k=k, oc=oc: e.matmul(
                            ps_q[oc % 2][t_][:], wqb[oc % 2][:, k * 128:(k + 1) * 128],
                            xnb1[:, k * TPC + t_ * 512:k * TPC + (t_ + 1) * 512],
                            start=(k == 0), stop=(k == NDC - 1)),
                            inc="pe" if k == NDC - 1 else None)
                pe_oc = b.c["pe"]
                b.w("vector", "pe", pe_oc)
                if oc < NCC:
                    dst = [qtb[:, oc * TPC + t * 512:oc * TPC + (t + 1) * 512] for t in range(2)]
                else:
                    dst = [evb[oc % 2][:, t * 512:(t + 1) * 512] for t in range(2)]
                    b.w("vector", "gd", b.c["gd"])
                for t in range(2):
                    dvv = b.op("vector", lambda e, t_=t, oc=oc, d=dst[t]: e.tensor_scalar(
                        d, ps_q[oc % 2][t_][:], _col(bq, oc), None, ALU.add), inc="dv")
                dv_oc[oc % 2] = dvv
                if oc >= NCC:
                    b.w("gpsimd", "dv", dvv)
                    b.dma("gpsimd", io["kvloc"][(oc - NCC) * 128:(oc - NCC + 1) * 128, :],
                          evb[oc % 2][:])
            if cut == "qkv":
                b.w("gpsimd", "dv", b.c["dv"])
                for cc in range(NCC):
                    b.dma("gpsimd", io["qto"][cc], qtb[:, cc * TPC:(cc + 1) * TPC])
                for dc in range(NDC):
                    b.dma("gpsimd", io["xn1o"][dc], xnb1[:, dc * TPC:(dc + 1) * TPC])
                b.dma("gpsimd", io["arpart"][:, :], arsb[:])
                b.w("gpsimd", "gd", b.c["gd"])
                b.w("gpsimd", "ld", b.c["ld"])
                return
            b.dma("gpsimd", acb[:].rearrange("p (c r) -> p c r", r=128), io["AcatT"].rearrange("c p r -> p c r"))
            b.w("tensor", "gd", b.c["gd"])
            b.w("tensor", "dv", b.c["dv"])
            for t in range(2):
                for k in range(NDC):
                    b.op("tensor", lambda e, t_=t, k=k: e.matmul(
                        ps_q[0][t_][:], acb[:, k * 128:(k + 1) * 128],
                        xnb1[:, k * TPC + t_ * 512:k * TPC + (t_ + 1) * 512],
                        start=(k == 0), stop=(k == NDC - 1)),
                        inc="pe" if k == NDC - 1 else None)
            b.w("vector", "pe")
            for t in range(2):
                b.op("vector", lambda e, t_=t: e.tensor_copy(
                    t1f[:, t_ * 512:(t_ + 1) * 512], ps_q[0][t_][:]), inc="dv")
            if mode == "L2":
                b.w("gpsimd", "dv", b.c["dv"])
                for dc in range(NDC):
                    b.dma("gpsimd", io["xn1o"][dc], xnb1[:, dc * TPC:(dc + 1) * TPC])
                for cc in range(NCC):
                    b.dma("gpsimd", io["qto"][cc], qtb[:, cc * TPC:(cc + 1) * TPC])
                b.dma("gpsimd", io["arpart"][:, :], arsb[:])
                b.dma("gpsimd", io["t1o"][:, :], t1f[:])
                b.w("gpsimd", "gd", b.c["gd"])
                b.w("gpsimd", "ld", b.c["ld"])
            else:
                b.w("gpsimd", "dv", b.c["dv"])
                for dc in range(NDC):
                    b.dma("gpsimd", io["xn1d"][dc], xnb1[:, dc * TPC:(dc + 1) * TPC])
                b.dma("gpsimd", io["ar_in"][:, :], arsb[:])
                b.w("gpsimd", "gd", b.c["gd"])
                b.op("gpsimd", lambda e: e.collective_compute(
                    "AllGather", ALU.bypass, replica_groups=[[0, 1, 2, 3], [4, 5, 6, 7]],
                    ins=[io["kvloc"].opt()], outs=[io["kvg"].opt()]), inc="cc", drain=False)
                b.op("gpsimd", lambda e: e.collective_compute(
                    "AllReduce", ALU.add, replica_groups=[[0, 1, 2, 3], [4, 5, 6, 7]],
                    ins=[io["ar_in"].opt()], outs=[io["ar_out"].opt()]), inc="cc", drain=False)
                for cc in range(NCC):
                    b.dma("gpsimd", io["qtd"][cc], qtb[:, cc * TPC:(cc + 1) * TPC])
                b.dma("gpsimd", io["t1d"][:, :], t1f[:])
                b.w("gpsimd", "gd", b.c["gd"])
            b.bar()

    if mode not in ("L3", "ALL"):
        return

    import os as _os
    cut3 = _os.environ.get("K_L3CUT", "") if mode == "L3" else ""

    def fin():
        b.dma("gpsimd", io["out_t"].rearrange("c p t -> (c p) t"),
              io["ACC"].rearrange("c p t -> (c p) t"), sem="so")
        b.w("gpsimd", "so", b.c["so"])

    qsrcs = io["qti"] if mode == "L3" else io["qtd"]
    kvg = io["kvg"]
    ph = ExitStack()
    with ph:
        sb = lambda nm, sh, dt: ph.enter_context(nc.sbuf_tensor(nm, sh, dt))
        pstn = lambda nm, sh, dt=F32: ph.enter_context(nc.psum_tensor(nm, sh, dt))
        qtb = sb("a_qtb", [128, NCC * TPC], BF16)
        Oall = sb("a_oall", [128, 64 * 256], F32)
        mall = sb("a_mall", [128, 64], F32)
        lall = sb("a_lall", [128, 64], F32)
        ksl = sb("a_ksl", [128, NCC * 512], BF16)
        vsl = [sb(f"a_vsl{j}", [128, 2048], BF16) for j in range(4)]
        smf = sb("a_smf", [128, 512], F32)
        pbf = sb("a_pbf", [128, 512], BF16)
        ptb = sb("a_ptb", [128, 512], BF16)
        maskb = sb("a_maskb", [128, NSLOT * 512], F32)
        cmax = sb("a_cmax", [128, 1], F32)
        mnew = sb("a_mnew", [128, 1], F32)
        negm = sb("a_negm", [128, 1], F32)
        alph = sb("a_alph", [128, 1], F32)
        rs = sb("a_rs", [128, 1], F32)
        rl = sb("a_rl", [128, 1], F32)
        vvb = sb("a_vvb", [128, 256], BF16)
        arsb2 = sb("a_arsb", [128, NDC], F32)
        ar4 = sb("a_ar4", [128, NDC], F32)
        qcolb = sb("a_qcol", [128, NDC], BF16)
        rwb = sb("a_rwb", [128, NDC * 8], BF16)
        rbc = sb("a_rbc", [8, 1], F32)
        lg = sb("a_lg", [8, 1], F32)
        l8 = sb("a_l8", [1, 8], F32)
        e8 = sb("a_e8", [1, 8], F32)
        sm1 = sb("a_sm1", [1, 1], F32)
        sm1b = sb("a_sm1b", [1, 1], F32)
        e8b = sb("a_e8b", [1, 8], F32)
        rcol = sb("a_rcol", [8, 1], F32)
        rvc = sb("a_rvc", [128, 1], F32)
        ervb = sb("a_ervb", [8, 128], F32)
        t1f = sb("a_t1f", [128, TPC], F32)
        t1s = t1s_g
        dsm = sb("a_dsm", [128, 512], F32) if "d_smf00" in io else None
        dpb = sb("a_dpb", [128, 512], BF16) if "d_smf00" in io else None
        dob = sb("a_dob", [128, 256], F32) if "d_smf00" in io else None
        ps_s = pstn("a_pss", [128, 512])
        ps_pt = pstn("a_pspt", [128, 512], BF16)
        ps_av = pstn("a_psav", [128, 256])
        ps_r = pstn("a_psr", [8, 1])
        ps_rv = pstn("a_psrv", [128, 1])

        for s in range(NSLOT):
            b.dma("sync", maskb[:, s * 512:(s + 1) * 512], io["masks"][s])
        for cc in range(NCC):
            b.dma("sync", qtb[:, cc * TPC:(cc + 1) * TPC], qsrcs[cc])
        b.dma("sync", ervb[:], io["Erv"])
        b.dma("sync", rbc[:], io["rb_t"])
        b.dma("sync", t1f[:], io["t1i"][:, :] if mode == "L3" else io["t1d"][:, :])
        b.dma("gpsimd", rwb[:].rearrange("p (c r) -> p c r", r=8), io["rWT"].rearrange("c p r -> p c r"))
        if mode == "L3":
            b.dma("sync", arsb2[:], io["arall"][0:128, :])
            for qq in range(1, 4):
                b.dma("sync", ar4[:], io["arall"][qq * 128:(qq + 1) * 128, :])
                b.w("vector", "ld", b.c["ld"])
                b.op("vector", lambda e: e.tensor_tensor(arsb2[:], arsb2[:], ar4[:], ALU.add),
                     inc="dv")
                b.w("sync", "dv", b.c["dv"])
        else:
            b.w("sync", "cc", b.c["cc"])
            b.dma("sync", arsb2[:], io["ar_out"][:, :])
        b.bar(engs=("vector", "tensor", "scalar", "gpsimd"))
        b.op("vector", lambda e: e.tensor_scalar(qcolb[:], arsb2[:], R_INV, None, ALU.mult), inc="dv")
        b.w("tensor", "dv")
        for dc in range(NDC):
            b.op("tensor", lambda e, dc=dc: e.matmul(
                ps_r[:], rwb[:, dc * 8:(dc + 1) * 8], qcolb[:, dc:dc + 1],
                start=(dc == 0), stop=(dc == NDC - 1)), inc="pe" if dc == NDC - 1 else None)
        b.w("vector", "pe")
        b.op("vector", lambda e: e.tensor_tensor(lg[:], ps_r[:], rbc[:], ALU.add), inc="dv")
        b.w("gpsimd", "dv", b.c["dv"])
        b.dma("gpsimd", io["bnc2"], lg[:], sem="gd")
        b.w("sync", "gd", b.c["gd"])
        b.dma("sync", l8[:], io["bnc2"].rearrange("a b -> b a"))
        b.w("vector", "ld", b.c["ld"])
        b.op("vector", lambda e: e.reduce_max(sm1[:], l8[:], axis=AX))
        b.op("vector", lambda e: e.drain())
        b.op("vector", lambda e: e.tensor_scalar(sm1[:], sm1[:], -1.0, None, ALU.mult), inc="dv")
        b.w("scalar", "dv", b.c["dv"])
        b.act(lambda e: e.activation(e8[:], l8[:], AF.Exp, bias=sm1[:]))
        b.w("vector", "ac", b.c["ac"])
        b.op("vector", lambda e: e.tensor_reduce(sm1[:], e8[:], axis=AX, op=ALU.add))
        b.op("vector", lambda e: e.drain())
        b.op("vector", lambda e: e.reciprocal(sm1b[:], sm1[:]))
        b.op("vector", lambda e: e.drain())
        b.op("vector", lambda e: e.tensor_scalar(e8b[:], e8[:], sm1b[:], 2.0, ALU.mult, ALU.mult),
             inc="dv")
        b.w("gpsimd", "dv", b.c["dv"])
        b.dma("gpsimd", io["bnc1"], e8b[:], sem="gd")
        b.w("sync", "gd", b.c["gd"])
        b.dma("sync", rcol[:], io["bnc1"].rearrange("a b -> b a"))
        b.w("tensor", "ld", b.c["ld"])
        b.op("tensor", lambda e: e.matmul(ps_rv[:], ervb[:], rcol[:], start=True, stop=True),
             inc="pe")
        b.w("vector", "pe")
        b.op("vector", lambda e: e.tensor_copy(rvc[:], ps_rv[:]))
        b.op("vector", lambda e: e.drain())
        b.op("vector", lambda e: e.tensor_scalar(t1s[:], t1f[:], rvc[:], None, ALU.mult), inc="dv")
        if cut3 == "rt":
            b.w("gpsimd", "dv", b.c["dv"])
            b.w("gpsimd", "ld", b.c["ld"])
            if "d_rvc" in io:
                b.dma("gpsimd", io["d_rvc"], rvc[:])
                b.dma("gpsimd", io["d_lg"], lg[:])
                b.dma("gpsimd", io["d_l8"], l8[:])
                b.dma("gpsimd", io["d_e8"], e8[:])
                b.dma("gpsimd", io["d_e8b"], e8b[:])
                b.dma("gpsimd", io["d_sm1"], sm1[:])
                b.dma("gpsimd", io["d_sm1b"], sm1b[:])
                b.dma("gpsimd", io["d_rcol"], rcol[:])
            b.w("gpsimd", "gd", b.c["gd"])
            fin()
            b.w("gpsimd", "ld", b.c["ld"])
            return

        ld_fc = 0
        for fc in range(NSLOT):
            if fc > 0:
                b.w("sync", "pe", b.c["pe"])
            for j in range(4):
                tg = fc * 4 + j
                qp, sp = G2QS[tg]
                base = qp * DM
                src = kvg[base:base + 2048, sp * 128:(sp + 1) * 128]
                b.dma("sync", ksl[:].rearrange("p (c w) -> p c w", w=512)[:, :, j * 128:(j + 1) * 128],
                      src.rearrange("(c p) f -> p c f", p=128))
                b.dma("sync", vsl[j][:], kvg[base + 2048:base + 4096, sp * 128:(sp + 1) * 128],
                      transpose=True)
            ld_fc = b.c["ld"]
            b.w("tensor", "ld", ld_fc)
            for s in range(fc, NSLOT):
                for h in range(HEADS):
                    ii = s * 8 + h
                    pev = None
                    for kc in range(2):
                        cc = 2 * h + kc
                        pev = b.op("tensor", lambda e, cc=cc, s=s, kc=kc: e.matmul(
                            ps_s[:], qtb[:, cc * TPC + s * 128:cc * TPC + (s + 1) * 128],
                            ksl[:, cc * 512:(cc + 1) * 512], start=(kc == 0), stop=(kc == 1)),
                            inc="pe")
                    b.w("vector", "pe", pev)
                    if fc == s:
                        b.op("vector", lambda e, s=s: e.tensor_tensor(
                            smf[:], ps_s[:], maskb[:, s * 512:(s + 1) * 512], ALU.add))
                    else:
                        b.op("vector", lambda e: e.tensor_copy(smf[:], ps_s[:]))
                    dvv = b.op("vector", lambda e: e.reduce_max(cmax[:], smf[:], axis=AX), inc="dv")
                    if fc == 0 and s == 0 and h == 0 and dsm is not None:
                        b.op("vector", lambda e: e.tensor_copy(dsm[:], smf[:]))
                    if fc == 0:
                        b.op("vector", lambda e, ii=ii: e.tensor_copy(_col(mall, ii), cmax[:]))
                        b.op("vector", lambda e: e.tensor_scalar(
                            negm[:], cmax[:], -1.0, None, ALU.mult), inc="dv")
                    else:
                        b.op("vector", lambda e, ii=ii: e.tensor_tensor(
                            mnew[:], _col(mall, ii), cmax[:], ALU.max))
                        b.op("vector", lambda e: e.drain())
                        b.op("vector", lambda e, ii=ii: e.tensor_tensor(
                            alph[:], _col(mall, ii), mnew[:], ALU.subtract))
                        b.op("vector", lambda e, ii=ii: e.tensor_copy(_col(mall, ii), mnew[:]))
                        b.op("vector", lambda e: e.tensor_scalar(
                            negm[:], mnew[:], -1.0, None, ALU.mult), inc="dv")
                        b.w("scalar", "dv", b.c["dv"])
                        acv = b.act(lambda e: e.activation(alph[:], alph[:], AF.Exp))
                        b.w("vector", "ac", acv)
                        b.op("vector", lambda e, ii=ii: e.tensor_scalar(
                            Oall[:, ii * 256:(ii + 1) * 256], Oall[:, ii * 256:(ii + 1) * 256],
                            alph[:], None, ALU.mult))
                        b.op("vector", lambda e, ii=ii: e.tensor_scalar(
                            _col(lall, ii), _col(lall, ii), alph[:], None, ALU.mult), inc="dv")
                    b.w("scalar", "dv", b.c["dv"])
                    acv = b.act(lambda e: e.activation(
                        pbf[:], smf[:], AF.Exp, bias=negm[:]))
                    b.w("vector", "ac", acv)
                    if fc == 0 and s == 0 and h == 0 and dsm is not None:
                        b.op("vector", lambda e: e.tensor_copy(dpb[:], pbf[:]))
                    b.op("vector", lambda e: e.tensor_reduce(rs[:], pbf[:], axis=AX, op=ALU.add))
                    b.op("vector", lambda e: e.drain())
                    if fc == 0:
                        b.op("vector", lambda e, ii=ii: e.tensor_copy(_col(lall, ii), rs[:]),
                             inc="dv")
                    else:
                        b.op("vector", lambda e, ii=ii: e.tensor_tensor(
                            _col(lall, ii), _col(lall, ii), rs[:], ALU.add), inc="dv")
                    b.w("tensor", "dv", b.c["dv"])
                    pev = None
                    for j in range(4):
                        pev = b.op("tensor", lambda e, j=j: e.transpose(
                            ps_pt[:, j * 128:(j + 1) * 128], pbf[:, j * 128:(j + 1) * 128],
                            P["ident"][:]), inc="pe")
                    b.w("vector", "pe", pev)
                    dvv = b.op("vector", lambda e: e.tensor_copy(ptb[:], ps_pt[:]), inc="dv")
                    b.w("tensor", "dv", dvv)
                    pev = None
                    for j in range(4):
                        pev = b.op("tensor", lambda e, j=j, h=h: e.matmul(
                            ps_av[:], ptb[:, j * 128:(j + 1) * 128],
                            vsl[j][:, h * 256:(h + 1) * 256], start=(j == 0), stop=(j == 3)),
                            inc="pe")
                    b.w("vector", "pe", pev)
                    if fc == 0:
                        b.op("vector", lambda e, ii=ii: e.tensor_copy(
                            Oall[:, ii * 256:(ii + 1) * 256], ps_av[:]), inc="dv")
                        if s == 0 and h == 0 and dsm is not None:
                            b.op("vector", lambda e: e.tensor_copy(dob[:], ps_av[:]))
                    else:
                        b.op("vector", lambda e, ii=ii: e.tensor_tensor(
                            Oall[:, ii * 256:(ii + 1) * 256], Oall[:, ii * 256:(ii + 1) * 256],
                            ps_av[:], ALU.add), inc="dv")
                    b.w("tensor", "dv", b.c["dv"])
        for s in range(NSLOT):
            for h in range(HEADS):
                ii = s * 8 + h
                b.op("vector", lambda e, ii=ii: e.reciprocal(rl[:], _col(lall, ii)))
                b.op("vector", lambda e: e.drain())
                b.op("vector", lambda e, ii=ii: e.tensor_scalar(
                    vvb[:], Oall[:, ii * 256:(ii + 1) * 256], rl[:], None, ALU.mult), inc="dv")
                b.w("tensor", "dv", b.c["dv"])
                for kc in range(2):
                    cc = 2 * h + kc
                    pev = b.op("tensor", lambda e, kc=kc: e.transpose(
                        ps_pt[:, kc * 128:(kc + 1) * 128], vvb[:, kc * 128:(kc + 1) * 128],
                        P["ident"][:]), inc="pe")
                b.w("vector", "pe", b.c["pe"])
                for kc in range(2):
                    cc = 2 * h + kc
                    b.op("vector", lambda e, cc=cc, s=s, kc=kc: e.tensor_copy(
                        qtb[:, cc * TPC + s * 128:cc * TPC + (s + 1) * 128],
                        ps_pt[:, kc * 128:(kc + 1) * 128]), inc="dv")
                b.w("tensor", "dv", b.c["dv"])
        b.bar()
        if cut3 == "fa":
            if "d_vals" in io:
                b.dma("gpsimd", io["d_vals"][:, :], qtb[:])
                b.dma("gpsimd", io["d_smf00"], dsm[:])
                b.dma("gpsimd", io["d_pb00"], dpb[:])
                b.dma("gpsimd", io["d_o00"], dob[:])
                b.dma("gpsimd", io["d_mall"], mall[:])
                b.dma("gpsimd", io["d_lall"], lall[:])
                b.dma("gpsimd", io["d_ksl0"], ksl[:, 0:512])
                b.dma("gpsimd", io["d_vsl0"], vsl[0][:, 0:256])
                b.dma("gpsimd", io["d_q00"], qtb[:, 0:128])
            b.w("gpsimd", "gd", b.c["gd"])
            fin()
            b.w("gpsimd", "ld", b.c["ld"])
            return
        wob = [sb(f"a_wo{i}", [128, NCC * 128], BF16) for i in range(2)]
        bob = sb("a_bob", [128, NDC], F32)
        evo = [sb(f"a_evo{i}", [128, TPC], F32) for i in range(2)]
        ps_o = [pstn(f"a_pso{t}", [128, 512]) for t in range(2)]
        b.dma("sync", bob[:], io["bo_t"])
        gd_w = [0, 0]
        gd_w[0] = b.dma("gpsimd", wob[0][:].rearrange("p (c r) -> p c r", r=128),
                        io["woT"].rearrange("c p (o r) -> p c r o", r=128)[:, :, :, 0])
        pe_oc = 0
        dv_oc = [0, 0]
        for oc in range(NDC):
            if oc + 1 < NDC:
                b.w("gpsimd", "pe", pe_oc)
                gd_w[(oc + 1) % 2] = b.dma(
                    "gpsimd", wob[(oc + 1) % 2][:].rearrange("p (c r) -> p c r", r=128),
                    io["woT"].rearrange("c p (o r) -> p c r o", r=128)[:, :, :, oc + 1])
            b.w("tensor", "gd", gd_w[oc % 2])
            b.w("tensor", "dv", b.c["dv"])
            for t in range(2):
                for k in range(NCC):
                    b.op("tensor", lambda e, t_=t, k=k, oc=oc: e.matmul(
                        ps_o[t_][:], wob[oc % 2][:, k * 128:(k + 1) * 128],
                        qtb[:, k * TPC + t_ * 512:k * TPC + (t_ + 1) * 512],
                        start=(k == 0), stop=(k == NCC - 1)),
                        inc="pe" if k == NCC - 1 else None)
            pe_oc = b.c["pe"]
            b.w("vector", "pe", pe_oc)
            evv = evo[oc % 2]
            for t in range(2):
                dvv = b.op("vector", lambda e, t_=t, oc=oc, ev=evv: e.tensor_scalar(
                    ev[:, t_ * 512:(t_ + 1) * 512], ps_o[t_][:], _col(bob, oc), None, ALU.add),
                    inc="dv")
            dv_oc[oc % 2] = dvv
            b.w("gpsimd", "dv", dvv)
            b.w("gpsimd", "gd", gd_w[oc % 2])
            b.dma("gpsimd", io["ACC"][oc], evv[:], accum_op=ALU.add)
        b.bar()
        if cut3 == "op":
            fin()
            b.w("gpsimd", "ld", b.c["ld"])
            return

    ph2 = ExitStack()
    with ph2:
        sb = lambda nm, sh, dt: ph2.enter_context(nc.sbuf_tensor(nm, sh, dt))
        pstn = lambda nm, sh: ph2.enter_context(nc.psum_tensor(nm, sh, F32))
        xnb2 = sb("w_xnb", [128, NDC * TPC], BF16)
        wwb = [sb(f"w_ww{i}", [128, NDC * 128], BF16) for i in range(2)]
        bcb = sb("w_bcb", [128, DM], BF16)
        bwb = sb("w_bwb", [128, NDC], F32)
        evw = [sb(f"w_evw{i}", [128, TPC], F32) for i in range(2)]
        ps_w = [[pstn(f"w_psw{i}{t}", [128, 512]) for t in range(2)] for i in range(2)]
        xsrc = io["xn1i"] if mode == "L3" else io["xn1d"]
        for dc in range(NDC):
            b.dma("sync", xnb2[:, dc * TPC:(dc + 1) * TPC], xsrc[dc])
        b.dma("sync", bwb[:], io["bW_t"])
        b.dma("gpsimd", bcb[:], io["BcatT"][:, :])
        b.bar(engs=("tensor", "vector", "gpsimd"))
        gd_w = [0, 0]
        gd_w[0] = b.dma("gpsimd", wwb[0][:].rearrange("p (c r) -> p c r", r=128),
                        io["WT"].rearrange("c p (o r) -> p c r o", r=128)[:, :, :, 0])
        pe_oc = 0
        dv_oc = [0, 0]
        for oc in range(NDC):
            if oc + 1 < NDC:
                b.w("gpsimd", "pe", pe_oc)
                gd_w[(oc + 1) % 2] = b.dma(
                    "gpsimd", wwb[(oc + 1) % 2][:].rearrange("p (c r) -> p c r", r=128),
                    io["WT"].rearrange("c p (o r) -> p c r o", r=128)[:, :, :, oc + 1])
            b.w("tensor", "gd", gd_w[oc % 2])
            b.w("tensor", "dv", dv_oc[oc % 2])
            for t in range(2):
                for k in range(NDC):
                    b.op("tensor", lambda e, t_=t, k=k, oc=oc: e.matmul(
                        ps_w[oc % 2][t_][:], wwb[oc % 2][:, k * 128:(k + 1) * 128],
                        xnb2[:, k * TPC + t_ * 512:k * TPC + (t_ + 1) * 512],
                        start=(k == 0), stop=False))
                b.op("tensor", lambda e, t_=t, oc=oc: e.matmul(
                    ps_w[oc % 2][t_][:], bcb[:, oc * 128:(oc + 1) * 128],
                    t1s_g[:, t_ * 512:(t_ + 1) * 512], start=False, stop=True),
                    inc="pe")
            pe_oc = b.c["pe"]
            b.w("vector", "pe", pe_oc)
            evv = evw[oc % 2]
            for t in range(2):
                dvv = b.op("vector", lambda e, t_=t, oc=oc, ev=evv: e.tensor_scalar(
                    ev[:, t_ * 512:(t_ + 1) * 512], ps_w[oc % 2][t_][:], _col(bwb, oc), None, ALU.add),
                    inc="dv")
            dv_oc[oc % 2] = dvv
            b.w("gpsimd", "dv", dvv)
            b.dma("gpsimd", io["ACC"][oc], evv[:], accum_op=ALU.add)
        b.bar()
    fin()



DEBUG = False


def build_program(mode):
    nc = bass.Bass("TRN2", target_bir_lowering=False)
    ctx = ExitStack()
    ET = lambda nm, sh, dt: nc.dram_tensor(nm, sh, dt, kind="ExternalInput").ap()
    OT = lambda nm, sh, dt: nc.dram_tensor(nm, sh, dt, kind="ExternalOutput").ap()
    IT = lambda nm, sh, dt: nc.dram_tensor(nm, sh, dt).ap()

    io = {}
    names = []

    def EI(nm, sh, dt):
        io[nm] = ET(nm, sh, dt)
        names.append(nm)

    EI("x_t", [NDC, 128, TPC], F32)
    for nm, sh, dt in [
        ("A_log_t", [NDC, 128, NST], F32), ("D_t", [128, NDC], F32),
        ("dtb_t", [128, NDC], F32), ("xpwT", [NDC, 128, 160], F32),
        ("dtwT", [NDC, 128, 128], F32),
        ("E32", [128, NDC * 32], BF16), ("E16", [16, 2048], F32),
        ("ident", [128, 128], BF16), ("ones128f", [128, 1], F32),
        ("ones1x128f", [1, 128], F32), ("ones128b", [128, 1], BF16),
        ("o1x128b", [1, 128], BF16), ("onesKM", [128, 32], BF16),
        ("hmsk", [NSLOT, 128, NST * 32], BF16),
    ]:
        EI(nm, sh, dt)
    if mode in ("L1", "ALL"):
        EI("Esg", [32, 4096], BF16)
        EI("Esd", [8, 1024], BF16)
        EI("Edn", [128, NHC * 8], BF16)
        for nm, sh in [("qg", [NHC, 128, NDC * 128]), ("qu", [NHC, 128, NDC * 128]),
                       ("qd", [NDC, 128, NHC * 128])]:
            EI(nm, sh, I32)
        for nm, sh in [("zg", [32, HFF]), ("zu", [32, HFF]), ("zd", [NHC, DM])]:
            EI(nm, sh, I32)
        for nm, sh in [("sg", [32, HFF]), ("su", [32, HFF]), ("sd", [NHC, DM])]:
            EI(nm, sh, F32)
    if mode in ("L2", "ALL"):
        EI("wqkvT", [NDC, 128, 6144], F32)
        EI("bqkv_t", [128, NQKV], F32)
        EI("AcatT", [NDC, 128, 128], F32)
        EI("wvb", [128, TPC], F32)
    if mode in ("L3", "ALL"):
        EI("woT", [NCC, 128, 4096], F32)
        EI("bo_t", [128, NDC], F32)
        EI("WT", [NDC, 128, 4096], F32)
        EI("bW_t", [128, NDC], F32)
        EI("BcatT", [128, DM], F32)
        EI("rWT", [NDC, 128, 8], F32)
        EI("rb_t", [8, 1], F32)
        EI("masks", [NSLOT, 128, 512], F32)
        EI("Erv", [8, 128], F32)

    if mode == "LNDBG":
        io["d_mrow"] = OT("d_mrow", [1, TPC], F32)
        io["d_vrow"] = OT("d_vrow", [1, TPC], F32)
        io["d_mb"] = OT("d_mb", [128, TPC], F32)
        io["d_rstdb"] = OT("d_rstdb", [128, TPC], F32)
    elif mode == "L1":
        io["ACC"] = OT("ACC", [NDC, 128, TPC], F32)
        io["stloc"] = OT("stloc", [DM, 136], F32)
        if DEBUG:
            io["d_xnb"] = OT("d_xnb", [128, NDC * TPC], BF16)
            io["d_sg"] = OT("d_sg", [32, TPC], BF16)
            io["d_bf"] = OT("d_bf", [16, TPC], F32)
            io["d_cf"] = OT("d_cf", [16, TPC], F32)
            io["d_dtin"] = OT("d_dtin", [128, TPC], BF16)
            io["d_bb"] = OT("d_bb", [128, NST * TPC], BF16)
            io["d_wg0"] = OT("d_wg0", [128, NDC * 128], BF16)
            io["d_h0"] = OT("d_h0", [128, TPC], BF16)
            io["d_da00"] = OT("d_da00", [128, TPC], F32)
            io["d_u00"] = OT("d_u00", [128, TPC], BF16)
            io["d_hb00"] = OT("d_hb00", [128, TPC], BF16)
            io["d_dtx0"] = OT("d_dtx0", [128, TPC], BF16)
            io["d_e16"] = OT("d_e16", [16, 2048], F32)
            io["d_ps1"] = OT("d_ps1", [128, 512], F32)
            io["d_ssbg"] = OT("d_ssbg", [32, 1024], BF16)
            io["d_qb0"] = OT("d_qb0", [128, NDC * 128], BF16)
            io["d_sb0"] = OT("d_sb0", [128, 1024], F32)
            io["d_esg_e"] = OT("d_esg_e", [32, 4096], BF16)
            io["d_esg_l"] = OT("d_esg_l", [32, 4096], BF16)
            io["d_wd0"] = OT("d_wd0", [128, 1024], BF16)
            io["d_wd24"] = OT("d_wd24", [128, 1024], BF16)
            io["d_sdb"] = OT("d_sdb", [8, 2048], BF16)
            io["d_szd"] = OT("d_szd", [8, 2048], BF16)
            io["d_shblk"] = OT("d_shblk", [8, TPC], BF16)
            io["d_mrow"] = OT("d_mrow", [1, TPC], F32)
            io["d_vrow"] = OT("d_vrow", [1, TPC], F32)
            io["d_mb"] = OT("d_mb", [128, TPC], F32)
            io["d_rstdb"] = OT("d_rstdb", [128, TPC], F32)
    elif mode == "L2":
        EI("ACCi", [NDC, 128, TPC], F32)
        io["ACC"] = OT("ACC", [NDC, 128, TPC], F32)
        EI("stg", [4 * DM, 136], F32)
        io["xn1o"] = OT("xn1o", [NDC, 128, TPC], BF16)
        io["qto"] = OT("qto", [NCC, 128, TPC], BF16)
        io["kvloc"] = OT("kvloc", [2 * 2048, TPC], BF16)
        io["arpart"] = OT("arpart", [128, NDC], F32)
        io["t1o"] = OT("t1o", [128, TPC], F32)
        import os as _os
        if _os.environ.get("K_DBG2"):
            io["d_afin"] = OT("d_afin", [128, NDC * NST], F32)
            io["d_ones"] = OT("d_ones", [128, 128], F32)
            io["d_dtinf"] = OT("d_dtinf", [128, TPC], BF16)
            io["d_dtbf"] = OT("d_dtbf", [128, NDC], F32)
            io["d_stag20"] = OT("d_stag20", [128, 544], F32)
            io["d_sng0"] = OT("d_sng0", [128, 512], F32)
            io["d_tg0"] = OT("d_tg0", [128, 32], F32)
            io["d_hin0"] = OT("d_hin0", [128, 512], F32)
            io["d_hsel0"] = OT("d_hsel0", [128, 128], F32)
            io["d_cum0"] = OT("d_cum0", [128, TPC], F32)
            io["d_dtf0"] = OT("d_dtf0", [128, TPC], F32)
    elif mode == "L3":
        EI("ACCi", [NDC, 128, TPC], F32)
        io["ACC"] = IT("ACC", [NDC, 128, TPC], F32)
        EI("xn1i", [NDC, 128, TPC], BF16)
        EI("qti", [NCC, 128, TPC], BF16)
        EI("kvg", [4 * DM, TPC], BF16)
        EI("arall", [4 * 128, NDC], F32)
        EI("t1i", [128, TPC], F32)
        io["out_t"] = OT("out_t", [NDC, 128, TPC], F32)
        import os as _os
        if _os.environ.get("K_DBG2"):
            io["d_rvc"] = OT("d_rvc", [128, 1], F32)
            io["d_lg"] = OT("d_lg", [8, 1], F32)
            io["d_l8"] = OT("d_l8", [1, 8], F32)
            io["d_e8"] = OT("d_e8", [1, 8], F32)
            io["d_rcol"] = OT("d_rcol", [8, 1], F32)
            io["d_e8b"] = OT("d_e8b", [1, 8], F32)
            io["d_sm1"] = OT("d_sm1", [1, 1], F32)
            io["d_sm1b"] = OT("d_sm1b", [1, 1], F32)
            io["d_vals"] = OT("d_vals", [128, NCC * TPC], BF16)
            io["d_smf00"] = OT("d_smf00", [128, 512], F32)
            io["d_pb00"] = OT("d_pb00", [128, 512], BF16)
            io["d_o00"] = OT("d_o00", [128, 256], F32)
            io["d_mall"] = OT("d_mall", [128, 64], F32)
            io["d_lall"] = OT("d_lall", [128, 64], F32)
            io["d_ksl0"] = OT("d_ksl0", [128, 512], BF16)
            io["d_vsl0"] = OT("d_vsl0", [128, 256], BF16)
            io["d_q00"] = OT("d_q00", [128, 128], BF16)
    else:
        io["ACC"] = IT("ACC", [NDC, 128, TPC], F32)
        io["stloc"] = IT("stloc", [DM, 136], F32)
        io["stg"] = IT("stg", [4 * DM, 136], F32)
        io["kvloc"] = IT("kvloc", [2 * 2048, TPC], BF16)
        io["kvg"] = IT("kvg", [4 * DM, TPC], BF16)
        io["ar_in"] = IT("ar_in", [128, NDC], F32)
        io["ar_out"] = IT("ar_out", [128, NDC], F32)
        io["xn1d"] = IT("xn1d", [NDC, 128, TPC], BF16)
        io["qtd"] = IT("qtd", [NCC, 128, TPC], BF16)
        io["t1d"] = IT("t1d", [128, TPC], F32)
        io["out_t"] = OT("out_t", [NDC, 128, TPC], F32)
    io["bnc1"] = IT("bnc1", [1, 8], F32)
    io["bnc2"] = IT("bnc2", [8, 1], F32)

    with ctx:
        b = Bld(nc, ctx)
        P = {}
        for nm, sh, dt in [
            ("A", [128, NDC * NST], F32), ("Dg", [128, NDC], F32),
            ("dtb", [128, NDC], F32),
            ("dtw", [128, NDC * 128], BF16),
            ("o128", [128, 1], F32), ("o1x128", [1, 128], F32),
            ("o128b", [128, 1], BF16), ("o1x128b", [1, 128], BF16),
            ("oKM", [128, 32], BF16), ("ident", [128, 128], BF16),
            ("dtin", [128, TPC], BF16), ("Bf", [16, TPC], F32), ("Cf", [16, TPC], F32),
            ("Sg", [32, TPC], BF16), ("ones128t", [128, 128], F32),
        ]:
            P[nm] = b.sb("p_" + nm, sh, dt)

        b.dma("sync", P["o128"][:], io["ones128f"])
        b.dma("sync", P["o1x128"][:], io["ones1x128f"])
        b.dma("sync", P["o128b"][:], io["ones128b"])
        b.dma("sync", P["o1x128b"][:], io["o1x128b"])
        b.dma("sync", P["oKM"][:], io["onesKM"])
        b.dma("sync", P["ident"][:], io["ident"])
        b.dma("sync", P["dtb"][:], io["dtb_t"])
        b.dma("sync", P["Dg"][:], io["D_t"])
        b.dma("sync", P["A"][:].rearrange("p (c r) -> p c r", r=NST), io["A_log_t"].rearrange("c p r -> p c r"))
        b.dma("gpsimd", P["dtw"][:].rearrange("p (c r) -> p c r", r=128), io["dtwT"].rearrange("c p r -> p c r"))
        b.bar(engs=("scalar", "vector", "gpsimd"))
        b.act(lambda e: e.activation(P["A"][:], P["A"][:], AF.Exp))
        b.w("vector", "ac")
        b.op("vector", lambda e: e.tensor_scalar(P["A"][:], P["A"][:], -1.0, None, ALU.mult),
             inc="dv")
        b.op("gpsimd", lambda e: e.memset(P["ones128t"][:], 1.0), inc="pl")
        b.bar(engs=("vector", "scalar", "tensor", "gpsimd"))

        if mode in ("L1", "ALL"):
            b.dma("gpsimd", io["ACC"].rearrange("c p t -> (c p) t"),
                  io["x_t"].rearrange("c p t -> (c p) t"))
        elif mode == "L2":
            b.dma("gpsimd", io["ACC"].rearrange("c p t -> (c p) t"),
                  io["ACCi"].rearrange("c p t -> (c p) t"))
        if mode == "L3":
            b.dma("gpsimd", io["ACC"].rearrange("c p t -> (c p) t"),
                  io["ACCi"].rearrange("c p t -> (c p) t"))
        b.w("gpsimd", "gd", b.c["gd"])
        b.bar(engs=("sync", "vector", "tensor", "scalar"), sems=("gd",))

        if mode == "LNDBG":
            xnb0 = b.sb("xnb0", [128, NDC * TPC], BF16)
            scd = ExitStack()
            with scd:
                mb0 = scd.enter_context(nc.sbuf_tensor("mb0", [128, TPC], F32))
                rstdb0 = scd.enter_context(nc.sbuf_tensor("rstdb0", [128, TPC], F32))
                import os
                _sg = os.environ.get("LNV", "11")
                _ln_phase(b, P, io["x_t"], io, with_sg=_sg[0] == "1", with_xp=_sg[1] == "1",
                          xnb=xnb0 if _sg != "00" else None, mb=mb0, rstdb=rstdb0)
            b.w("gpsimd", "gd", b.c["gd"])
            b.w("gpsimd", "ld", b.c["ld"])
            b.finalize()
            return nc, names
        if mode in ("L1", "ALL"):
            sc0x = ExitStack()
            with sc0x:
                xnb0 = sc0x.enter_context(nc.sbuf_tensor("xnb0", [128, NDC * TPC], BF16))
                sc1 = ExitStack()
                with sc1:
                    mb0 = sc1.enter_context(nc.sbuf_tensor("mb0", [128, TPC], F32))
                    rstdb0 = sc1.enter_context(nc.sbuf_tensor("rstdb0", [128, TPC], F32))
                    _ln_phase(b, P, io["x_t"], io, with_sg=True, with_xp=True, xnb=xnb0,
                              mb=mb0, rstdb=rstdb0)
                    Bb = sc1.enter_context(nc.sbuf_tensor("m_Bb", [128, NST * TPC], BF16))
                    Cb = sc1.enter_context(nc.sbuf_tensor("m_Cb", [128, NST * TPC], BF16))
                    _build_bbcb(b, P, Bb, Cb, io["E16"],
                                dbg_io=io if (DEBUG and mode == "L1") else None)
                    if DEBUG and mode == "L1":
                        b.dma("gpsimd", io["d_xnb"][:, :], xnb0[:])
                        b.dma("gpsimd", io["d_sg"][:, :], P["Sg"][:])
                        b.dma("gpsimd", io["d_bf"][:, :], P["Bf"][:])
                        b.dma("gpsimd", io["d_cf"][:, :], P["Cf"][:])
                        b.dma("gpsimd", io["d_dtin"][:, :], P["dtin"][:])
                        b.dma("gpsimd", io["d_bb"][:, :], Bb[:])
                    _mamba_pass1(b, io, P, Bb, Cb, mb0, rstdb0)
                _ffn0(b, io, P, xnb0)
                b.bar()

        if mode == "L1":
            b.w("gpsimd", "gd", b.c["gd"])
            b.w("gpsimd", "ld", b.c["ld"])
            b.finalize()
            return nc, names

        if mode == "ALL":
            b.w("gpsimd", "gd", b.c["gd"])
            b.op("gpsimd", lambda e: e.collective_compute(
                "AllGather", ALU.bypass, replica_groups=[[0, 1, 2, 3], [4, 5, 6, 7]],
                ins=[io["stloc"].opt()], outs=[io["stg"].opt()]), inc="cc", drain=False)
            b.w("gpsimd", "cc")
            b.bar(engs=("sync", "vector", "tensor", "scalar"), sems=("cc",))

        import os
        cut = os.environ.get("K_L2CUT", "") if mode == "L2" else ""
        if mode == "L2":
            sc0 = ExitStack()
            with sc0:
                mbx = sc0.enter_context(nc.sbuf_tensor("mbx", [128, TPC], F32))
                rstdbx = sc0.enter_context(nc.sbuf_tensor("rstdbx", [128, TPC], F32))
                _ln_phase(b, P, io["x_t"], io, with_sg=False, with_xp=True, xnb=None,
                          mb=mbx, rstdb=rstdbx)
        if mode in ("L2", "ALL") and cut != "ln0":
            sc2 = ExitStack()
            with sc2:
                Bb = sc2.enter_context(nc.sbuf_tensor("c_Bb", [128, NST * TPC], BF16))
                Cb = sc2.enter_context(nc.sbuf_tensor("c_Cb", [128, NST * TPC], BF16))
                _build_bbcb(b, P, Bb, Cb, io["E16"])
                _combine_fix(b, io, P, Bb, Cb)

        if cut not in ("ln0", "fix"):
            _l1_phases(b, io, P, mode)
        if mode == "L2":
            if "d_afin" in io:
                b.dma("gpsimd", io["d_afin"][:, :], P["A"][:])
                b.dma("gpsimd", io["d_ones"][:, :], P["ones128t"][:])
                b.dma("gpsimd", io["d_dtinf"][:, :], P["dtin"][:])
                b.dma("gpsimd", io["d_dtbf"][:, :], P["dtb"][:])
            b.w("gpsimd", "gd", b.c["gd"])
            b.w("gpsimd", "ld", b.c["ld"])
        b.finalize()
        return nc, names



_CACHE = {}
FUSED = True


def _get(mode):
    if mode not in _CACHE:
        _CACHE[mode] = build_program(mode)
    return _CACHE[mode]


def _run(mode, shared, cores, extra_core=None):
    nc, names = _get(mode)
    in_maps = []
    for c in range(8):
        m = {}
        for nm in names:
            if nm in shared:
                m[nm] = shared[nm]
            elif nm in cores[c]:
                m[nm] = cores[c][nm]
            elif extra_core is not None and nm in extra_core[c]:
                m[nm] = extra_core[c][nm]
            else:
                raise KeyError(f"missing input {nm} for mode {mode}")
        in_maps.append(m)
    res = run_bass_kernel_spmd(nc, in_maps, core_ids=list(range(8)))
    return res.results


def kernel(**inputs):
    shared = _prep_shared(inputs)
    cores = [_prep_core(inputs, c) for c in range(8)]
    if FUSED:
        res = _run("ALL", shared, cores)
        outs = [np.asarray(r["out_t"], dtype=np.float32) for r in res]
    else:
        r1 = _run("L1", shared, cores)
        ex2 = []
        for c in range(8):
            bb = c // 4
            stg = np.concatenate([np.asarray(r1[bb * 4 + qq]["stloc"]) for qq in range(4)], 0)
            ex2.append({"ACCi": np.asarray(r1[c]["ACC"]), "stg": stg})
        r2 = _run("L2", shared, cores, ex2)
        ex3 = []
        for c in range(8):
            bb = c // 4
            kvg = np.concatenate([np.asarray(r2[bb * 4 + qq]["kvloc"]) for qq in range(4)], 0)
            arall = np.concatenate([np.asarray(r2[bb * 4 + qq]["arpart"]) for qq in range(4)], 0)
            ex3.append({
                "ACCi": np.asarray(r2[c]["ACC"]), "xn1i": np.asarray(r2[c]["xn1o"]),
                "qti": np.asarray(r2[c]["qto"]), "kvg": kvg, "arall": arall,
                "t1i": np.asarray(r2[c]["t1o"]),
            })
        r3 = _run("L3", shared, cores, ex3)
        outs = [np.asarray(r["out_t"], dtype=np.float32) for r in r3]
    out = np.zeros((NBATCH, SEQ, DM), np.float32)
    for c in range(8):
        bb, q = c // 4, c % 4
        o = outs[c].reshape(DM, TPC).T
        for s in range(NSLOT):
            g = GQ[q][s]
            out[bb, g * TILE:(g + 1) * TILE, :] = o[s * TILE:(s + 1) * TILE, :]
    return out
